# version 04
# baseline (speedup 1.0000x reference)
import numpy as np
import ml_dtypes
import concourse.bass as bass
import concourse.mybir as mybir
from concourse.bass_utils import run_bass_kernel_spmd

F32 = mybir.dt.float32
BF16 = mybir.dt.bfloat16
AF = mybir.ActivationFunctionType
ALU = mybir.AluOpType
AX = mybir.AxisListType
BF = ml_dtypes.bfloat16

ENGINES = ("sync", "gpsimd", "tensor", "vector", "scalar")
SEMS = ("LD", "LDG", "PE", "ACT", "DVE", "POOL", "ST")

T = 2048
NTC = 4
D = 1024
NCP = 8
EPS = 1e-5
SHIFT = -20.0

N_B = 4
TAPS = ()
LAST_STAGE = "all"
_dbg = {}

BIAS_LAYOUT = [
    ("ln1_g", 8), ("ln1_b", 8), ("lnx_g", 8), ("lnx_b", 8),
    ("ln2_g", 8), ("ln2_b", 8),
    ("cattn_b", 24), ("aproj_b", 8), ("qattn_b", 8), ("xkv_b", 16),
    ("xproj_b", 8), ("fc_b", 32), ("mproj_b", 8),
    ("sb1", 16), ("sb2", 8), ("rb1", 128), ("rb2", 64),
    ("scorer", 16), ("gate_b", 1), ("c_eps", 1), ("c_shift", 1),
]
BIAS_OFF = {}
_o = 0
for _n, _c in BIAS_LAYOUT:
    BIAS_OFF[_n] = _o
    _o += _c
BIAS_COLS = _o


class Prog:
    def __init__(self):
        self.q = {e: [] for e in ENGINES}
        self.c = {}
        self.sem = {}
        self.last_wait = {e: {} for e in ENGINES}
        self.real_top = {e: False for e in ENGINES}

    def on(self, eng, fn):
        self.q[eng].append(fn)
        self.real_top[eng] = True

    OWN = {"vector": "DVE", "scalar": "ACT", "gpsimd": "POOL"}

    def inc(self, eng, fn, sem, n=1):
        own = self.OWN.get(eng)
        if own is not None:
            self.w(eng, own, self.c.get(own, 0))
        self.c[sem] = self.c.get(sem, 0) + n
        s = self.sem[sem]
        self.q[eng].append(lambda e, fn=fn, s=s, n=n: fn(e).then_inc(s, n))
        self.real_top[eng] = False
        return self.c[sem]

    def mark(self, eng, sem):
        assert self.real_top[eng], "mark() must follow a real engine op"
        f = self.q[eng].pop()
        self.c[sem] = self.c.get(sem, 0) + 1
        s = self.sem[sem]
        self.q[eng].append(lambda e, f=f, s=s: f(e).then_inc(s, 1))
        self.real_top[eng] = False
        return self.c[sem]

    def w(self, eng, sem, val):
        if val <= 0 or val <= self.last_wait[eng].get(sem, 0):
            return
        self.last_wait[eng][sem] = val
        s = self.sem[sem]
        self.q[eng].append(lambda e, s=s, v=val: e.wait_ge(s, v))
        self.real_top[eng] = False

    def barrier(self):
        engs = ("tensor", "vector", "scalar", "gpsimd")
        m = {"tensor": "PE", "vector": "DVE", "scalar": "ACT", "gpsimd": "POOL"}
        for e in engs:
            if self.real_top[e]:
                self.mark(e, m[e])
        snap = dict(self.c)
        for e in engs:
            for o in engs:
                if o != e:
                    self.w(e, m[o], snap.get(m[o], 0))

    def install(self, block):
        for e in ENGINES:
            ops = self.q[e]
            getattr(block, e)(lambda eng, ops=ops: [f(eng) for f in ops])


GATE_SYNC = False
GATE_BANKS = (0, 1)
DBG_LN = False
DBG_LN_TC = 0
DBG_LN_SRC = [None]
N_EXP_DBG = 9
MOVE_LE = False
DBG_SKIP = frozenset()
PHASE_LOG = []


def build_nc(n_b, taps, last_stage):
    nc = bass.Bass("TRN2", target_bir_lowering=False)
    P = Prog()
    taps = set(taps)

    x_fm = nc.dram_tensor("x_fm", [n_b, D, T], F32, kind="ExternalInput").ap()
    enc_fm = nc.dram_tensor("enc_fm", [D, T], F32, kind="ExternalInput").ap()
    biases_i = nc.dram_tensor("biases", [128, BIAS_COLS], F32,
                              kind="ExternalInput").ap()
    mask_i = nc.dram_tensor("maskbf", [128, 644], F32, kind="ExternalInput").ap()
    ones_i = nc.dram_tensor("ones_bf", [128, 128], F32, kind="ExternalInput").ap()
    sel_i = nc.dram_tensor("sel33", [33, 128], F32, kind="ExternalInput").ap()
    id_i = nc.dram_tensor("id32", [128, 128], F32, kind="ExternalInput").ap()
    gbrow_i = nc.dram_tensor("gbrow", [128, 8], F32, kind="ExternalInput").ap()
    wshape = dict(cattn=[D, 3072], qattn=[D, D], xkv=[D, 2048], aproj=[D, D],
                  xproj=[D, D], fc=[D, 4096], mproj=[4096, D], gate=[D, 8],
                  sw1=[D, 2048], sw2=[D, D], rw1=[8, D, 2048], rw2=[8, D, D])
    w_in, wb = {}, {}
    for nme, shp in wshape.items():
        w_in[nme] = nc.dram_tensor("w_" + nme, shp, F32, kind="ExternalInput").ap()
        wb[nme] = nc.dram_tensor("wb_" + nme, shp, BF16).ap()
    y = nc.dram_tensor("y", [n_b, D, T], F32, kind="ExternalOutput").ap()
    xbf = nc.dram_tensor("xbf", [n_b, D, T], BF16).ap()
    k2t_s = nc.dram_tensor("k2t_s", [D, T], BF16).ap()
    v2_s = nc.dram_tensor("v2_s", [T, D], BF16).ap()

    tap_out = {}
    for nme, shp in [("init", [128, BIAS_COLS]),
                     ("h0", [n_b, D, T]), ("xln1", [n_b, D, T]),
                     ("qt", [n_b, D, T]), ("kt", [n_b, D, T]),
                     ("v", [n_b, T, D]), ("hself", [n_b, D, T]),
                     ("k2t", [D, T]), ("v2", [T, D]),
                     ("hcross", [n_b, D, T]),
                     ("hmlp", [n_b, D, T]), ("x1", [n_b, D, T]),
                     ("x1ln", [n_b, D, T]), ("cw", [n_b, 128, 128]),
                     ("acc", [n_b, D, T])]:
        if nme in taps:
            tap_out[nme] = nc.dram_tensor("dbg_" + nme, shp, F32,
                                          kind="ExternalOutput").ap()

    A = nc.alloc_sbuf_tensor
    ones_bf = A("s_ones_bf", [128, 128], BF16)
    sel33 = A("s_sel33", [33, 128], BF16)
    maskbf = A("s_maskbf", [128, 644], BF16)
    idbf = A("s_idbf", [128, 128], BF16)
    id32 = A("s_id32", [128, 128], F32)
    bia = A("bia", [128, BIAS_COLS], F32)
    scor20 = A("scor20", [128, 16], F32)
    mstat = A("mstat", [128, 512], F32)
    rstdb = A("rstdb", [128, 512], BF16)
    sqbuf = A("sqbuf", [128, 1024], BF16)
    rden = A("rden", [33, 512], BF16)
    bcsb = A("bcsb", [128, 1024], BF16)
    tmp1 = A("tmp1", [128, 1024], BF16)
    ostage = A("ostage", [128, 512], F32)
    expst = A("expst", [128, 2048], BF16)
    xchunk = A("xchunk", [128, 4096], BF16)
    wbuf = A("wbuf", [128, 8192], BF16)
    h_sb = A("h_sb", [128, 16384], BF16)
    xln_sb = A("xln_sb", [128, 16384], BF16)
    qkt = A("qkt", [128, 32768], BF16)
    v_sb = A("v_sb", [128, 16384], BF16)
    gsm = A("gsm", [128, 48], F32)
    gwb = A("gwb", [128, 64], BF16)
    gbb = A("gbb", [128, 8], F32)
    cwtm = A("cwtm", [128, 128], F32)
    crow = A("crow", [1, 512], BF16)
    accum = qkt.bitcast(F32)

    p01 = nc.alloc_psum_tensor("p01", [128, 1024], F32)
    p2 = nc.alloc_psum_tensor("p2", [128, 512], F32)
    p3 = nc.alloc_psum_tensor("p3", [128, 512], F32)
    p45 = nc.alloc_psum_tensor("p45", [128, 1024], F32)
    p6 = nc.alloc_psum_tensor("p6", [128, 512], F32)
    p7 = nc.alloc_psum_tensor("p7", [128, 512], F32)

    def bank(i, r0=0, r1=128, c0=0, c1=512):
        if i == 0:
            return p01[r0:r1, c0:c1]
        if i == 1:
            return p01[r0:r1, 512 + c0:512 + c1]
        if i == 4:
            return p45[r0:r1, c0:c1]
        if i == 5:
            return p45[r0:r1, 512 + c0:512 + c1]
        return {2: p2, 3: p3, 6: p6, 7: p7}[i][r0:r1, c0:c1]

    DMA_SEMS = (["WC", "WCH", "WX", "WXG", "WG", "WE1", "WE2", "WV0", "WV1", "KL", "VL",
                 "KS", "OS", "TS"]
                + ["WS%d" % j for j in range(4)] + ["WM%d" % j for j in range(2)]
                + ["cx%d" % b for b in range(n_b)]
                + ["cw_" + nm for nm in ("cattn", "qattn", "xkv", "aproj",
                                         "xproj", "fc", "mproj", "gate",
                                         "sw1", "sw2")]
                + ["cr1_%d" % e for e in range(8)]
                + ["cr2_%d" % e for e in range(8)])
    for s in list(SEMS) + DMA_SEMS:
        P.sem[s] = nc.semaphore(s).__enter__()
        P.c[s] = 0

    def ld(sem, dst, src):
        return (sem, P.inc("sync", lambda e, d=dst, s=src: e.dma_start(d, s),
                           sem, 16))

    def ldg(sem, dst, src):
        return (sem, P.inc("gpsimd", lambda e, d=dst, s=src: e.dma_start(d, s),
                           sem, 16))

    def store(sem, dst, src):
        return (sem, P.inc("sync", lambda e, d=dst, s=src: e.dma_start(d, s),
                           sem, 16))

    mmc = [0]

    def phase(nm):
        PHASE_LOG.append((nm, mmc[0]))

    def mm(out, lhsT, rhs, start, stop, tp=None, skip=False):
        mmc[0] += 1
        P.on("tensor", lambda e, o=out, l=lhsT, r=rhs, st=start, sp=stop, tp=tp,
             sk=skip: e.matmul(o, l, r, start=st, stop=sp, tile_position=tp,
                               skip_group_check=sk))

    def pe_mark():
        return P.mark("tensor", "PE")

    def tr(out, in_, ident):
        mmc[0] += 1
        P.on("tensor", lambda e, o=out, i=in_, d=ident: e.transpose(o, i, d))
        return pe_mark()

    bank_free = {}

    def take_bank(i):
        if i in bank_free:
            s, v = bank_free[i]
            P.w("tensor", s, v)

    def bias_ap(nme, j, n=1):
        off = BIAS_OFF[nme] + j
        return bia[:, off:off + n]

    def qguard():
        P.w("sync", "PE", P.c["PE"])
        P.w("sync", "DVE", P.c["DVE"])
        P.w("sync", "ACT", P.c["ACT"])
        P.w("sync", "POOL", P.c["POOL"])

    ldg("WC", ones_bf[:, :], ones_i[:, :])
    ldg("WC", sel33[:, :], sel_i[:, :])
    ldg("WC", maskbf[:, :], mask_i[:, :])
    ldg("WC", idbf[:, :], id_i[:, :])
    ld("WCH", id32[:, :], id_i[:, :])
    ld("WCH", gbb[:, :], gbrow_i[:, :])
    ld("WCH", bia[:, :], biases_i[:, :])
    WC_ALL = P.c["WC"]
    WCH_ALL = P.c["WCH"]

    cast_done = {}
    for b in range(n_b):
        cast_done["x%d" % b] = ldg("cx%d" % b, xbf[b], x_fm[b])
    for nme in ("cattn", "qattn", "xkv", "aproj", "xproj", "fc", "mproj",
                "gate", "sw1", "sw2"):
        cast_done[nme] = ldg("cw_" + nme, wb[nme], w_in[nme])
    for e8 in range(8):
        cast_done["rw1_%d" % e8] = ldg("cr1_%d" % e8, wb["rw1"][e8],
                                       w_in["rw1"][e8])
        cast_done["rw2_%d" % e8] = ldg("cr2_%d" % e8, wb["rw2"][e8],
                                       w_in["rw2"][e8])

    for eng in ("vector", "tensor", "scalar"):
        P.w(eng, "WC", WC_ALL)
        P.w(eng, "WCH", WCH_ALL)
    P.inc("vector", lambda e: e.tensor_scalar(
        scor20[:, :], bia[:, BIAS_OFF["scorer"]:BIAS_OFF["scorer"] + 16],
        SHIFT, None, ALU.add), "DVE")
    P.inc("vector", lambda e: e.memset(rden[:, :], 0.0), "DVE")
    P.barrier()

    class Ring:
        def __init__(self, lo_col, n_slots, slot_cols, sempfx):
            self.base, self.n, self.cols = lo_col, n_slots, slot_cols
            self.free_pe = [0] * n_slots
            self.sems = ["%s%d" % (sempfx, j) for j in range(n_slots)]
            self.i = 0

        def bump(self, pe):
            self.free_pe = [max(pe, x) for x in self.free_pe]

        def load(self, src, dep=None):
            s = self.i % self.n
            self.i += 1
            P.w("sync", "PE", self.free_pe[s])
            if dep:
                P.w("sync", *dep)
            dst = wbuf[:, self.base + s * self.cols:
                       self.base + (s + 1) * self.cols]
            c = ld(self.sems[s], dst, src)
            return s, dst, c

        def done(self, s, pe_cnt):
            self.free_pe[s] = pe_cnt

    ring1 = Ring(0, 4, 1024, "WS")
    ring_mp = Ring(4096, 2, 2048, "WM")

    def fm_proj(wdram, wdep, fps, src_fn, evac, wfp_base=0, tcs=range(NTC)):
        wr = wdram.rearrange("(cp p) f -> p cp f", p=128)
        for fp in fps:
            s, wsl, c = ring1.load(
                wr[:, :, (wfp_base + fp) * 128:(wfp_base + fp + 1) * 128], wdep)
            pe_c = 0
            for tc in tcs:
                bk = (fp * len(tcs) + tc) % 4
                take_bank(bk)
                P.w("tensor", *c)
                for cp in range(NCP):
                    mm(bank(bk), wsl[:, cp * 128:(cp + 1) * 128],
                       src_fn(cp, tc), cp == 0, cp == NCP - 1)
                pe_c = pe_mark()
                bank_free[bk] = evac(fp, tc, bk, pe_c)
            ring1.done(s, pe_c)

    def evac_act(dst_fn, bias_name, bias_js=None, func=AF.Identity, scale=1.0):
        def f(fp, tc, bk, pe_c):
            P.w("scalar", "PE", pe_c)
            bap = bias_ap(bias_name, bias_js[fp] if bias_js else fp) \
                if bias_name else 0.0
            v = P.inc("scalar", lambda e, d=dst_fn(fp, tc), p=bank(bk), b=bap:
                      e.activation(d, p, func, bias=b, scale=scale), "ACT")
            return ("ACT", v)
        return f

    def evac_resid(dst_fn, bias_name, res_fn):
        def f(fp, tc, bk, pe_c):
            P.w("vector", "PE", pe_c)
            v = P.inc("vector", lambda e, d=dst_fn(fp, tc), p=bank(bk),
                      b=bias_ap(bias_name, fp), r=res_fn(fp, tc):
                      e.scalar_tensor_tensor(d, p, b, r, ALU.add, ALU.add),
                      "DVE")
            return ("DVE", v)
        return f

    def fm_ln(src_fn, dst_fn, g=None, bnm=None, tcs=range(NTC), out_fn=None):
        for tc in tcs:
            mb_, vb_ = 6, 7
            take_bank(mb_)
            P.w("tensor", "DVE", P.c["DVE"])
            for cp in range(NCP):
                mm(bank(mb_), ones_bf[:, :], src_fn(cp, tc), cp == 0,
                   cp == NCP - 1)
            pe_mean = pe_mark()
            P.w("scalar", "PE", pe_mean)
            P.w("scalar", "DVE", P.c["DVE"])
            a_m = P.inc("scalar", lambda e: e.activation(
                mstat[:, :], bank(mb_), AF.Copy, bias=0.0, scale=1.0 / D), "ACT")
            bank_free[mb_] = ("ACT", a_m)
            take_bank(vb_)
            sq_free = [0, 0]
            last_xc = 0
            for cp in range(NCP):
                P.w("vector", "ACT", a_m)
                sl = cp % 2
                P.w("vector", "PE", sq_free[sl])
                d = dst_fn(cp, tc)
                last_xc = P.inc("vector", lambda e, d=d, s=src_fn(cp, tc):
                                e.tensor_tensor(d, s, mstat[:, :], ALU.subtract),
                                "DVE")
                sq = sqbuf[:, sl * 512:(sl + 1) * 512]
                v_sq = P.inc("vector", lambda e, q=sq, d=d:
                             e.tensor_tensor(q, d, d, ALU.mult), "DVE")
                P.w("tensor", "DVE", v_sq)
                mm(bank(vb_), ones_bf[:, :], sq, cp == 0, cp == NCP - 1)
                sq_free[sl] = pe_mark()
            pe_var = P.c["PE"]
            P.w("scalar", "PE", pe_var)
            P.w("scalar", "DVE", last_xc)
            a_ln = P.inc("scalar", lambda e: e.activation(
                mstat[:, :], bank(vb_), AF.Ln, bias=bias_ap("c_eps", 0),
                scale=1.0 / D), "ACT")
            bank_free[vb_] = ("ACT", a_ln)
            a_rs = P.inc("scalar", lambda e: e.activation(
                rstdb[:, :], mstat[:, :], AF.Exp, bias=0.0, scale=-0.5), "ACT")
            for cp in range(NCP):
                P.w("vector", "ACT", a_rs)
                d = dst_fn(cp, tc)
                P.inc("vector", lambda e, d=d: e.tensor_tensor(
                    d, d, rstdb[:, :], ALU.mult), "DVE")
                if g is not None:
                    P.inc("vector", lambda e, d=d, gg=bias_ap(g, cp),
                          bb=bias_ap(bnm, cp): e.tensor_scalar(
                              d, d, gg, bb, ALU.mult, ALU.add), "DVE")
                if out_fn is not None:
                    out_fn(cp, tc, d)

    INSTR = [dict(STb=(0, 1), OTb=2, DENb=3, ex=0),
             dict(STb=(4, 5), OTb=6, DENb=7, ex=1)]
    inst_free = [{"ACT": 0, "DVE": 0}, {"ACT": 0, "DVE": 0}]

    def st_ap(R, c0, c1):
        if R["STb"] == (0, 1):
            return p01[:, c0:c1]
        return p45[:, c0:c1]

    def attention(cross):
        vb_name = "cattn_b" if not cross else "xkv_b"
        vb_js = [16 + p for p in range(8)] if not cross else \
                [8 + p for p in range(8)]

        def make_steps(hp, qc, R):
            ex = R["ex"]
            exA0, exB0 = ex * 1024, ex * 1024 + 512
            if not cross:
                kbs = list(range(4 * qc + 4))
            else:
                kbs = list(range(min(15, 4 * qc + 4) + 1))
            steps = []
            for i, kb in enumerate(kbs):
                eb = expst if i % 2 == 0 else xln_sb
                off = kb * 128 - qc * 512
                lo = max(0, off) if not cross else max(0, off - 2)
                n = 512 - lo
                first, lastkb = i == 0, i == len(kbs) - 1
                qc0 = hp * 2048 + qc * 512 + lo
                kc0 = 16384 + hp * 2048 + kb * 128

                masked = off >= 0
                m0 = 0 if not cross else (0 if off >= 2 else 2)

                def ph1(kc0=kc0, qc0=qc0, lo=lo, n=n, R=R, masked=masked,
                        m0=m0):
                    if masked:
                        msl = maskbf[:, m0:m0 + n]
                        mm(st_ap(R, lo, lo + n), idbf[:, :], msl, True, False)
                        mm(st_ap(R, 512 + lo, 512 + lo + n), idbf[:, :], msl,
                           True, False)
                    st = not masked
                    mm(st_ap(R, lo, lo + n), qkt[0:64, kc0:kc0 + 128],
                       qkt[0:64, qc0:qc0 + n], st, True, (0, 0))
                    mm(st_ap(R, 512 + lo, 512 + lo + n),
                       qkt[64:128, kc0:kc0 + 128], qkt[64:128, qc0:qc0 + n],
                       st, True, (64, 0))
                    return pe_mark()

                def phe(kb=kb, ex=ex, R=R, lo=lo, n=n, eb=eb):
                    bias = bias_ap("c_shift", 0) if not cross \
                        else scor20[:, kb:kb + 1]
                    if lo == 0:
                        return P.inc("scalar", lambda e, b=bias: e.activation(
                            eb[:, ex * 1024:(ex + 1) * 1024],
                            st_ap(R, 0, 1024), AF.Exp, bias=b, scale=0.125),
                            "ACT")
                    P.inc("scalar", lambda e, b=bias: e.activation(
                        eb[:, ex * 1024 + lo:ex * 1024 + 512],
                        st_ap(R, lo, 512), AF.Exp, bias=b, scale=0.125), "ACT")
                    return P.inc("scalar", lambda e, b=bias: e.activation(
                        eb[:, ex * 1024 + 512 + lo:(ex + 1) * 1024],
                        st_ap(R, 512 + lo, 1024), AF.Exp, bias=b, scale=0.125),
                        "ACT")

                def ph2(kb=kb, lo=lo, n=n, first=first, lastkb=lastkb, hp=hp,
                        ex=ex, R=R, eb=eb):
                    va = v_sb[:, kb * 1024 + hp * 128:kb * 1024 + hp * 128 + 64]
                    vb_ = v_sb[:, kb * 1024 + hp * 128 + 64:
                               kb * 1024 + hp * 128 + 128]
                    ob = R["OTb"]
                    db = R["DENb"]
                    mm(bank(ob, 0, 64, lo, lo + n), va,
                       eb[:, exA0 + lo:exA0 + lo + n], first, lastkb, (0, 0),
                       skip=True)
                    mm(bank(ob, 64, 128, lo, lo + n), vb_,
                       eb[:, exB0 + lo:exB0 + lo + n], first, lastkb, (0, 64),
                       skip=True)
                    mm(bank(db, 0, 1, lo, lo + n), ones_bf[:, 0:1],
                       eb[:, exA0 + lo:exA0 + lo + n], first, lastkb, (0, 0),
                       skip=True)
                    mm(bank(db, 32, 33, lo, lo + n), ones_bf[:, 1:2],
                       eb[:, exB0 + lo:exB0 + lo + n], first, lastkb, (0, 32),
                       skip=True)
                    return pe_mark()

                steps.append((ph1, phe, ph2))

            def tail(hp=hp, qc=qc, R=R):
                pe_den = P.c["PE"]
                db, ob, ex = R["DENb"], R["OTb"], R["ex"]
                P.w("scalar", "PE", pe_den)
                P.inc("scalar", lambda e: e.activation(
                    mstat[0:1, :], bank(db, 0, 1), AF.Ln, bias=0.0,
                    scale=1.0), "ACT")
                P.inc("scalar", lambda e: e.activation(
                    rden[0:1, :], mstat[0:1, :], AF.Exp, bias=0.0,
                    scale=-1.0), "ACT")
                P.inc("scalar", lambda e: e.activation(
                    mstat[32:33, :], bank(db, 32, 33), AF.Ln, bias=0.0,
                    scale=1.0), "ACT")
                a_rd = P.inc("scalar", lambda e: e.activation(
                    rden[32:33, :], mstat[32:33, :], AF.Exp, bias=0.0,
                    scale=-1.0), "ACT")
                P.w("tensor", "ACT", a_rd)
                mm(bank(db), sel33[:, :], rden[:, :], True, True)
                c_bc = pe_mark()
                bch = bcsb[:, ex * 512:(ex + 1) * 512]
                dst = qkt[:, hp * 2048 + qc * 512:hp * 2048 + (qc + 1) * 512]
                P.w("vector", "PE", c_bc)
                P.inc("vector", lambda e, bc=bch, db=db: e.tensor_copy(
                    bc, bank(db)), "DVE")
                P.inc("vector", lambda e, d=dst, bc=bch:
                      e.tensor_tensor(d, bank(ob), bc, ALU.mult), "DVE")
                v_fin = P.inc("vector", lambda e, d=dst, bb=bias_ap(
                    vb_name, vb_js[hp]): e.tensor_scalar(
                        d, d, bb, None, ALU.add), "DVE")
                inst_free[ex] = {"ACT": a_rd, "DVE": v_fin}

            return steps, tail

        pend = [None, None]
        last_e = [0, 0]
        unit_dve = [inst_free[0]["DVE"], inst_free[1]["DVE"]]
        for hp0 in range(0, 8, 2):
            for qc in range(NTC):
                sA, tA = make_steps(hp0, qc, INSTR[0])
                sB, tB = make_steps(hp0 + 1, qc, INSTR[1])
                for i in range(len(sA)):
                    for ex, steps in ((0, sA), (1, sB)):
                        ph1, phe, ph2 = steps[i]
                        P.w("tensor", "ACT", last_e[ex])
                        pe_c = ph1()
                        P.w("scalar", "PE", pe_c)
                        last_e[ex] = phe()
                        prev = pend[ex]
                        pend[ex] = (ph2, last_e[ex], i == 0)
                        if prev is not None:
                            p2, ec, isf = prev
                            if isf:
                                P.w("tensor", "DVE", unit_dve[ex])
                            P.w("tensor", "ACT", ec)
                            p2()
                for ex, tl in ((0, tA), (1, tB)):
                    p2, ec, isf = pend[ex]
                    pend[ex] = None
                    if isf:
                        P.w("tensor", "DVE", unit_dve[ex])
                    P.w("tensor", "ACT", ec)
                    p2()
                    tl()
                    unit_dve[ex] = inst_free[ex]["DVE"]
        P.barrier()

    def src_xln(cp, tc):
        return xln_sb[:, cp * 2048 + tc * 512:cp * 2048 + (tc + 1) * 512]

    def src_h(cp, tc):
        return h_sb[:, cp * 2048 + tc * 512:cp * 2048 + (tc + 1) * 512]

    def src_x1(cp, tc):
        return v_sb[:, cp * 2048 + tc * 512:cp * 2048 + (tc + 1) * 512]

    def src_qt(cp, tc):
        return qkt[:, cp * 2048 + tc * 512:cp * 2048 + (tc + 1) * 512]

    def load_x_chunk(b, tc):
        src = xbf[b].rearrange("(cp p) t -> p cp t", p=128)[
            :, :, tc * 512:(tc + 1) * 512]
        P.w("sync", *cast_done["x%d" % b])
        qguard()
        return ld("WX", xchunk[:, :], src)

    def gstore(dst, src):
        P.w("gpsimd", "PE", P.c["PE"])
        P.w("gpsimd", "DVE", P.c["DVE"])
        P.w("gpsimd", "ACT", P.c["ACT"])
        return P.inc("gpsimd", lambda e, d=dst, s=src: e.dma_start(d, s),
                     "TS", 16)

    def fm_tap(nme, b, srct):
        if nme not in taps:
            return
        dst = tap_out[nme] if b is None else tap_out[nme][b]
        c = gstore(dst.rearrange("(cp p) t -> p cp t", p=128),
                   srct.rearrange("p (cp t) -> p cp t", cp=8))
        P.w("gpsimd", "TS", c)

    pool_free = {}

    def run_b(b):
        phase("b%d:ln0" % b)
        for tc in range(NTC):
            c = load_x_chunk(b, tc)
            P.w("tensor", *c)
            P.w("vector", *c)
            fm_ln(lambda cp, t: xchunk[:, cp * 512:(cp + 1) * 512],
                  lambda cp, t, tc=tc: src_h(cp, tc), tcs=[tc])
        P.barrier()
        fm_tap("h0", b, h_sb)
        if last_stage == "h0":
            return

        phase("ln1")
        fm_ln(src_h, src_xln, g="ln1_g", bnm="ln1_b")
        P.barrier()
        fm_tap("xln1", b, xln_sb)
        if last_stage == "xln1":
            return

        phase("qkv_qk")
        fm_proj(wb["cattn"], cast_done["cattn"], range(8), src_xln,
                evac_act(lambda fp, tc: qkt[:, fp * 2048 + tc * 512:
                                            fp * 2048 + (tc + 1) * 512],
                         "cattn_b"))
        fm_proj(wb["cattn"], cast_done["cattn"], range(8), src_xln,
                evac_act(lambda fp, tc: qkt[:, 16384 + fp * 2048 + tc * 512:
                                            16384 + fp * 2048 + (tc + 1) * 512],
                         "cattn_b", bias_js=[8 + j for j in range(8)]),
                wfp_base=8)

        def v_tm(wdram, wdep, f_base, vdst16):
            wv = wdram.rearrange("(cp p) f -> p cp f", p=128)
            qguard()
            P.w("sync", *wdep)
            ca = ld("WV0", wbuf[:, 0:4096], wv[:, :, f_base:f_base + 512])
            cb = ld("WV1", wbuf[:, 4096:8192],
                    wv[:, :, f_base + 512:f_base + 1024])
            kv = 0
            for tt in range(16):
                tc, to = tt // 4, (tt % 4) * 128
                for fc in range(2):
                    bk = kv % 4
                    kv += 1
                    take_bank(bk)
                    P.w("tensor", *(cb if fc else ca))
                    for cp in range(NCP):
                        mm(bank(bk),
                           vdst16(cp, tc, to),
                           wbuf[:, fc * 4096 + cp * 512:
                                fc * 4096 + (cp + 1) * 512],
                           cp == 0, cp == NCP - 1)
                    pe_c = pe_mark()
                    P.w("vector", "PE", pe_c)
                    dstv = v_sb[:, tt * 1024 + fc * 512:tt * 1024 + (fc + 1) * 512]
                    v_c = P.inc("vector", lambda e, d=dstv, p=bank(bk):
                                e.tensor_copy(d, p), "DVE")
                    bank_free[bk] = ("DVE", v_c)

        phase("qkv_v")
        v_tm(wb["cattn"], cast_done["cattn"], 2048,
             lambda cp, tc, to: src_xln(cp, tc)[:, to:to + 128])
        ring1.bump(P.c["PE"])
        ring_mp.bump(P.c["PE"])
        P.barrier()
        if "qt" in taps:
            c = gstore(tap_out["qt"][b].rearrange("(cp p) t -> p cp t", p=128),
                       qkt[:, 0:16384].rearrange("p (cp t) -> p cp t", cp=8))
            c = gstore(tap_out["kt"][b].rearrange("(cp p) t -> p cp t", p=128),
                       qkt[:, 16384:32768].rearrange("p (cp t) -> p cp t", cp=8))
            c = gstore(tap_out["v"][b].rearrange("(tt p) f -> p tt f", p=128),
                       v_sb[:, :].rearrange("p (tt f) -> p tt f", tt=16))
            P.w("gpsimd", "TS", c)
        if last_stage == "qkv":
            return

        phase("selfattn")
        attention(False)
        if last_stage == "selfattn":
            fm_tap("hself", b, qkt[0:128, 0:16384])
            return

        phase("aproj")
        fm_proj(wb["aproj"], cast_done["aproj"], range(8), src_qt,
                evac_resid(src_h, "aproj_b", src_h))
        P.barrier()
        fm_tap("hself", b, h_sb)
        if last_stage == "hself":
            return

        phase("enckv")
        if b == 0:
            for tc in range(NTC):
                src = enc_fm.rearrange("(cp p) t -> p cp t", p=128)[
                    :, :, tc * 512:(tc + 1) * 512]
                P.w("gpsimd", "PE", P.c["PE"])
                P.w("gpsimd", "DVE", P.c["DVE"])
                c = ldg("WXG", xchunk[:, :], src)
                P.w("tensor", *c)
                fm_proj(wb["xkv"], cast_done["xkv"], range(8),
                        lambda cp, t: xchunk[:, cp * 512:(cp + 1) * 512],
                        evac_act(lambda fp, t, tc=tc: qkt[
                            :, 16384 + fp * 2048 + tc * 512:
                            16384 + fp * 2048 + (tc + 1) * 512], "xkv_b"),
                        tcs=[0])
                wv2 = wb["xkv"].rearrange("(cp p) f -> p cp f", p=128)
                qguard()
                P.w("sync", *cast_done["xkv"])
                ca = ld("WV0", wbuf[:, 0:4096], wv2[:, :, 1024:1536])
                cb = ld("WV1", wbuf[:, 4096:8192], wv2[:, :, 1536:2048])
                for ti in range(4):
                    tt = tc * 4 + ti
                    for fc in range(2):
                        bk = (ti * 2 + fc) % 4
                        take_bank(bk)
                        P.w("tensor", *(cb if fc else ca))
                        for cp in range(NCP):
                            mm(bank(bk),
                               xchunk[:, cp * 512 + ti * 128:
                                      cp * 512 + ti * 128 + 128],
                               wbuf[:, fc * 4096 + cp * 512:
                                    fc * 4096 + (cp + 1) * 512],
                               cp == 0, cp == NCP - 1)
                        pe_c = pe_mark()
                        P.w("vector", "PE", pe_c)
                        dstv = v_sb[:, tt * 1024 + fc * 512:
                                    tt * 1024 + (fc + 1) * 512]
                        v_c = P.inc("vector", lambda e, d=dstv, p=bank(bk):
                                    e.tensor_copy(d, p), "DVE")
                        bank_free[bk] = ("DVE", v_c)
                ring1.bump(P.c["PE"])
                ring_mp.bump(P.c["PE"])
            P.barrier()
            qguard()
            store("KS", k2t_s.rearrange("(cp p) t -> p cp t", p=128),
                  qkt[:, 16384:32768].rearrange("p (cp t) -> p cp t", cp=8))
            c2 = store("KS", v2_s.rearrange("(tt p) f -> p tt f", p=128),
                       v_sb[:, :].rearrange("p (tt f) -> p tt f", tt=16))
            if "k2t" in taps:
                gstore(tap_out["k2t"].rearrange("(cp p) t -> p cp t", p=128),
                       qkt[:, 16384:32768].rearrange("p (cp t) -> p cp t", cp=8))
                cx = gstore(tap_out["v2"].rearrange("(tt p) f -> p tt f", p=128),
                            v_sb[:, :].rearrange("p (tt f) -> p tt f", tt=16))
                P.w("gpsimd", "TS", cx)
            enc_dep = c2
        else:
            qguard()
            P.w("sync", "KS", 32)
            ld("KL", qkt[:, 16384:32768].rearrange("p (cp t) -> p cp t", cp=8),
               k2t_s.rearrange("(cp p) t -> p cp t", p=128))
            c2 = ld("VL", v_sb[:, :].rearrange("p (tt f) -> p tt f", tt=16),
                    v2_s.rearrange("(tt p) f -> p tt f", p=128))
            P.w("tensor", "KL", P.c["KL"])
            enc_dep = c2

        phase("lnx")
        fm_ln(src_h, src_xln, g="lnx_g", bnm="lnx_b")
        P.barrier()
        phase("qattn")
        fm_proj(wb["qattn"], cast_done["qattn"], range(8), src_xln,
                evac_act(lambda fp, tc: qkt[:, fp * 2048 + tc * 512:
                                            fp * 2048 + (tc + 1) * 512],
                         "qattn_b"))
        P.w("tensor", *enc_dep)
        P.barrier()
        phase("crossattn")
        attention(True)
        if last_stage == "crossattn":
            fm_tap("hcross", b, qkt[0:128, 0:16384])
            return

        phase("xproj")
        fm_proj(wb["xproj"], cast_done["xproj"], range(8), src_qt,
                evac_resid(src_h, "xproj_b", src_h))
        P.barrier()
        fm_tap("hcross", b, h_sb)
        if last_stage == "hcross":
            return

        phase("ln2")
        fm_ln(src_h, src_xln, g="ln2_g", bnm="ln2_b")
        P.barrier()
        phase("fc_mproj")
        wfc = wb["fc"].rearrange("(cp p) f -> p cp f", p=128)
        wmp = wb["mproj"].rearrange("(hp p) f -> p hp f", p=128)
        hid_free_pe = 0
        for tc in range(NTC):
            for hp in range(32):
                s, wsl, c = ring1.load(wfc[:, :, hp * 128:(hp + 1) * 128],
                                       cast_done["fc"])
                bk = hp % 4
                take_bank(bk)
                P.w("tensor", *c)
                for cp in range(NCP):
                    mm(bank(bk), wsl[:, cp * 128:(cp + 1) * 128],
                       src_xln(cp, tc), cp == 0, cp == NCP - 1)
                pe_c = pe_mark()
                ring1.done(s, pe_c)
                P.w("scalar", "PE", max(pe_c, hid_free_pe))
                dsth = qkt[:, 16384 + hp * 512:16384 + (hp + 1) * 512]
                a_c = P.inc("scalar", lambda e, d=dsth, p=bank(bk),
                            bb=bias_ap("fc_b", hp): e.activation(
                                d, p, AF.Gelu_apprx_tanh, bias=bb, scale=1.0),
                            "ACT")
                bank_free[bk] = ("ACT", a_c)
            for fp in range(8):
                bk = 4 + fp % 2
                take_bank(bk)
                P.w("tensor", "ACT", P.c["ACT"])
                for half in range(2):
                    s, wsl, c = ring_mp.load(
                        wmp[:, half * 16:(half + 1) * 16,
                            fp * 128:(fp + 1) * 128], cast_done["mproj"])
                    P.w("tensor", *c)
                    for hh in range(16):
                        hp = half * 16 + hh
                        mm(bank(bk), wsl[:, hh * 128:(hh + 1) * 128],
                           qkt[:, 16384 + hp * 512:16384 + (hp + 1) * 512],
                           hp == 0, hp == 31)
                    pe_c = pe_mark()
                    ring_mp.done(s, pe_c)
                P.w("vector", "PE", pe_c)
                v_c = P.inc("vector", lambda e, d=src_h(fp, tc), p=bank(bk),
                            bb=bias_ap("mproj_b", fp), r=src_h(fp, tc):
                            e.scalar_tensor_tensor(d, p, bb, r, ALU.add,
                                                   ALU.add), "DVE")
                bank_free[bk] = ("DVE", v_c)
            hid_free_pe = P.c["PE"]
        P.barrier()
        fm_tap("hmlp", b, h_sb)
        if last_stage == "hmlp":
            return

        phase("outer1")
        for tc in range(NTC):
            c = load_x_chunk(b, tc)
            P.w("vector", *c)

            def out_add(cp, t, d, tc=tc):
                P.inc("vector", lambda e, d=d, o=src_x1(cp, tc),
                      a=xchunk[:, cp * 512:(cp + 1) * 512]:
                      e.tensor_tensor(o, d, a, ALU.add), "DVE")

            fm_ln(src_h, lambda cp, t, tc=tc: src_xln(cp, tc), tcs=[tc],
                  out_fn=out_add)
        P.barrier()
        fm_tap("x1", b, v_sb)
        if last_stage == "x1":
            return

        phase("x1ln")
        fm_ln(src_x1, src_xln)
        P.barrier()
        fm_tap("x1ln", b, xln_sb)

        def load_expert(e_, w_pe=(0, 0)):
            if e_ == -1:
                s1_, s2_ = wb["sw1"], wb["sw2"]
                d1, d2 = cast_done["sw1"], cast_done["sw2"]
            else:
                s1_, s2_ = wb["rw1"][e_], wb["rw2"][e_]
                d1, d2 = cast_done["rw1_%d" % e_], cast_done["rw2_%d" % e_]
            P.w("sync", "PE", w_pe[0])
            P.w("sync", *d1)
            c1 = ld("WE1", h_sb[:, :].rearrange("p (cp f) -> p cp f", cp=8),
                    s1_.rearrange("(cp p) f -> p cp f", p=128))
            P.w("sync", "PE", w_pe[1])
            P.w("sync", *d2)
            c2 = ld("WE2", wbuf[:, 0:8192].rearrange("p (hp f) -> p hp f", hp=8),
                    s2_.rearrange("(hp p) f -> p hp f", p=128))
            return c1, c2

        if MOVE_LE:
            cexp = None
        else:
            qguard()
            cexp = load_expert(-1)

        phase("gate")
        wg = wb["gate"].rearrange("(cp p) f -> p cp f", p=128)
        P.w("sync", *cast_done["gate"])
        cg = ld("WG", gwb[:, 0:64], wg[:, :, :])
        for tt in range(16):
            tc, to = tt // 4, (tt % 4) * 128
            bkg = GATE_BANKS[tt % 2]
            take_bank(bkg)
            P.w("tensor", *cg)
            for cp in range(NCP):
                mm(bank(bkg, 0, 128, 0, 8), src_xln(cp, tc)[:, to:to + 128],
                   gwb[:, cp * 8:(cp + 1) * 8], cp == 0, cp == NCP - 1)
            pe_c = pe_mark()
            P.w("vector", "PE", pe_c)
            g0 = gsm[:, 0:8]
            P.inc("vector", lambda e, p=bank(bkg, 0, 128, 0, 8): e.tensor_tensor(
                g0, p, gbb[:, :], ALU.add), "DVE")
            bank_free[bkg] = ("DVE", P.c["DVE"])
            m1 = gsm[:, 8:9]
            P.inc("vector", lambda e: e.reduce_max(m1, g0, AX.X,
                                                   negate=True), "DVE")
            ge = gsm[:, 16:24]
            P.w("scalar", "DVE", P.c["DVE"])
            a_e = P.inc("scalar", lambda e: e.activation(
                ge, g0, AF.Exp, bias=m1, scale=1.0), "ACT")
            P.w("vector", "ACT", a_e)
            sm = gsm[:, 9:10]
            P.inc("vector", lambda e: e.reduce_sum(sm, ge, AX.X), "DVE")
            rc = gsm[:, 10:11]
            P.inc("vector", lambda e: e.reciprocal(rc, sm), "DVE")
            pr = gsm[:, 24:32]
            P.inc("vector", lambda e: e.tensor_scalar(pr, ge, rc, None,
                                                      ALU.mult), "DVE")
            mp = gsm[:, 11:12]
            P.inc("vector", lambda e: e.reduce_max(mp, pr, AX.X), "DVE")
            s1 = gsm[:, 32:40]
            P.inc("vector", lambda e: e.tensor_scalar(s1, pr, mp, None,
                                                      ALU.is_ge), "DVE")
            pq = gsm[:, 40:48]
            P.inc("vector", lambda e: e.scalar_tensor_tensor(
                pq, s1, -1e30, pr, ALU.mult, ALU.add), "DVE")
            m2 = gsm[:, 12:13]
            P.inc("vector", lambda e: e.reduce_max(m2, pq, AX.X), "DVE")
            P.inc("vector", lambda e: e.tensor_scalar(s1, pr, m2, None,
                                                      ALU.is_ge), "DVE")
            v_cw = P.inc("vector", lambda e, tt=tt: e.tensor_tensor(
                cwtm[:, tt * 8:(tt + 1) * 8], pr, s1, ALU.mult), "DVE")
            if GATE_SYNC:
                P.barrier()
        P.barrier()
        if "cw" in taps:
            c = gstore(tap_out["cw"][b], cwtm[:, :])
            P.w("gpsimd", "TS", c)
        if last_stage == "gate":
            return

        if cexp is None:
            qguard()
            cexp = load_expert(-1)
        swz_free_pe = 0
        for ei, e_ in enumerate([-1, 0, 1, 2, 3, 4, 5, 6, 7][:N_EXP_DBG]):
            phase("exp%d" % e_)
            c1, c2 = cexp
            routed = e_ >= 0
            last_w1 = last_w2 = 0
            for tc in range(NTC):
                if routed:
                    take_bank(6)
                    for ti in range(4):
                        mmc[0] += 1
                        P.on("tensor", lambda e,
                             o=bank(6, 0, 1, ti * 128, ti * 128 + 128),
                             i_=cwtm[:, (tc * 4 + ti) * 8 + e_:
                                     (tc * 4 + ti) * 8 + e_ + 1]:
                             e.transpose(o, i_, id32[:, :]))
                    c_tr = pe_mark()
                    P.w("vector", "PE", c_tr)
                    v_cr = P.inc("vector", lambda e: e.tensor_copy(
                        crow[0:1, :], bank(6, 0, 1, 0, 512)), "DVE")
                    P.w("tensor", "DVE", v_cr)
                    mm(bank(6), ones_bf[0:1, :], crow[0:1, :], True, True)
                    c_bc = pe_mark()
                    P.w("vector", "PE", c_bc)
                    v_bc = P.inc("vector", lambda e: e.tensor_copy(
                        bcsb[:, 0:512], bank(6)), "DVE")
                    bank_free[6] = ("DVE", v_bc)
                P.w("vector", "PE", swz_free_pe)
                for j in range(8 if "hid" not in DBG_SKIP else 0):
                    bka, bkb = (j % 2) * 2, (j % 2) * 2 + 1
                    take_bank(bka)
                    P.w("tensor", *c1)
                    for cp in range(NCP):
                        mm(bank(bka), h_sb[:, cp * 2048 + j * 128:
                                           cp * 2048 + j * 128 + 128],
                           src_xln(cp, tc), cp == 0, cp == NCP - 1)
                    take_bank(bkb)
                    for cp in range(NCP):
                        mm(bank(bkb), h_sb[:, cp * 2048 + 1024 + j * 128:
                                           cp * 2048 + 1024 + j * 128 + 128],
                           src_xln(cp, tc), cp == 0, cp == NCP - 1)
                    pe_c = pe_mark()
                    if tc == NTC - 1 and j == 7:
                        last_w1 = pe_c
                    if "silu" in DBG_SKIP:
                        bank_free[bka] = ("PE", pe_c)
                        bank_free[bkb] = ("PE", pe_c)
                        continue
                    P.w("scalar", "PE", pe_c)
                    sl = sqbuf[:, (j % 2) * 512:(j % 2 + 1) * 512]
                    bj = ("sb1", j) if not routed else ("rb1", e_ * 16 + j)
                    a_c = P.inc("scalar", lambda e, s=sl, p=bank(bka),
                                bb=bias_ap(*bj): e.activation(
                                    s, p, AF.Silu, bias=bb, scale=1.0), "ACT")
                    bank_free[bka] = ("ACT", a_c)
                    if "stt" in DBG_SKIP:
                        bank_free[bkb] = ("ACT", a_c)
                        continue
                    P.w("vector", "ACT", a_c)
                    bj2 = ("sb1", 8 + j) if not routed else \
                          ("rb1", e_ * 16 + 8 + j)
                    v_c = P.inc("vector", lambda e,
                                d=xchunk[:, j * 512:(j + 1) * 512],
                                p=bank(bkb), bb=bias_ap(*bj2), s=sl:
                                e.scalar_tensor_tensor(d, p, bb, s, ALU.add,
                                                       ALU.mult), "DVE")
                    bank_free[bkb] = ("DVE", v_c)
                for fp in range(8):
                    if "w2" in DBG_SKIP:
                        break
                    if "w2one" in DBG_SKIP and (fp > 0 or tc > 0):
                        break
                    bk = 4 + fp % 2
                    take_bank(bk)
                    P.w("tensor", *c2)
                    if "pedrain" in DBG_SKIP:
                        P.w("tensor", "PE", P.c["PE"])
                    if "now2dve" not in DBG_SKIP:
                        P.w("tensor", "DVE", P.c["DVE"])
                    for hp in range(NCP):
                        rhs_ = xchunk[:, hp * 512:(hp + 1) * 512] \
                            if "w2xln" not in DBG_SKIP else src_xln(hp, tc)
                        mm(bank(bk), wbuf[:, hp * 1024 + fp * 128:
                                          hp * 1024 + fp * 128 + 128],
                           rhs_, hp == 0, hp == NCP - 1)
                    pe_c = pe_mark()
                    if tc == NTC - 1 and fp == 7:
                        last_w2 = pe_c
                    acc_ap = accum[:, fp * 2048 + tc * 512:
                                   fp * 2048 + (tc + 1) * 512]
                    P.w("vector", "PE", pe_c)
                    if "acc" in DBG_SKIP:
                        bank_free[bk] = ("PE", pe_c)
                        continue
                    if not routed:
                        v_c = P.inc("vector", lambda e, a=acc_ap, p=bank(bk),
                                    bb=bias_ap("sb2", fp): e.tensor_scalar(
                                        a, p, bb, None, ALU.add), "DVE")
                        bank_free[bk] = ("DVE", v_c)
                    else:
                        tsl = tmp1[:, (fp % 2) * 512:(fp % 2 + 1) * 512]
                        P.w("vector", "POOL", pool_free.get(fp % 2, 0))
                        v_c = P.inc("vector", lambda e, t=tsl, p=bank(bk),
                                    bb=bias_ap("rb2", e_ * 8 + fp):
                                    e.scalar_tensor_tensor(
                                        t, p, bb, bcsb[:, 0:512], ALU.add,
                                        ALU.mult), "DVE")
                        bank_free[bk] = ("DVE", v_c)
                        P.w("gpsimd", "DVE", v_c)
                        pc = P.inc("gpsimd", lambda e, a=acc_ap, t=tsl:
                                   e.tensor_tensor(a, a, t, ALU.add), "POOL")
                        pool_free[fp % 2] = pc
                swz_free_pe = P.c["PE"]
            if ei < 8 and "pref" not in DBG_SKIP:
                cexp = load_expert(e_ + 1 if routed else 0,
                                   w_pe=(last_w1, last_w2))
        ring1.bump(P.c["PE"])
        ring_mp.bump(P.c["PE"])
        P.barrier()
        if "acc" in taps:
            c = gstore(tap_out["acc"][b].rearrange("(cp p) t -> p cp t", p=128),
                       accum[:, :].rearrange("p (cp t) -> p cp t", cp=8))
            P.w("gpsimd", "TS", c)
        if last_stage == "acc":
            return

        phase("finalln")
        st_prev = [0]
        for tc in range(NTC):
            P.barrier()
            P.w("vector", "PE", P.c["PE"])
            P.w("vector", "POOL", P.c["POOL"])
            for cp in range(NCP):
                P.inc("vector", lambda e, cp=cp, tc=tc: e.tensor_copy(
                    xchunk[:, cp * 512:(cp + 1) * 512],
                    accum[:, cp * 2048 + tc * 512:cp * 2048 + (tc + 1) * 512]),
                    "DVE")

            def out_store(cp, t, d, tc=tc):
                P.w("vector", "OS", st_prev[0])
                ost = ostage[:, 0:512]
                v_c = P.inc("vector", lambda e, o=ost, d=d,
                            a=src_x1(cp, tc): e.tensor_tensor(o, d, a, ALU.add),
                            "DVE")
                P.w("sync", "DVE", v_c)
                st_prev[0] = store(
                    "OS", y[b][cp * 128:(cp + 1) * 128,
                               tc * 512:(tc + 1) * 512], ost)[1]

            fsrc = lambda cp, t: xchunk[:, cp * 512:(cp + 1) * 512]
            DBG_LN_SRC[0] = fsrc
            fm_ln(fsrc, lambda cp, t, tc=tc: src_xln(cp, tc), tcs=[tc],
                  out_fn=out_store)
        P.barrier()

    if last_stage == "init":
        if "init" in taps:
            qguard()
            c = store("OS", tap_out["init"], bia[:, :])
    else:
        for b in range(n_b):
            run_b(b)

    for s in DMA_SEMS:
        eng = "gpsimd" if (s == "TS" or s.startswith(("cx", "cw_", "cr"))
                           or s in ("WC", "WXG")) else "sync"
        P.w(eng, s, P.c[s])

    with nc.Block() as block:
        with nc.allow_low_precision(reason="bf16 operand staging is intentional"):
            P.install(block)
    return nc, list(tap_out)


def _pack_biases(i):
    cols = np.zeros((128, BIAS_COLS), dtype=np.float32)

    def put(nme, vec):
        v = np.asarray(vec, dtype=np.float32).reshape(-1)
        n = v.size // 128
        cols[:, BIAS_OFF[nme]:BIAS_OFF[nme] + n] = v.reshape(n, 128).T

    put("ln1_g", i["L0_attn_ln1_g"]); put("ln1_b", i["L0_attn_ln1_b"])
    put("lnx_g", i["L0_attn_lnx_g"]); put("lnx_b", i["L0_attn_lnx_b"])
    put("ln2_g", i["L0_attn_ln2_g"]); put("ln2_b", i["L0_attn_ln2_b"])
    put("cattn_b", i["L0_attn_c_attn_b"])
    put("aproj_b", i["L0_attn_attn_proj_b"])
    put("qattn_b", i["L0_attn_q_attn_b"])
    put("xkv_b", i["L0_attn_x_kv_b"])
    put("xproj_b", i["L0_attn_x_proj_b"])
    put("fc_b", i["L0_attn_fc_b"])
    put("mproj_b", i["L0_attn_mlp_proj_b"])
    put("sb1", i["L0_ffn_sb1"]); put("sb2", i["L0_ffn_sb2"])
    put("rb1", i["L0_ffn_rb1"]); put("rb2", i["L0_ffn_rb2"])
    put("scorer", i["L0_attn_scorer_logits"])
    gb = np.zeros(128, dtype=np.float32)
    gb[0:8] = np.asarray(i["L0_ffn_gate_b"], dtype=np.float32)
    cols[:, BIAS_OFF["gate_b"]] = gb
    cols[:, BIAS_OFF["c_eps"]] = 1e-5
    cols[:, BIAS_OFF["c_shift"]] = -20.0
    return cols


def kernel(**inputs):
    n_b = N_B
    i = inputs
    cc = np.ascontiguousarray

    x = np.asarray(i["L0_attn_hidden_states"], dtype=np.float32)
    enc = np.asarray(i["L0_attn_encoder_hidden_states"], dtype=np.float32)[0]
    biases = _pack_biases(i)
    jj = np.arange(644)[None, :]
    pp = np.arange(128)[:, None]
    mask = np.where(jj >= pp, 0.0, -10000.0).astype(np.float32)
    ones_bf = np.ones((128, 128), dtype=np.float32)
    sel = np.zeros((33, 128), dtype=np.float32)
    sel[0, 0:64] = 1.0
    sel[32, 64:128] = 1.0
    id32 = np.eye(128, dtype=np.float32)

    base = dict(
        gbrow=np.tile(np.asarray(i["L0_ffn_gate_b"], dtype=np.float32)[None, :],
                      (128, 1)),
        enc_fm=cc(enc.T),
        biases=biases,
        maskbf=mask,
        ones_bf=ones_bf,
        sel33=sel,
        id32=id32,
        w_cattn=np.asarray(i["L0_attn_c_attn_w"], dtype=np.float32),
        w_qattn=np.asarray(i["L0_attn_q_attn_w"], dtype=np.float32),
        w_xkv=np.asarray(i["L0_attn_x_kv_w"], dtype=np.float32),
        w_aproj=np.asarray(i["L0_attn_attn_proj_w"], dtype=np.float32),
        w_xproj=np.asarray(i["L0_attn_x_proj_w"], dtype=np.float32),
        w_fc=np.asarray(i["L0_attn_fc_w"], dtype=np.float32),
        w_mproj=np.asarray(i["L0_attn_mlp_proj_w"], dtype=np.float32),
        w_gate=np.asarray(i["L0_ffn_gate_w"], dtype=np.float32),
        w_sw1=np.asarray(i["L0_ffn_sw1"], dtype=np.float32)[0],
        w_sw2=np.asarray(i["L0_ffn_sw2"], dtype=np.float32)[0],
        w_rw1=np.asarray(i["L0_ffn_rw1"], dtype=np.float32),
        w_rw2=np.asarray(i["L0_ffn_rw2"], dtype=np.float32),
    )

    nc, tap_names = build_nc(n_b, TAPS, LAST_STAGE)
    in_maps = []
    for c in range(8):
        m = dict(base)
        xs = x[c * n_b:(c + 1) * n_b]
        m["x_fm"] = cc(xs.transpose(0, 2, 1))
        in_maps.append(m)

    res = run_bass_kernel_spmd(nc, in_maps, core_ids=list(range(8)))

    global _dbg
    _dbg = {t: [np.asarray(r["dbg_" + t]) for r in res.results]
            for t in tap_names}

    outs = []
    for c in range(8):
        yc = np.asarray(res.results[c]["y"])
        outs.append(yc.transpose(0, 2, 1))
    return np.concatenate(outs, axis=0).astype(np.float32)


# version 05
# speedup vs baseline: 1.2349x; 1.2349x over previous
import numpy as np
import ml_dtypes
import concourse.bass as bass
import concourse.mybir as mybir
from concourse.bass_utils import run_bass_kernel_spmd

F32 = mybir.dt.float32
BF16 = mybir.dt.bfloat16
AF = mybir.ActivationFunctionType
ALU = mybir.AluOpType
AX = mybir.AxisListType
BF = ml_dtypes.bfloat16

ENGINES = ("sync", "gpsimd", "tensor", "vector", "scalar")
SEMS = ("LD", "LDG", "PE", "ACT", "DVE", "POOL", "ST")

T = 2048
NTC = 4
D = 1024
NCP = 8
EPS = 1e-5
SHIFT = -20.0

N_B = 4
TAPS = ()
LAST_STAGE = "all"
_dbg = {}

BIAS_LAYOUT = [
    ("ln1_g", 8), ("ln1_b", 8), ("lnx_g", 8), ("lnx_b", 8),
    ("ln2_g", 8), ("ln2_b", 8),
    ("cattn_b", 24), ("aproj_b", 8), ("qattn_b", 8), ("xkv_b", 16),
    ("xproj_b", 8), ("fc_b", 32), ("mproj_b", 8),
    ("sb1", 16), ("sb2", 8), ("rb1", 128), ("rb2", 64),
    ("scorer", 16), ("gate_b", 1), ("c_eps", 1), ("c_shift", 1),
]
BIAS_OFF = {}
_o = 0
for _n, _c in BIAS_LAYOUT:
    BIAS_OFF[_n] = _o
    _o += _c
BIAS_COLS = _o


class Prog:
    def __init__(self):
        self.q = {e: [] for e in ENGINES}
        self.c = {}
        self.sem = {}
        self.last_wait = {e: {} for e in ENGINES}
        self.real_top = {e: False for e in ENGINES}

    def on(self, eng, fn):
        self.q[eng].append(fn)
        self.real_top[eng] = True

    OWN = {"vector": "DVE", "scalar": "ACT", "gpsimd": "POOL"}

    def inc(self, eng, fn, sem, n=1):
        own = self.OWN.get(eng)
        if own is not None:
            self.w(eng, own, self.c.get(own, 0))
        self.c[sem] = self.c.get(sem, 0) + n
        s = self.sem[sem]
        self.q[eng].append(lambda e, fn=fn, s=s, n=n: fn(e).then_inc(s, n))
        self.real_top[eng] = False
        return self.c[sem]

    def mark(self, eng, sem):
        assert self.real_top[eng], "mark() must follow a real engine op"
        f = self.q[eng].pop()
        self.c[sem] = self.c.get(sem, 0) + 1
        s = self.sem[sem]
        self.q[eng].append(lambda e, f=f, s=s: f(e).then_inc(s, 1))
        self.real_top[eng] = False
        return self.c[sem]

    def w(self, eng, sem, val):
        if val <= 0 or val <= self.last_wait[eng].get(sem, 0):
            return
        self.last_wait[eng][sem] = val
        s = self.sem[sem]
        self.q[eng].append(lambda e, s=s, v=val: e.wait_ge(s, v))
        self.real_top[eng] = False

    def barrier(self):
        engs = ("tensor", "vector", "scalar", "gpsimd")
        m = {"tensor": "PE", "vector": "DVE", "scalar": "ACT", "gpsimd": "POOL"}
        for e in engs:
            if self.real_top[e]:
                self.mark(e, m[e])
        snap = dict(self.c)
        for e in engs:
            for o in engs:
                if o != e:
                    self.w(e, m[o], snap.get(m[o], 0))

    def install(self, block):
        for e in ENGINES:
            ops = self.q[e]
            getattr(block, e)(lambda eng, ops=ops: [f(eng) for f in ops])


GATE_SYNC = False
GATE_BANKS = (0, 1)
DBG_LN = False
DBG_LN_TC = 0
DBG_LN_SRC = [None]
N_EXP_DBG = 9
MOVE_LE = False
DBG_SKIP = frozenset()
PHASE_LOG = []


def build_nc(n_b, taps, last_stage):
    nc = bass.Bass("TRN2", target_bir_lowering=False)
    P = Prog()
    taps = set(taps)

    x_fm = nc.dram_tensor("x_fm", [n_b, D, T], F32, kind="ExternalInput").ap()
    enc_fm = nc.dram_tensor("enc_fm", [D, T], F32, kind="ExternalInput").ap()
    biases_i = nc.dram_tensor("biases", [128, BIAS_COLS], F32,
                              kind="ExternalInput").ap()
    mask_i = nc.dram_tensor("maskbf", [128, 644], F32, kind="ExternalInput").ap()
    ones_i = nc.dram_tensor("ones_bf", [128, 128], F32, kind="ExternalInput").ap()
    sel_i = nc.dram_tensor("sel33", [33, 128], F32, kind="ExternalInput").ap()
    id_i = nc.dram_tensor("id32", [128, 128], F32, kind="ExternalInput").ap()
    gbrow_i = nc.dram_tensor("gbrow", [128, 8], F32, kind="ExternalInput").ap()
    wshape = dict(cattn=[D, 3072], qattn=[D, D], xkv=[D, 2048], aproj=[D, D],
                  xproj=[D, D], fc=[D, 4096], mproj=[4096, D], gate=[D, 8],
                  sw1=[D, 2048], sw2=[D, D], rw1=[8, D, 2048], rw2=[8, D, D])
    w_in, wb = {}, {}
    for nme, shp in wshape.items():
        w_in[nme] = nc.dram_tensor("w_" + nme, shp, F32, kind="ExternalInput").ap()
        wb[nme] = nc.dram_tensor("wb_" + nme, shp, BF16).ap()
    y = nc.dram_tensor("y", [n_b, D, T], F32, kind="ExternalOutput").ap()
    xbf = nc.dram_tensor("xbf", [n_b, D, T], BF16).ap()
    k2t_s = nc.dram_tensor("k2t_s", [D, T], BF16).ap()
    v2_s = nc.dram_tensor("v2_s", [T, D], BF16).ap()

    tap_out = {}
    for nme, shp in [("init", [128, BIAS_COLS]),
                     ("h0", [n_b, D, T]), ("xln1", [n_b, D, T]),
                     ("qt", [n_b, D, T]), ("kt", [n_b, D, T]),
                     ("v", [n_b, T, D]), ("hself", [n_b, D, T]),
                     ("k2t", [D, T]), ("v2", [T, D]),
                     ("hcross", [n_b, D, T]),
                     ("hmlp", [n_b, D, T]), ("x1", [n_b, D, T]),
                     ("x1ln", [n_b, D, T]), ("cw", [n_b, 128, 128]),
                     ("acc", [n_b, D, T])]:
        if nme in taps:
            tap_out[nme] = nc.dram_tensor("dbg_" + nme, shp, F32,
                                          kind="ExternalOutput").ap()

    A = nc.alloc_sbuf_tensor
    ones_bf = A("s_ones_bf", [128, 128], BF16)
    sel33 = A("s_sel33", [33, 128], BF16)
    maskbf = A("s_maskbf", [128, 644], BF16)
    idbf = A("s_idbf", [128, 128], BF16)
    id32 = A("s_id32", [128, 128], F32)
    bia = A("bia", [128, BIAS_COLS], F32)
    scor20 = A("scor20", [128, 16], F32)
    mstat = A("mstat", [128, 512], F32)
    rstdb = A("rstdb", [128, 512], BF16)
    sqbuf = A("sqbuf", [128, 1024], BF16)
    rden = A("rden", [33, 512], BF16)
    bcsb = A("bcsb", [128, 1024], BF16)
    tmp1 = A("tmp1", [128, 1024], BF16)
    ostage = A("ostage", [128, 512], F32)
    expst = A("expst", [128, 2048], BF16)
    xchunk = A("xchunk", [128, 4096], BF16)
    wbuf = A("wbuf", [128, 8192], BF16)
    h_sb = A("h_sb", [128, 16384], BF16)
    xln_sb = A("xln_sb", [128, 16384], BF16)
    qkt = A("qkt", [128, 32768], BF16)
    v_sb = A("v_sb", [128, 16384], BF16)
    gsm = A("gsm", [128, 48], F32)
    gwb = A("gwb", [128, 64], BF16)
    gbb = A("gbb", [128, 8], F32)
    cwtm = A("cwtm", [128, 128], F32)
    crow = A("crow", [1, 512], BF16)
    accum = qkt.bitcast(F32)

    p01 = nc.alloc_psum_tensor("p01", [128, 1024], F32)
    p2 = nc.alloc_psum_tensor("p2", [128, 512], F32)
    p3 = nc.alloc_psum_tensor("p3", [128, 512], F32)
    p45 = nc.alloc_psum_tensor("p45", [128, 1024], F32)
    p6 = nc.alloc_psum_tensor("p6", [128, 512], F32)
    p7 = nc.alloc_psum_tensor("p7", [128, 512], F32)

    def bank(i, r0=0, r1=128, c0=0, c1=512):
        if i == 0:
            return p01[r0:r1, c0:c1]
        if i == 1:
            return p01[r0:r1, 512 + c0:512 + c1]
        if i == 4:
            return p45[r0:r1, c0:c1]
        if i == 5:
            return p45[r0:r1, 512 + c0:512 + c1]
        return {2: p2, 3: p3, 6: p6, 7: p7}[i][r0:r1, c0:c1]

    DMA_SEMS = (["WC", "WCH", "WX", "WXG", "WG", "WE1", "WE2", "WV0", "WV1", "KL", "VL",
                 "KS", "OS", "TS"]
                + ["WS%d" % j for j in range(4)] + ["WM%d" % j for j in range(2)]
                + ["cx%d" % b for b in range(n_b)]
                + ["cw_" + nm for nm in ("cattn", "qattn", "xkv", "aproj",
                                         "xproj", "fc", "mproj", "gate",
                                         "sw1", "sw2")]
                + ["cr1_%d" % e for e in range(8)]
                + ["cr2_%d" % e for e in range(8)])
    for s in list(SEMS) + DMA_SEMS:
        P.sem[s] = nc.semaphore(s).__enter__()
        P.c[s] = 0

    def ld(sem, dst, src):
        return (sem, P.inc("sync", lambda e, d=dst, s=src: e.dma_start(d, s),
                           sem, 16))

    def ldg(sem, dst, src):
        return (sem, P.inc("gpsimd", lambda e, d=dst, s=src: e.dma_start(d, s),
                           sem, 16))

    def store(sem, dst, src):
        return (sem, P.inc("sync", lambda e, d=dst, s=src: e.dma_start(d, s),
                           sem, 16))

    mmc = [0]

    def phase(nm):
        PHASE_LOG.append((nm, mmc[0]))

    def mm(out, lhsT, rhs, start, stop, tp=None, skip=False):
        mmc[0] += 1
        P.on("tensor", lambda e, o=out, l=lhsT, r=rhs, st=start, sp=stop, tp=tp,
             sk=skip: e.matmul(o, l, r, start=st, stop=sp, tile_position=tp,
                               skip_group_check=sk))

    def pe_mark():
        return P.mark("tensor", "PE")

    def tr(out, in_, ident):
        mmc[0] += 1
        P.on("tensor", lambda e, o=out, i=in_, d=ident: e.transpose(o, i, d))
        return pe_mark()

    bank_free = {}

    def take_bank(i):
        if i in bank_free:
            s, v = bank_free[i]
            P.w("tensor", s, v)

    def bias_ap(nme, j, n=1):
        off = BIAS_OFF[nme] + j
        return bia[:, off:off + n]

    def qguard():
        P.w("sync", "PE", P.c["PE"])
        P.w("sync", "DVE", P.c["DVE"])
        P.w("sync", "ACT", P.c["ACT"])
        P.w("sync", "POOL", P.c["POOL"])

    ldg("WC", ones_bf[:, :], ones_i[:, :])
    ldg("WC", sel33[:, :], sel_i[:, :])
    ldg("WC", maskbf[:, :], mask_i[:, :])
    ldg("WC", idbf[:, :], id_i[:, :])
    ld("WCH", id32[:, :], id_i[:, :])
    ld("WCH", gbb[:, :], gbrow_i[:, :])
    ld("WCH", bia[:, :], biases_i[:, :])
    WC_ALL = P.c["WC"]
    WCH_ALL = P.c["WCH"]

    cast_done = {}
    for b in range(n_b):
        cast_done["x%d" % b] = ldg("cx%d" % b, xbf[b], x_fm[b])
    for nme in ("cattn", "qattn", "xkv", "aproj", "xproj", "fc", "mproj",
                "gate", "sw1", "sw2"):
        cast_done[nme] = ldg("cw_" + nme, wb[nme], w_in[nme])
    for e8 in range(8):
        cast_done["rw1_%d" % e8] = ldg("cr1_%d" % e8, wb["rw1"][e8],
                                       w_in["rw1"][e8])
        cast_done["rw2_%d" % e8] = ldg("cr2_%d" % e8, wb["rw2"][e8],
                                       w_in["rw2"][e8])

    for eng in ("vector", "tensor", "scalar"):
        P.w(eng, "WC", WC_ALL)
        P.w(eng, "WCH", WCH_ALL)
    P.inc("vector", lambda e: e.tensor_scalar(
        scor20[:, :], bia[:, BIAS_OFF["scorer"]:BIAS_OFF["scorer"] + 16],
        SHIFT, None, ALU.add), "DVE")
    P.inc("vector", lambda e: e.memset(rden[:, :], 0.0), "DVE")
    P.barrier()

    class Ring:
        def __init__(self, lo_col, n_slots, slot_cols, sempfx):
            self.base, self.n, self.cols = lo_col, n_slots, slot_cols
            self.free_pe = [0] * n_slots
            self.sems = ["%s%d" % (sempfx, j) for j in range(n_slots)]
            self.i = 0

        def bump(self, pe):
            self.free_pe = [max(pe, x) for x in self.free_pe]

        def load(self, src, dep=None):
            s = self.i % self.n
            self.i += 1
            P.w("sync", "PE", self.free_pe[s])
            if dep:
                P.w("sync", *dep)
            dst = wbuf[:, self.base + s * self.cols:
                       self.base + (s + 1) * self.cols]
            c = ld(self.sems[s], dst, src)
            return s, dst, c

        def done(self, s, pe_cnt):
            self.free_pe[s] = pe_cnt

    ring1 = Ring(0, 4, 1024, "WS")
    ring_mp = Ring(4096, 2, 2048, "WM")

    def fm_proj(wdram, wdep, fps, src_fn, evac, wfp_base=0, tcs=range(NTC)):
        wr = wdram.rearrange("(cp p) f -> p cp f", p=128)
        for fp in fps:
            s, wsl, c = ring1.load(
                wr[:, :, (wfp_base + fp) * 128:(wfp_base + fp + 1) * 128], wdep)
            pe_c = 0
            for tc in tcs:
                bk = (fp * len(tcs) + tc) % 4
                take_bank(bk)
                P.w("tensor", *c)
                for cp in range(NCP):
                    mm(bank(bk), wsl[:, cp * 128:(cp + 1) * 128],
                       src_fn(cp, tc), cp == 0, cp == NCP - 1)
                pe_c = pe_mark()
                bank_free[bk] = evac(fp, tc, bk, pe_c)
            ring1.done(s, pe_c)

    def evac_act(dst_fn, bias_name, bias_js=None, func=AF.Identity, scale=1.0):
        def f(fp, tc, bk, pe_c):
            P.w("scalar", "PE", pe_c)
            bap = bias_ap(bias_name, bias_js[fp] if bias_js else fp) \
                if bias_name else 0.0
            v = P.inc("scalar", lambda e, d=dst_fn(fp, tc), p=bank(bk), b=bap:
                      e.activation(d, p, func, bias=b, scale=scale), "ACT")
            return ("ACT", v)
        return f

    def evac_resid(dst_fn, bias_name, res_fn):
        def f(fp, tc, bk, pe_c):
            P.w("vector", "PE", pe_c)
            v = P.inc("vector", lambda e, d=dst_fn(fp, tc), p=bank(bk),
                      b=bias_ap(bias_name, fp), r=res_fn(fp, tc):
                      e.scalar_tensor_tensor(d, p, b, r, ALU.add, ALU.add),
                      "DVE")
            return ("DVE", v)
        return f

    def fm_ln(src_fn, dst_fn, g=None, bnm=None, tcs=range(NTC), out_fn=None):
        for tc in tcs:
            mb_, vb_ = 6, 7
            take_bank(mb_)
            P.w("tensor", "DVE", P.c["DVE"])
            for cp in range(NCP):
                mm(bank(mb_), ones_bf[:, :], src_fn(cp, tc), cp == 0,
                   cp == NCP - 1)
            pe_mean = pe_mark()
            P.w("scalar", "PE", pe_mean)
            P.w("scalar", "DVE", P.c["DVE"])
            a_m = P.inc("scalar", lambda e: e.activation(
                mstat[:, :], bank(mb_), AF.Copy, bias=0.0, scale=1.0 / D), "ACT")
            bank_free[mb_] = ("ACT", a_m)
            take_bank(vb_)
            sq_free = [0, 0]
            last_xc = 0
            for cp in range(NCP):
                P.w("vector", "ACT", a_m)
                sl = cp % 2
                P.w("vector", "PE", sq_free[sl])
                d = dst_fn(cp, tc)
                last_xc = P.inc("vector", lambda e, d=d, s=src_fn(cp, tc):
                                e.tensor_tensor(d, s, mstat[:, :], ALU.subtract),
                                "DVE")
                sq = sqbuf[:, sl * 512:(sl + 1) * 512]
                v_sq = P.inc("vector", lambda e, q=sq, d=d:
                             e.tensor_tensor(q, d, d, ALU.mult), "DVE")
                P.w("tensor", "DVE", v_sq)
                mm(bank(vb_), ones_bf[:, :], sq, cp == 0, cp == NCP - 1)
                sq_free[sl] = pe_mark()
            pe_var = P.c["PE"]
            P.w("scalar", "PE", pe_var)
            P.w("scalar", "DVE", last_xc)
            a_ln = P.inc("scalar", lambda e: e.activation(
                mstat[:, :], bank(vb_), AF.Ln, bias=bias_ap("c_eps", 0),
                scale=1.0 / D), "ACT")
            bank_free[vb_] = ("ACT", a_ln)
            a_rs = P.inc("scalar", lambda e: e.activation(
                rstdb[:, :], mstat[:, :], AF.Exp, bias=0.0, scale=-0.5), "ACT")
            for cp in range(NCP):
                P.w("vector", "ACT", a_rs)
                d = dst_fn(cp, tc)
                P.inc("vector", lambda e, d=d: e.tensor_tensor(
                    d, d, rstdb[:, :], ALU.mult), "DVE")
                if g is not None:
                    P.inc("vector", lambda e, d=d, gg=bias_ap(g, cp),
                          bb=bias_ap(bnm, cp): e.tensor_scalar(
                              d, d, gg, bb, ALU.mult, ALU.add), "DVE")
                if out_fn is not None:
                    out_fn(cp, tc, d)

    INSTR = [dict(STb=(0, 1), OTb=2, DENb=3, ex=0),
             dict(STb=(4, 5), OTb=6, DENb=7, ex=1)]
    inst_free = [{"ACT": 0, "DVE": 0}, {"ACT": 0, "DVE": 0}]

    def st_ap(R, c0, c1):
        if R["STb"] == (0, 1):
            return p01[:, c0:c1]
        return p45[:, c0:c1]

    def attention(cross):
        vb_name = "cattn_b" if not cross else "xkv_b"
        vb_js = [16 + p for p in range(8)] if not cross else \
                [8 + p for p in range(8)]

        def make_steps(hp, qc, R):
            ex = R["ex"]
            exA0, exB0 = ex * 1024, ex * 1024 + 512
            if not cross:
                kbs = list(range(4 * qc + 4))
            else:
                kbs = list(range(min(15, 4 * qc + 4) + 1))
            steps = []
            for i, kb in enumerate(kbs):
                eb = expst if i % 2 == 0 else xln_sb
                off = kb * 128 - qc * 512
                lo = max(0, off) if not cross else max(0, off - 2)
                n = 512 - lo
                first, lastkb = i == 0, i == len(kbs) - 1
                qc0 = hp * 2048 + qc * 512 + lo
                kc0 = 16384 + hp * 2048 + kb * 128

                masked = off >= 0
                m0 = 0 if not cross else (0 if off >= 2 else 2)

                def ph1(kc0=kc0, qc0=qc0, lo=lo, n=n, R=R, masked=masked,
                        m0=m0):
                    if masked:
                        msl = maskbf[:, m0:m0 + n]
                        mm(st_ap(R, lo, lo + n), idbf[:, :], msl, True, False)
                        mm(st_ap(R, 512 + lo, 512 + lo + n), idbf[:, :], msl,
                           True, False)
                    st = not masked
                    mm(st_ap(R, lo, lo + n), qkt[0:64, kc0:kc0 + 128],
                       qkt[0:64, qc0:qc0 + n], st, True, (0, 0))
                    mm(st_ap(R, 512 + lo, 512 + lo + n),
                       qkt[64:128, kc0:kc0 + 128], qkt[64:128, qc0:qc0 + n],
                       st, True, (64, 0))
                    return pe_mark()

                def phe(kb=kb, ex=ex, R=R, lo=lo, n=n, eb=eb):
                    bias = bias_ap("c_shift", 0) if not cross \
                        else scor20[:, kb:kb + 1]
                    if lo == 0:
                        return P.inc("scalar", lambda e, b=bias: e.activation(
                            eb[:, ex * 1024:(ex + 1) * 1024],
                            st_ap(R, 0, 1024), AF.Exp, bias=b, scale=0.125),
                            "ACT")
                    P.inc("scalar", lambda e, b=bias: e.activation(
                        eb[:, ex * 1024 + lo:ex * 1024 + 512],
                        st_ap(R, lo, 512), AF.Exp, bias=b, scale=0.125), "ACT")
                    return P.inc("scalar", lambda e, b=bias: e.activation(
                        eb[:, ex * 1024 + 512 + lo:(ex + 1) * 1024],
                        st_ap(R, 512 + lo, 1024), AF.Exp, bias=b, scale=0.125),
                        "ACT")

                def ph2(kb=kb, lo=lo, n=n, first=first, lastkb=lastkb, hp=hp,
                        ex=ex, R=R, eb=eb):
                    va = v_sb[:, kb * 1024 + hp * 128:kb * 1024 + hp * 128 + 64]
                    vb_ = v_sb[:, kb * 1024 + hp * 128 + 64:
                               kb * 1024 + hp * 128 + 128]
                    ob = R["OTb"]
                    db = R["DENb"]
                    mm(bank(ob, 0, 64, lo, lo + n), va,
                       eb[:, exA0 + lo:exA0 + lo + n], first, lastkb, (0, 0),
                       skip=True)
                    mm(bank(ob, 64, 128, lo, lo + n), vb_,
                       eb[:, exB0 + lo:exB0 + lo + n], first, lastkb, (0, 64),
                       skip=True)
                    mm(bank(db, 0, 1, lo, lo + n), ones_bf[:, 0:1],
                       eb[:, exA0 + lo:exA0 + lo + n], first, lastkb, (0, 0),
                       skip=True)
                    mm(bank(db, 32, 33, lo, lo + n), ones_bf[:, 1:2],
                       eb[:, exB0 + lo:exB0 + lo + n], first, lastkb, (0, 32),
                       skip=True)
                    return pe_mark()

                steps.append((ph1, phe, ph2))

            def tail(hp=hp, qc=qc, R=R):
                pe_den = P.c["PE"]
                db, ob, ex = R["DENb"], R["OTb"], R["ex"]
                P.w("scalar", "PE", pe_den)
                P.inc("scalar", lambda e: e.activation(
                    mstat[0:1, :], bank(db, 0, 1), AF.Ln, bias=0.0,
                    scale=1.0), "ACT")
                P.inc("scalar", lambda e: e.activation(
                    rden[0:1, :], mstat[0:1, :], AF.Exp, bias=0.0,
                    scale=-1.0), "ACT")
                P.inc("scalar", lambda e: e.activation(
                    mstat[32:33, :], bank(db, 32, 33), AF.Ln, bias=0.0,
                    scale=1.0), "ACT")
                a_rd = P.inc("scalar", lambda e: e.activation(
                    rden[32:33, :], mstat[32:33, :], AF.Exp, bias=0.0,
                    scale=-1.0), "ACT")
                P.w("tensor", "ACT", a_rd)
                mm(bank(db), sel33[:, :], rden[:, :], True, True)
                c_bc = pe_mark()
                bch = bcsb[:, ex * 512:(ex + 1) * 512]
                dst = qkt[:, hp * 2048 + qc * 512:hp * 2048 + (qc + 1) * 512]
                P.w("vector", "PE", c_bc)
                P.inc("vector", lambda e, bc=bch, db=db: e.tensor_copy(
                    bc, bank(db)), "DVE")
                P.inc("vector", lambda e, d=dst, bc=bch:
                      e.tensor_tensor(d, bank(ob), bc, ALU.mult), "DVE")
                v_fin = P.inc("vector", lambda e, d=dst, bb=bias_ap(
                    vb_name, vb_js[hp]): e.tensor_scalar(
                        d, d, bb, None, ALU.add), "DVE")
                inst_free[ex] = {"ACT": a_rd, "DVE": v_fin}

            return steps, tail

        pend = [None, None]
        last_e = [0, 0]
        unit_dve = [inst_free[0]["DVE"], inst_free[1]["DVE"]]
        for hp0 in range(0, 8, 2):
            for qc in range(NTC):
                sA, tA = make_steps(hp0, qc, INSTR[0])
                sB, tB = make_steps(hp0 + 1, qc, INSTR[1])
                for i in range(len(sA)):
                    for ex, steps in ((0, sA), (1, sB)):
                        ph1, phe, ph2 = steps[i]
                        P.w("tensor", "ACT", last_e[ex])
                        pe_c = ph1()
                        P.w("scalar", "PE", pe_c)
                        last_e[ex] = phe()
                        prev = pend[ex]
                        pend[ex] = (ph2, last_e[ex], i == 0)
                        if prev is not None:
                            p2, ec, isf = prev
                            if isf:
                                P.w("tensor", "DVE", unit_dve[ex])
                            P.w("tensor", "ACT", ec)
                            p2()
                for ex, tl in ((0, tA), (1, tB)):
                    p2, ec, isf = pend[ex]
                    pend[ex] = None
                    if isf:
                        P.w("tensor", "DVE", unit_dve[ex])
                    P.w("tensor", "ACT", ec)
                    p2()
                    tl()
                    unit_dve[ex] = inst_free[ex]["DVE"]
        P.barrier()

    def src_xln(cp, tc):
        return xln_sb[:, cp * 2048 + tc * 512:cp * 2048 + (tc + 1) * 512]

    def src_h(cp, tc):
        return h_sb[:, cp * 2048 + tc * 512:cp * 2048 + (tc + 1) * 512]

    def src_x1(cp, tc):
        return v_sb[:, cp * 2048 + tc * 512:cp * 2048 + (tc + 1) * 512]

    def src_qt(cp, tc):
        return qkt[:, cp * 2048 + tc * 512:cp * 2048 + (tc + 1) * 512]

    def load_x_chunk(b, tc):
        src = xbf[b].rearrange("(cp p) t -> p cp t", p=128)[
            :, :, tc * 512:(tc + 1) * 512]
        P.w("sync", *cast_done["x%d" % b])
        qguard()
        return ld("WX", xchunk[:, :], src)

    def gstore(dst, src):
        P.w("gpsimd", "PE", P.c["PE"])
        P.w("gpsimd", "DVE", P.c["DVE"])
        P.w("gpsimd", "ACT", P.c["ACT"])
        return P.inc("gpsimd", lambda e, d=dst, s=src: e.dma_start(d, s),
                     "TS", 16)

    def fm_tap(nme, b, srct):
        if nme not in taps:
            return
        dst = tap_out[nme] if b is None else tap_out[nme][b]
        c = gstore(dst.rearrange("(cp p) t -> p cp t", p=128),
                   srct.rearrange("p (cp t) -> p cp t", cp=8))
        P.w("gpsimd", "TS", c)

    pool_free = {}

    def run_b(b):
        phase("b%d:ln0" % b)
        for tc in range(NTC):
            c = load_x_chunk(b, tc)
            P.w("tensor", *c)
            P.w("vector", *c)
            fm_ln(lambda cp, t: xchunk[:, cp * 512:(cp + 1) * 512],
                  lambda cp, t, tc=tc: src_h(cp, tc), tcs=[tc])
        P.barrier()
        fm_tap("h0", b, h_sb)
        if last_stage == "h0":
            return

        phase("ln1")
        fm_ln(src_h, src_xln, g="ln1_g", bnm="ln1_b")
        P.barrier()
        fm_tap("xln1", b, xln_sb)
        if last_stage == "xln1":
            return

        phase("qkv_qk")
        fm_proj(wb["cattn"], cast_done["cattn"], range(8), src_xln,
                evac_act(lambda fp, tc: qkt[:, fp * 2048 + tc * 512:
                                            fp * 2048 + (tc + 1) * 512],
                         "cattn_b"))
        fm_proj(wb["cattn"], cast_done["cattn"], range(8), src_xln,
                evac_act(lambda fp, tc: qkt[:, 16384 + fp * 2048 + tc * 512:
                                            16384 + fp * 2048 + (tc + 1) * 512],
                         "cattn_b", bias_js=[8 + j for j in range(8)]),
                wfp_base=8)

        def v_tm(wdram, wdep, f_base, vdst16):
            wv = wdram.rearrange("(cp p) f -> p cp f", p=128)
            qguard()
            P.w("sync", *wdep)
            ca = ld("WV0", wbuf[:, 0:4096], wv[:, :, f_base:f_base + 512])
            cb = ld("WV1", wbuf[:, 4096:8192],
                    wv[:, :, f_base + 512:f_base + 1024])
            kv = 0
            for tt in range(16):
                tc, to = tt // 4, (tt % 4) * 128
                for fc in range(2):
                    bk = kv % 4
                    kv += 1
                    take_bank(bk)
                    P.w("tensor", *(cb if fc else ca))
                    for cp in range(NCP):
                        mm(bank(bk),
                           vdst16(cp, tc, to),
                           wbuf[:, fc * 4096 + cp * 512:
                                fc * 4096 + (cp + 1) * 512],
                           cp == 0, cp == NCP - 1)
                    pe_c = pe_mark()
                    P.w("vector", "PE", pe_c)
                    dstv = v_sb[:, tt * 1024 + fc * 512:tt * 1024 + (fc + 1) * 512]
                    v_c = P.inc("vector", lambda e, d=dstv, p=bank(bk):
                                e.tensor_copy(d, p), "DVE")
                    bank_free[bk] = ("DVE", v_c)

        phase("qkv_v")
        v_tm(wb["cattn"], cast_done["cattn"], 2048,
             lambda cp, tc, to: src_xln(cp, tc)[:, to:to + 128])
        ring1.bump(P.c["PE"])
        ring_mp.bump(P.c["PE"])
        P.barrier()
        if "qt" in taps:
            c = gstore(tap_out["qt"][b].rearrange("(cp p) t -> p cp t", p=128),
                       qkt[:, 0:16384].rearrange("p (cp t) -> p cp t", cp=8))
            c = gstore(tap_out["kt"][b].rearrange("(cp p) t -> p cp t", p=128),
                       qkt[:, 16384:32768].rearrange("p (cp t) -> p cp t", cp=8))
            c = gstore(tap_out["v"][b].rearrange("(tt p) f -> p tt f", p=128),
                       v_sb[:, :].rearrange("p (tt f) -> p tt f", tt=16))
            P.w("gpsimd", "TS", c)
        if last_stage == "qkv":
            return

        phase("selfattn")
        attention(False)
        if last_stage == "selfattn":
            fm_tap("hself", b, qkt[0:128, 0:16384])
            return

        phase("aproj")
        fm_proj(wb["aproj"], cast_done["aproj"], range(8), src_qt,
                evac_resid(src_h, "aproj_b", src_h))
        P.barrier()
        fm_tap("hself", b, h_sb)
        if last_stage == "hself":
            return

        phase("enckv")
        if b == 0:
            for tc in range(NTC):
                src = enc_fm.rearrange("(cp p) t -> p cp t", p=128)[
                    :, :, tc * 512:(tc + 1) * 512]
                P.w("gpsimd", "PE", P.c["PE"])
                P.w("gpsimd", "DVE", P.c["DVE"])
                c = ldg("WXG", xchunk[:, :], src)
                P.w("tensor", *c)
                fm_proj(wb["xkv"], cast_done["xkv"], range(8),
                        lambda cp, t: xchunk[:, cp * 512:(cp + 1) * 512],
                        evac_act(lambda fp, t, tc=tc: qkt[
                            :, 16384 + fp * 2048 + tc * 512:
                            16384 + fp * 2048 + (tc + 1) * 512], "xkv_b"),
                        tcs=[0])
                wv2 = wb["xkv"].rearrange("(cp p) f -> p cp f", p=128)
                qguard()
                P.w("sync", *cast_done["xkv"])
                ca = ld("WV0", wbuf[:, 0:4096], wv2[:, :, 1024:1536])
                cb = ld("WV1", wbuf[:, 4096:8192], wv2[:, :, 1536:2048])
                for ti in range(4):
                    tt = tc * 4 + ti
                    for fc in range(2):
                        bk = (ti * 2 + fc) % 4
                        take_bank(bk)
                        P.w("tensor", *(cb if fc else ca))
                        for cp in range(NCP):
                            mm(bank(bk),
                               xchunk[:, cp * 512 + ti * 128:
                                      cp * 512 + ti * 128 + 128],
                               wbuf[:, fc * 4096 + cp * 512:
                                    fc * 4096 + (cp + 1) * 512],
                               cp == 0, cp == NCP - 1)
                        pe_c = pe_mark()
                        P.w("vector", "PE", pe_c)
                        dstv = v_sb[:, tt * 1024 + fc * 512:
                                    tt * 1024 + (fc + 1) * 512]
                        v_c = P.inc("vector", lambda e, d=dstv, p=bank(bk):
                                    e.tensor_copy(d, p), "DVE")
                        bank_free[bk] = ("DVE", v_c)
                ring1.bump(P.c["PE"])
                ring_mp.bump(P.c["PE"])
            P.barrier()
            qguard()
            store("KS", k2t_s.rearrange("(cp p) t -> p cp t", p=128),
                  qkt[:, 16384:32768].rearrange("p (cp t) -> p cp t", cp=8))
            c2 = store("KS", v2_s.rearrange("(tt p) f -> p tt f", p=128),
                       v_sb[:, :].rearrange("p (tt f) -> p tt f", tt=16))
            if "k2t" in taps:
                gstore(tap_out["k2t"].rearrange("(cp p) t -> p cp t", p=128),
                       qkt[:, 16384:32768].rearrange("p (cp t) -> p cp t", cp=8))
                cx = gstore(tap_out["v2"].rearrange("(tt p) f -> p tt f", p=128),
                            v_sb[:, :].rearrange("p (tt f) -> p tt f", tt=16))
                P.w("gpsimd", "TS", cx)
            enc_dep = c2
        else:
            qguard()
            P.w("sync", "KS", 32)
            ld("KL", qkt[:, 16384:32768].rearrange("p (cp t) -> p cp t", cp=8),
               k2t_s.rearrange("(cp p) t -> p cp t", p=128))
            c2 = ld("VL", v_sb[:, :].rearrange("p (tt f) -> p tt f", tt=16),
                    v2_s.rearrange("(tt p) f -> p tt f", p=128))
            P.w("tensor", "KL", P.c["KL"])
            enc_dep = c2

        phase("lnx")
        fm_ln(src_h, src_xln, g="lnx_g", bnm="lnx_b")
        P.barrier()
        phase("qattn")
        fm_proj(wb["qattn"], cast_done["qattn"], range(8), src_xln,
                evac_act(lambda fp, tc: qkt[:, fp * 2048 + tc * 512:
                                            fp * 2048 + (tc + 1) * 512],
                         "qattn_b"))
        P.w("tensor", *enc_dep)
        P.barrier()
        phase("crossattn")
        attention(True)
        if last_stage == "crossattn":
            fm_tap("hcross", b, qkt[0:128, 0:16384])
            return

        phase("xproj")
        fm_proj(wb["xproj"], cast_done["xproj"], range(8), src_qt,
                evac_resid(src_h, "xproj_b", src_h))
        P.barrier()
        fm_tap("hcross", b, h_sb)
        if last_stage == "hcross":
            return

        phase("ln2")
        fm_ln(src_h, src_xln, g="ln2_g", bnm="ln2_b")
        P.barrier()
        phase("fc_mproj")
        wfc = wb["fc"].rearrange("(cp p) f -> p cp f", p=128)
        wmp = wb["mproj"].rearrange("(hp p) f -> p hp f", p=128)
        hid_free_pe = 0
        for tc in range(NTC):
            for hp in range(32):
                s, wsl, c = ring1.load(wfc[:, :, hp * 128:(hp + 1) * 128],
                                       cast_done["fc"])
                bk = hp % 4
                take_bank(bk)
                P.w("tensor", *c)
                for cp in range(NCP):
                    mm(bank(bk), wsl[:, cp * 128:(cp + 1) * 128],
                       src_xln(cp, tc), cp == 0, cp == NCP - 1)
                pe_c = pe_mark()
                ring1.done(s, pe_c)
                P.w("scalar", "PE", max(pe_c, hid_free_pe))
                dsth = qkt[:, 16384 + hp * 512:16384 + (hp + 1) * 512]
                a_c = P.inc("scalar", lambda e, d=dsth, p=bank(bk),
                            bb=bias_ap("fc_b", hp): e.activation(
                                d, p, AF.Gelu_apprx_tanh, bias=bb, scale=1.0),
                            "ACT")
                bank_free[bk] = ("ACT", a_c)
            for fp in range(8):
                bk = 4 + fp % 2
                take_bank(bk)
                P.w("tensor", "ACT", P.c["ACT"])
                for half in range(2):
                    s, wsl, c = ring_mp.load(
                        wmp[:, half * 16:(half + 1) * 16,
                            fp * 128:(fp + 1) * 128], cast_done["mproj"])
                    P.w("tensor", *c)
                    for hh in range(16):
                        hp = half * 16 + hh
                        mm(bank(bk), wsl[:, hh * 128:(hh + 1) * 128],
                           qkt[:, 16384 + hp * 512:16384 + (hp + 1) * 512],
                           hp == 0, hp == 31)
                    pe_c = pe_mark()
                    ring_mp.done(s, pe_c)
                P.w("vector", "PE", pe_c)
                v_c = P.inc("vector", lambda e, d=src_h(fp, tc), p=bank(bk),
                            bb=bias_ap("mproj_b", fp), r=src_h(fp, tc):
                            e.scalar_tensor_tensor(d, p, bb, r, ALU.add,
                                                   ALU.add), "DVE")
                bank_free[bk] = ("DVE", v_c)
            hid_free_pe = P.c["PE"]
        P.barrier()
        fm_tap("hmlp", b, h_sb)
        if last_stage == "hmlp":
            return

        phase("outer1")
        for tc in range(NTC):
            c = load_x_chunk(b, tc)
            P.w("vector", *c)

            def out_add(cp, t, d, tc=tc):
                P.inc("vector", lambda e, d=d, o=src_x1(cp, tc),
                      a=xchunk[:, cp * 512:(cp + 1) * 512]:
                      e.tensor_tensor(o, d, a, ALU.add), "DVE")

            fm_ln(src_h, lambda cp, t, tc=tc: src_xln(cp, tc), tcs=[tc],
                  out_fn=out_add)
        P.barrier()
        fm_tap("x1", b, v_sb)
        if last_stage == "x1":
            return

        phase("x1ln")
        fm_ln(src_x1, src_xln)
        P.barrier()
        fm_tap("x1ln", b, xln_sb)

        def load_expert(e_, w_pe=(0, 0)):
            if e_ == -1:
                s1_, s2_ = wb["sw1"], wb["sw2"]
                d1, d2 = cast_done["sw1"], cast_done["sw2"]
            else:
                s1_, s2_ = wb["rw1"][e_], wb["rw2"][e_]
                d1, d2 = cast_done["rw1_%d" % e_], cast_done["rw2_%d" % e_]
            P.w("sync", "PE", w_pe[0])
            P.w("sync", *d1)
            c1 = ld("WE1", h_sb[:, :].rearrange("p (cp f) -> p cp f", cp=8),
                    s1_.rearrange("(cp p) f -> p cp f", p=128))
            P.w("sync", "PE", w_pe[1])
            P.w("sync", *d2)
            c2 = ld("WE2", wbuf[:, 0:8192].rearrange("p (hp f) -> p hp f", hp=8),
                    s2_.rearrange("(hp p) f -> p hp f", p=128))
            return c1, c2

        if MOVE_LE:
            cexp = None
        else:
            qguard()
            cexp = load_expert(-1)

        phase("gate")
        wg = wb["gate"].rearrange("(cp p) f -> p cp f", p=128)
        P.w("sync", *cast_done["gate"])
        cg = ld("WG", gwb[:, 0:64], wg[:, :, :])
        for tt in range(16):
            tc, to = tt // 4, (tt % 4) * 128
            bkg = GATE_BANKS[tt % 2]
            take_bank(bkg)
            P.w("tensor", *cg)
            for cp in range(NCP):
                mm(bank(bkg, 0, 128, 0, 8), src_xln(cp, tc)[:, to:to + 128],
                   gwb[:, cp * 8:(cp + 1) * 8], cp == 0, cp == NCP - 1)
            pe_c = pe_mark()
            P.w("vector", "PE", pe_c)
            g0 = gsm[:, 0:8]
            P.inc("vector", lambda e, p=bank(bkg, 0, 128, 0, 8): e.tensor_tensor(
                g0, p, gbb[:, :], ALU.add), "DVE")
            bank_free[bkg] = ("DVE", P.c["DVE"])
            m1 = gsm[:, 8:9]
            P.inc("vector", lambda e: e.reduce_max(m1, g0, AX.X,
                                                   negate=True), "DVE")
            ge = gsm[:, 16:24]
            P.w("scalar", "DVE", P.c["DVE"])
            a_e = P.inc("scalar", lambda e: e.activation(
                ge, g0, AF.Exp, bias=m1, scale=1.0), "ACT")
            P.w("vector", "ACT", a_e)
            sm = gsm[:, 9:10]
            P.inc("vector", lambda e: e.reduce_sum(sm, ge, AX.X), "DVE")
            rc = gsm[:, 10:11]
            P.inc("vector", lambda e: e.reciprocal(rc, sm), "DVE")
            pr = gsm[:, 24:32]
            P.inc("vector", lambda e: e.tensor_scalar(pr, ge, rc, None,
                                                      ALU.mult), "DVE")
            mp = gsm[:, 11:12]
            P.inc("vector", lambda e: e.reduce_max(mp, pr, AX.X), "DVE")
            s1 = gsm[:, 32:40]
            P.inc("vector", lambda e: e.tensor_scalar(s1, pr, mp, None,
                                                      ALU.is_ge), "DVE")
            pq = gsm[:, 40:48]
            P.inc("vector", lambda e: e.scalar_tensor_tensor(
                pq, s1, -1e30, pr, ALU.mult, ALU.add), "DVE")
            m2 = gsm[:, 12:13]
            P.inc("vector", lambda e: e.reduce_max(m2, pq, AX.X), "DVE")
            P.inc("vector", lambda e: e.tensor_scalar(s1, pr, m2, None,
                                                      ALU.is_ge), "DVE")
            v_cw = P.inc("vector", lambda e, tt=tt: e.tensor_tensor(
                cwtm[:, tt * 8:(tt + 1) * 8], pr, s1, ALU.mult), "DVE")
            if GATE_SYNC:
                P.barrier()
        P.barrier()
        if "cw" in taps:
            c = gstore(tap_out["cw"][b], cwtm[:, :])
            P.w("gpsimd", "TS", c)
        if last_stage == "gate":
            return

        if cexp is None:
            qguard()
            cexp = load_expert(-1)
        swz_free_pe = 0
        for ei, e_ in enumerate([-1, 0, 1, 2, 3, 4, 5, 6, 7][:N_EXP_DBG]):
            phase("exp%d" % e_)
            c1, c2 = cexp
            routed = e_ >= 0
            last_w1 = last_w2 = 0
            for tc in range(NTC):
                xch_dve = [0] * 8
                P.w("vector", "PE", swz_free_pe)
                for j in range(8 if "hid" not in DBG_SKIP else 0):
                    bka, bkb = (j % 2) * 2, (j % 2) * 2 + 1
                    take_bank(bka)
                    P.w("tensor", *c1)
                    for cp in range(NCP):
                        mm(bank(bka), h_sb[:, cp * 2048 + j * 128:
                                           cp * 2048 + j * 128 + 128],
                           src_xln(cp, tc), cp == 0, cp == NCP - 1)
                    take_bank(bkb)
                    for cp in range(NCP):
                        mm(bank(bkb), h_sb[:, cp * 2048 + 1024 + j * 128:
                                           cp * 2048 + 1024 + j * 128 + 128],
                           src_xln(cp, tc), cp == 0, cp == NCP - 1)
                    pe_c = pe_mark()
                    if tc == NTC - 1 and j == 7:
                        last_w1 = pe_c
                    if "silu" in DBG_SKIP:
                        bank_free[bka] = ("PE", pe_c)
                        bank_free[bkb] = ("PE", pe_c)
                        continue
                    P.w("scalar", "PE", pe_c)
                    sl = sqbuf[:, (j % 2) * 512:(j % 2 + 1) * 512]
                    bj = ("sb1", j) if not routed else ("rb1", e_ * 16 + j)
                    a_c = P.inc("scalar", lambda e, s=sl, p=bank(bka),
                                bb=bias_ap(*bj): e.activation(
                                    s, p, AF.Silu, bias=bb, scale=1.0), "ACT")
                    bank_free[bka] = ("ACT", a_c)
                    if "stt" in DBG_SKIP:
                        bank_free[bkb] = ("ACT", a_c)
                        continue
                    P.w("vector", "ACT", a_c)
                    bj2 = ("sb1", 8 + j) if not routed else \
                          ("rb1", e_ * 16 + 8 + j)
                    v_c = P.inc("vector", lambda e,
                                d=xchunk[:, j * 512:(j + 1) * 512],
                                p=bank(bkb), bb=bias_ap(*bj2), s=sl:
                                e.scalar_tensor_tensor(d, p, bb, s, ALU.add,
                                                       ALU.mult), "DVE")
                    bank_free[bkb] = ("DVE", v_c)
                    xch_dve[j] = v_c
                if routed:
                    take_bank(6)
                    for ti in range(4):
                        mmc[0] += 1
                        P.on("tensor", lambda e,
                             o=bank(6, 0, 1, ti * 128, ti * 128 + 128),
                             i_=cwtm[:, (tc * 4 + ti) * 8 + e_:
                                     (tc * 4 + ti) * 8 + e_ + 1]:
                             e.transpose(o, i_, id32[:, :]))
                    c_tr = pe_mark()
                    P.w("vector", "PE", c_tr)
                    v_cr = P.inc("vector", lambda e: e.tensor_copy(
                        crow[0:1, :], bank(6, 0, 1, 0, 512)), "DVE")
                    P.w("tensor", "DVE", v_cr)
                    mm(bank(6), ones_bf[0:1, :], crow[0:1, :], True, True)
                    c_bc = pe_mark()
                    P.w("vector", "PE", c_bc)
                    v_bc = P.inc("vector", lambda e: e.tensor_copy(
                        bcsb[:, 0:512], bank(6)), "DVE")
                    bank_free[6] = ("DVE", v_bc)
                for fp in range(8):
                    if "w2" in DBG_SKIP:
                        break
                    if "w2one" in DBG_SKIP and (fp > 0 or tc > 0):
                        break
                    bk = 4 + fp % 2
                    take_bank(bk)
                    P.w("tensor", *c2)
                    if "pedrain" in DBG_SKIP:
                        P.w("tensor", "PE", P.c["PE"])
                    for hp in range(NCP):
                        P.w("tensor", "DVE", xch_dve[hp])
                        rhs_ = xchunk[:, hp * 512:(hp + 1) * 512]
                        mm(bank(bk), wbuf[:, hp * 1024 + fp * 128:
                                          hp * 1024 + fp * 128 + 128],
                           rhs_, hp == 0, hp == NCP - 1)
                    pe_c = pe_mark()
                    if tc == NTC - 1 and fp == 7:
                        last_w2 = pe_c
                    acc_ap = accum[:, fp * 2048 + tc * 512:
                                   fp * 2048 + (tc + 1) * 512]
                    P.w("vector", "PE", pe_c)
                    if "acc" in DBG_SKIP:
                        bank_free[bk] = ("PE", pe_c)
                        continue
                    if not routed:
                        v_c = P.inc("vector", lambda e, a=acc_ap, p=bank(bk),
                                    bb=bias_ap("sb2", fp): e.tensor_scalar(
                                        a, p, bb, None, ALU.add), "DVE")
                        bank_free[bk] = ("DVE", v_c)
                    else:
                        tsl = tmp1[:, (fp % 2) * 512:(fp % 2 + 1) * 512]
                        P.w("vector", "POOL", pool_free.get(fp % 2, 0))
                        v_c = P.inc("vector", lambda e, t=tsl, p=bank(bk),
                                    bb=bias_ap("rb2", e_ * 8 + fp):
                                    e.scalar_tensor_tensor(
                                        t, p, bb, bcsb[:, 0:512], ALU.add,
                                        ALU.mult), "DVE")
                        bank_free[bk] = ("DVE", v_c)
                        P.w("gpsimd", "DVE", v_c)
                        pc = P.inc("gpsimd", lambda e, a=acc_ap, t=tsl:
                                   e.tensor_tensor(a, a, t, ALU.add), "POOL")
                        pool_free[fp % 2] = pc
                swz_free_pe = P.c["PE"]
            if ei < 8 and "pref" not in DBG_SKIP:
                cexp = load_expert(e_ + 1 if routed else 0,
                                   w_pe=(last_w1, last_w2))
        ring1.bump(P.c["PE"])
        ring_mp.bump(P.c["PE"])
        P.barrier()
        if "acc" in taps:
            c = gstore(tap_out["acc"][b].rearrange("(cp p) t -> p cp t", p=128),
                       accum[:, :].rearrange("p (cp t) -> p cp t", cp=8))
            P.w("gpsimd", "TS", c)
        if last_stage == "acc":
            return

        phase("finalln")
        st_prev = [0]
        P.w("vector", "POOL", P.c["POOL"])
        for tc in range(NTC):
            P.w("vector", "PE", P.c["PE"])
            for cp in range(NCP):
                P.inc("vector", lambda e, cp=cp, tc=tc: e.tensor_copy(
                    xchunk[:, cp * 512:(cp + 1) * 512],
                    accum[:, cp * 2048 + tc * 512:cp * 2048 + (tc + 1) * 512]),
                    "DVE")

            def out_store(cp, t, d, tc=tc):
                P.w("vector", "OS", st_prev[0])
                ost = ostage[:, 0:512]
                v_c = P.inc("vector", lambda e, o=ost, d=d,
                            a=src_x1(cp, tc): e.tensor_tensor(o, d, a, ALU.add),
                            "DVE")
                P.w("sync", "DVE", v_c)
                st_prev[0] = store(
                    "OS", y[b][cp * 128:(cp + 1) * 128,
                               tc * 512:(tc + 1) * 512], ost)[1]

            fsrc = lambda cp, t: xchunk[:, cp * 512:(cp + 1) * 512]
            DBG_LN_SRC[0] = fsrc
            fm_ln(fsrc, lambda cp, t, tc=tc: src_xln(cp, tc), tcs=[tc],
                  out_fn=out_store)
        P.barrier()

    if last_stage == "init":
        if "init" in taps:
            qguard()
            c = store("OS", tap_out["init"], bia[:, :])
    else:
        for b in range(n_b):
            run_b(b)

    for s in DMA_SEMS:
        eng = "gpsimd" if (s == "TS" or s.startswith(("cx", "cw_", "cr"))
                           or s in ("WC", "WXG")) else "sync"
        P.w(eng, s, P.c[s])

    with nc.Block() as block:
        with nc.allow_low_precision(reason="bf16 operand staging is intentional"):
            P.install(block)
    return nc, list(tap_out)


def _pack_biases(i):
    cols = np.zeros((128, BIAS_COLS), dtype=np.float32)

    def put(nme, vec):
        v = np.asarray(vec, dtype=np.float32).reshape(-1)
        n = v.size // 128
        cols[:, BIAS_OFF[nme]:BIAS_OFF[nme] + n] = v.reshape(n, 128).T

    put("ln1_g", i["L0_attn_ln1_g"]); put("ln1_b", i["L0_attn_ln1_b"])
    put("lnx_g", i["L0_attn_lnx_g"]); put("lnx_b", i["L0_attn_lnx_b"])
    put("ln2_g", i["L0_attn_ln2_g"]); put("ln2_b", i["L0_attn_ln2_b"])
    put("cattn_b", i["L0_attn_c_attn_b"])
    put("aproj_b", i["L0_attn_attn_proj_b"])
    put("qattn_b", i["L0_attn_q_attn_b"])
    put("xkv_b", i["L0_attn_x_kv_b"])
    put("xproj_b", i["L0_attn_x_proj_b"])
    put("fc_b", i["L0_attn_fc_b"])
    put("mproj_b", i["L0_attn_mlp_proj_b"])
    put("sb1", i["L0_ffn_sb1"]); put("sb2", i["L0_ffn_sb2"])
    put("rb1", i["L0_ffn_rb1"]); put("rb2", i["L0_ffn_rb2"])
    put("scorer", i["L0_attn_scorer_logits"])
    gb = np.zeros(128, dtype=np.float32)
    gb[0:8] = np.asarray(i["L0_ffn_gate_b"], dtype=np.float32)
    cols[:, BIAS_OFF["gate_b"]] = gb
    cols[:, BIAS_OFF["c_eps"]] = 1e-5
    cols[:, BIAS_OFF["c_shift"]] = -20.0
    return cols


def kernel(**inputs):
    n_b = N_B
    i = inputs
    cc = np.ascontiguousarray

    x = np.asarray(i["L0_attn_hidden_states"], dtype=np.float32)
    enc = np.asarray(i["L0_attn_encoder_hidden_states"], dtype=np.float32)[0]
    biases = _pack_biases(i)
    jj = np.arange(644)[None, :]
    pp = np.arange(128)[:, None]
    mask = np.where(jj >= pp, 0.0, -10000.0).astype(np.float32)
    ones_bf = np.ones((128, 128), dtype=np.float32)
    sel = np.zeros((33, 128), dtype=np.float32)
    sel[0, 0:64] = 1.0
    sel[32, 64:128] = 1.0
    id32 = np.eye(128, dtype=np.float32)

    base = dict(
        gbrow=np.tile(np.asarray(i["L0_ffn_gate_b"], dtype=np.float32)[None, :],
                      (128, 1)),
        enc_fm=cc(enc.T),
        biases=biases,
        maskbf=mask,
        ones_bf=ones_bf,
        sel33=sel,
        id32=id32,
        w_cattn=np.asarray(i["L0_attn_c_attn_w"], dtype=np.float32),
        w_qattn=np.asarray(i["L0_attn_q_attn_w"], dtype=np.float32),
        w_xkv=np.asarray(i["L0_attn_x_kv_w"], dtype=np.float32),
        w_aproj=np.asarray(i["L0_attn_attn_proj_w"], dtype=np.float32),
        w_xproj=np.asarray(i["L0_attn_x_proj_w"], dtype=np.float32),
        w_fc=np.asarray(i["L0_attn_fc_w"], dtype=np.float32),
        w_mproj=np.asarray(i["L0_attn_mlp_proj_w"], dtype=np.float32),
        w_gate=np.asarray(i["L0_ffn_gate_w"], dtype=np.float32),
        w_sw1=np.asarray(i["L0_ffn_sw1"], dtype=np.float32)[0],
        w_sw2=np.asarray(i["L0_ffn_sw2"], dtype=np.float32)[0],
        w_rw1=np.asarray(i["L0_ffn_rw1"], dtype=np.float32),
        w_rw2=np.asarray(i["L0_ffn_rw2"], dtype=np.float32),
    )

    nc, tap_names = build_nc(n_b, TAPS, LAST_STAGE)
    in_maps = []
    for c in range(8):
        m = dict(base)
        xs = x[c * n_b:(c + 1) * n_b]
        m["x_fm"] = cc(xs.transpose(0, 2, 1))
        in_maps.append(m)

    res = run_bass_kernel_spmd(nc, in_maps, core_ids=list(range(8)))

    global _dbg
    _dbg = {t: [np.asarray(r["dbg_" + t]) for r in res.results]
            for t in tap_names}

    outs = []
    for c in range(8):
        yc = np.asarray(res.results[c]["y"])
        outs.append(yc.transpose(0, 2, 1))
    return np.concatenate(outs, axis=0).astype(np.float32)


# version 06
# speedup vs baseline: 1.2550x; 1.0163x over previous
import numpy as np
import ml_dtypes
import concourse.bass as bass
import concourse.mybir as mybir
from concourse.bass_utils import run_bass_kernel_spmd

F32 = mybir.dt.float32
BF16 = mybir.dt.bfloat16
AF = mybir.ActivationFunctionType
ALU = mybir.AluOpType
AX = mybir.AxisListType
BF = ml_dtypes.bfloat16

ENGINES = ("sync", "gpsimd", "tensor", "vector", "scalar")
SEMS = ("LD", "LDG", "PE", "ACT", "DVE", "POOL", "ST")

T = 2048
NTC = 4
D = 1024
NCP = 8
EPS = 1e-5
SHIFT = -20.0

N_B = 4
TAPS = ()
LAST_STAGE = "all"
_dbg = {}

BIAS_LAYOUT = [
    ("ln1_g", 8), ("ln1_b", 8), ("lnx_g", 8), ("lnx_b", 8),
    ("ln2_g", 8), ("ln2_b", 8),
    ("cattn_b", 24), ("aproj_b", 8), ("qattn_b", 8), ("xkv_b", 16),
    ("xproj_b", 8), ("fc_b", 32), ("mproj_b", 8),
    ("sb1", 16), ("sb2", 8), ("rb1", 128), ("rb2", 64),
    ("scorer", 16), ("gate_b", 1), ("c_eps", 1), ("c_shift", 1),
]
BIAS_OFF = {}
_o = 0
for _n, _c in BIAS_LAYOUT:
    BIAS_OFF[_n] = _o
    _o += _c
BIAS_COLS = _o


class Prog:
    def __init__(self):
        self.q = {e: [] for e in ENGINES}
        self.c = {}
        self.sem = {}
        self.last_wait = {e: {} for e in ENGINES}
        self.real_top = {e: False for e in ENGINES}

    def on(self, eng, fn):
        self.q[eng].append(fn)
        self.real_top[eng] = True

    OWN = {"vector": "DVE", "scalar": "ACT", "gpsimd": "POOL"}

    def inc(self, eng, fn, sem, n=1):
        own = self.OWN.get(eng)
        if own is not None:
            self.w(eng, own, self.c.get(own, 0))
        self.c[sem] = self.c.get(sem, 0) + n
        s = self.sem[sem]
        self.q[eng].append(lambda e, fn=fn, s=s, n=n: fn(e).then_inc(s, n))
        self.real_top[eng] = False
        return self.c[sem]

    def mark(self, eng, sem):
        assert self.real_top[eng], "mark() must follow a real engine op"
        f = self.q[eng].pop()
        self.c[sem] = self.c.get(sem, 0) + 1
        s = self.sem[sem]
        self.q[eng].append(lambda e, f=f, s=s: f(e).then_inc(s, 1))
        self.real_top[eng] = False
        return self.c[sem]

    def w(self, eng, sem, val):
        if val <= 0 or val <= self.last_wait[eng].get(sem, 0):
            return
        self.last_wait[eng][sem] = val
        s = self.sem[sem]
        self.q[eng].append(lambda e, s=s, v=val: e.wait_ge(s, v))
        self.real_top[eng] = False

    def barrier(self):
        engs = ("tensor", "vector", "scalar", "gpsimd")
        m = {"tensor": "PE", "vector": "DVE", "scalar": "ACT", "gpsimd": "POOL"}
        for e in engs:
            if self.real_top[e]:
                self.mark(e, m[e])
        snap = dict(self.c)
        for e in engs:
            for o in engs:
                if o != e:
                    self.w(e, m[o], snap.get(m[o], 0))

    def install(self, block):
        for e in ENGINES:
            ops = self.q[e]
            getattr(block, e)(lambda eng, ops=ops: [f(eng) for f in ops])


GATE_SYNC = False
GATE_BANKS = (0, 1)
DBG_LN = False
DBG_LN_TC = 0
DBG_LN_SRC = [None]
N_EXP_DBG = 9
MOVE_LE = False
DBG_SKIP = frozenset()
PHASE_LOG = []


def build_nc(n_b, taps, last_stage):
    nc = bass.Bass("TRN2", target_bir_lowering=False)
    P = Prog()
    taps = set(taps)

    x_fm = nc.dram_tensor("x_fm", [n_b, D, T], F32, kind="ExternalInput").ap()
    enc_fm = nc.dram_tensor("enc_fm", [D, T], F32, kind="ExternalInput").ap()
    biases_i = nc.dram_tensor("biases", [128, BIAS_COLS], F32,
                              kind="ExternalInput").ap()
    mask_i = nc.dram_tensor("maskbf", [128, 644], F32, kind="ExternalInput").ap()
    ones_i = nc.dram_tensor("ones_bf", [128, 128], F32, kind="ExternalInput").ap()
    sel_i = nc.dram_tensor("sel33", [33, 128], F32, kind="ExternalInput").ap()
    id_i = nc.dram_tensor("id32", [128, 128], F32, kind="ExternalInput").ap()
    gbrow_i = nc.dram_tensor("gbrow", [128, 8], F32, kind="ExternalInput").ap()
    wshape = dict(cattn=[D, 3072], qattn=[D, D], xkv=[D, 2048], aproj=[D, D],
                  xproj=[D, D], fc=[D, 4096], mproj=[4096, D], gate=[D, 8],
                  sw1=[D, 2048], sw2=[D, D], rw1=[8, D, 2048], rw2=[8, D, D])
    w_in, wb = {}, {}
    for nme, shp in wshape.items():
        w_in[nme] = nc.dram_tensor("w_" + nme, shp, F32, kind="ExternalInput").ap()
        wb[nme] = nc.dram_tensor("wb_" + nme, shp, BF16).ap()
    y = nc.dram_tensor("y", [n_b, D, T], F32, kind="ExternalOutput").ap()
    xbf = nc.dram_tensor("xbf", [n_b, D, T], BF16).ap()
    k2t_s = nc.dram_tensor("k2t_s", [D, T], BF16).ap()
    v2_s = nc.dram_tensor("v2_s", [T, D], BF16).ap()

    tap_out = {}
    for nme, shp in [("init", [128, BIAS_COLS]),
                     ("h0", [n_b, D, T]), ("xln1", [n_b, D, T]),
                     ("qt", [n_b, D, T]), ("kt", [n_b, D, T]),
                     ("v", [n_b, T, D]), ("hself", [n_b, D, T]),
                     ("k2t", [D, T]), ("v2", [T, D]),
                     ("hcross", [n_b, D, T]),
                     ("hmlp", [n_b, D, T]), ("x1", [n_b, D, T]),
                     ("x1ln", [n_b, D, T]), ("cw", [n_b, 128, 128]),
                     ("acc", [n_b, D, T])]:
        if nme in taps:
            tap_out[nme] = nc.dram_tensor("dbg_" + nme, shp, F32,
                                          kind="ExternalOutput").ap()

    A = nc.alloc_sbuf_tensor
    ones_bf = A("s_ones_bf", [128, 128], BF16)
    sel33 = A("s_sel33", [33, 128], BF16)
    maskbf = A("s_maskbf", [128, 644], BF16)
    idbf = A("s_idbf", [128, 128], BF16)
    id32 = A("s_id32", [128, 128], F32)
    bia = A("bia", [128, BIAS_COLS], F32)
    scor20 = A("scor20", [128, 16], F32)
    mstat = A("mstat", [128, 512], F32)
    rstdb = A("rstdb", [128, 512], BF16)
    sqbuf = A("sqbuf", [128, 1024], BF16)
    rden = A("rden", [33, 512], BF16)
    bcsb = A("bcsb", [128, 1024], BF16)
    tmp1 = A("tmp1", [128, 1024], BF16)
    ostage = A("ostage", [128, 512], F32)
    expst = A("expst", [128, 2048], BF16)
    xchunk = A("xchunk", [128, 4096], BF16)
    wbuf = A("wbuf", [128, 8192], BF16)
    h_sb = A("h_sb", [128, 16384], BF16)
    xln_sb = A("xln_sb", [128, 16384], BF16)
    qkt = A("qkt", [128, 32768], BF16)
    v_sb = A("v_sb", [128, 16384], BF16)
    gsm = A("gsm", [128, 48], F32)
    gwb = A("gwb", [128, 64], BF16)
    gbb = A("gbb", [128, 8], F32)
    cwtm = A("cwtm", [128, 128], F32)
    crow = A("crow", [1, 512], BF16)
    accum = qkt.bitcast(F32)

    p01 = nc.alloc_psum_tensor("p01", [128, 1024], F32)
    p2 = nc.alloc_psum_tensor("p2", [128, 512], F32)
    p3 = nc.alloc_psum_tensor("p3", [128, 512], F32)
    p45 = nc.alloc_psum_tensor("p45", [128, 1024], F32)
    p6 = nc.alloc_psum_tensor("p6", [128, 512], F32)
    p7 = nc.alloc_psum_tensor("p7", [128, 512], F32)

    def bank(i, r0=0, r1=128, c0=0, c1=512):
        if i == 0:
            return p01[r0:r1, c0:c1]
        if i == 1:
            return p01[r0:r1, 512 + c0:512 + c1]
        if i == 4:
            return p45[r0:r1, c0:c1]
        if i == 5:
            return p45[r0:r1, 512 + c0:512 + c1]
        return {2: p2, 3: p3, 6: p6, 7: p7}[i][r0:r1, c0:c1]

    DMA_SEMS = (["WC", "WCH", "WX", "WXG", "WG", "WE1", "WE2", "WV0", "WV1", "KL", "VL",
                 "KS", "OS", "TS"]
                + ["WS%d" % j for j in range(4)] + ["WM%d" % j for j in range(4)]
                + ["cx%d" % b for b in range(n_b)]
                + ["cw_" + nm for nm in ("cattn", "qattn", "xkv", "aproj",
                                         "xproj", "fc", "mproj", "gate",
                                         "sw1", "sw2")]
                + ["cr1_%d" % e for e in range(8)]
                + ["cr2_%d" % e for e in range(8)])
    for s in list(SEMS) + DMA_SEMS:
        P.sem[s] = nc.semaphore(s).__enter__()
        P.c[s] = 0

    def ld(sem, dst, src):
        return (sem, P.inc("sync", lambda e, d=dst, s=src: e.dma_start(d, s),
                           sem, 16))

    def ldg(sem, dst, src):
        return (sem, P.inc("gpsimd", lambda e, d=dst, s=src: e.dma_start(d, s),
                           sem, 16))

    def store(sem, dst, src):
        return (sem, P.inc("sync", lambda e, d=dst, s=src: e.dma_start(d, s),
                           sem, 16))

    mmc = [0]

    def phase(nm):
        PHASE_LOG.append((nm, mmc[0]))

    def mm(out, lhsT, rhs, start, stop, tp=None, skip=False):
        mmc[0] += 1
        P.on("tensor", lambda e, o=out, l=lhsT, r=rhs, st=start, sp=stop, tp=tp,
             sk=skip: e.matmul(o, l, r, start=st, stop=sp, tile_position=tp,
                               skip_group_check=sk))

    def pe_mark():
        return P.mark("tensor", "PE")

    def tr(out, in_, ident):
        mmc[0] += 1
        P.on("tensor", lambda e, o=out, i=in_, d=ident: e.transpose(o, i, d))
        return pe_mark()

    bank_free = {}

    def take_bank(i):
        if i in bank_free:
            s, v = bank_free[i]
            P.w("tensor", s, v)

    def bias_ap(nme, j, n=1):
        off = BIAS_OFF[nme] + j
        return bia[:, off:off + n]

    def qguard():
        P.w("sync", "PE", P.c["PE"])
        P.w("sync", "DVE", P.c["DVE"])
        P.w("sync", "ACT", P.c["ACT"])
        P.w("sync", "POOL", P.c["POOL"])

    ldg("WC", ones_bf[:, :], ones_i[:, :])
    ldg("WC", sel33[:, :], sel_i[:, :])
    ldg("WC", maskbf[:, :], mask_i[:, :])
    ldg("WC", idbf[:, :], id_i[:, :])
    ld("WCH", id32[:, :], id_i[:, :])
    ld("WCH", gbb[:, :], gbrow_i[:, :])
    ld("WCH", bia[:, :], biases_i[:, :])
    WC_ALL = P.c["WC"]
    WCH_ALL = P.c["WCH"]

    cast_done = {}
    for b in range(n_b):
        cast_done["x%d" % b] = ldg("cx%d" % b, xbf[b], x_fm[b])
    for nme in ("cattn", "qattn", "xkv", "aproj", "xproj", "fc", "mproj",
                "gate", "sw1", "sw2"):
        cast_done[nme] = ldg("cw_" + nme, wb[nme], w_in[nme])
    for e8 in range(8):
        cast_done["rw1_%d" % e8] = ldg("cr1_%d" % e8, wb["rw1"][e8],
                                       w_in["rw1"][e8])
        cast_done["rw2_%d" % e8] = ldg("cr2_%d" % e8, wb["rw2"][e8],
                                       w_in["rw2"][e8])

    for eng in ("vector", "tensor", "scalar"):
        P.w(eng, "WC", WC_ALL)
        P.w(eng, "WCH", WCH_ALL)
    P.inc("vector", lambda e: e.tensor_scalar(
        scor20[:, :], bia[:, BIAS_OFF["scorer"]:BIAS_OFF["scorer"] + 16],
        SHIFT, None, ALU.add), "DVE")
    P.inc("vector", lambda e: e.memset(rden[:, :], 0.0), "DVE")
    P.barrier()

    class Ring:
        def __init__(self, lo_col, n_slots, slot_cols, sempfx):
            self.base, self.n, self.cols = lo_col, n_slots, slot_cols
            self.free_pe = [0] * n_slots
            self.sems = ["%s%d" % (sempfx, j) for j in range(n_slots)]
            self.i = 0

        def bump(self, pe):
            self.free_pe = [max(pe, x) for x in self.free_pe]

        def load(self, src, dep=None):
            s = self.i % self.n
            self.i += 1
            P.w("sync", "PE", self.free_pe[s])
            if dep:
                P.w("sync", *dep)
            dst = wbuf[:, self.base + s * self.cols:
                       self.base + (s + 1) * self.cols]
            c = ld(self.sems[s], dst, src)
            return s, dst, c

        def done(self, s, pe_cnt):
            self.free_pe[s] = pe_cnt

    ring1 = Ring(0, 4, 1024, "WS")
    ring_mp = Ring(4096, 4, 1024, "WM")

    def fm_proj(wdram, wdep, fps, src_fn, evac, wfp_base=0, tcs=range(NTC)):
        wr = wdram.rearrange("(cp p) f -> p cp f", p=128)
        for fp in fps:
            s, wsl, c = ring1.load(
                wr[:, :, (wfp_base + fp) * 128:(wfp_base + fp + 1) * 128], wdep)
            pe_c = 0
            for tc in tcs:
                bk = (fp * len(tcs) + tc) % 4
                take_bank(bk)
                P.w("tensor", *c)
                for cp in range(NCP):
                    mm(bank(bk), wsl[:, cp * 128:(cp + 1) * 128],
                       src_fn(cp, tc), cp == 0, cp == NCP - 1)
                pe_c = pe_mark()
                bank_free[bk] = evac(fp, tc, bk, pe_c)
            ring1.done(s, pe_c)

    def evac_act(dst_fn, bias_name, bias_js=None, func=AF.Identity, scale=1.0):
        def f(fp, tc, bk, pe_c):
            P.w("scalar", "PE", pe_c)
            bap = bias_ap(bias_name, bias_js[fp] if bias_js else fp) \
                if bias_name else 0.0
            v = P.inc("scalar", lambda e, d=dst_fn(fp, tc), p=bank(bk), b=bap:
                      e.activation(d, p, func, bias=b, scale=scale), "ACT")
            return ("ACT", v)
        return f

    def evac_resid(dst_fn, bias_name, res_fn):
        def f(fp, tc, bk, pe_c):
            P.w("vector", "PE", pe_c)
            v = P.inc("vector", lambda e, d=dst_fn(fp, tc), p=bank(bk),
                      b=bias_ap(bias_name, fp), r=res_fn(fp, tc):
                      e.scalar_tensor_tensor(d, p, b, r, ALU.add, ALU.add),
                      "DVE")
            return ("DVE", v)
        return f

    def fm_ln(src_fn, dst_fn, g=None, bnm=None, tcs=range(NTC), out_fn=None):
        for tc in tcs:
            mb_, vb_ = 6, 7
            take_bank(mb_)
            P.w("tensor", "DVE", P.c["DVE"])
            for cp in range(NCP):
                mm(bank(mb_), ones_bf[:, :], src_fn(cp, tc), cp == 0,
                   cp == NCP - 1)
            pe_mean = pe_mark()
            P.w("scalar", "PE", pe_mean)
            P.w("scalar", "DVE", P.c["DVE"])
            a_m = P.inc("scalar", lambda e: e.activation(
                mstat[:, :], bank(mb_), AF.Copy, bias=0.0, scale=1.0 / D), "ACT")
            bank_free[mb_] = ("ACT", a_m)
            take_bank(vb_)
            sq_free = [0, 0]
            last_xc = 0
            for cp in range(NCP):
                P.w("vector", "ACT", a_m)
                sl = cp % 2
                P.w("vector", "PE", sq_free[sl])
                d = dst_fn(cp, tc)
                last_xc = P.inc("vector", lambda e, d=d, s=src_fn(cp, tc):
                                e.tensor_tensor(d, s, mstat[:, :], ALU.subtract),
                                "DVE")
                sq = sqbuf[:, sl * 512:(sl + 1) * 512]
                v_sq = P.inc("vector", lambda e, q=sq, d=d:
                             e.tensor_tensor(q, d, d, ALU.mult), "DVE")
                P.w("tensor", "DVE", v_sq)
                mm(bank(vb_), ones_bf[:, :], sq, cp == 0, cp == NCP - 1)
                sq_free[sl] = pe_mark()
            pe_var = P.c["PE"]
            P.w("scalar", "PE", pe_var)
            P.w("scalar", "DVE", last_xc)
            a_ln = P.inc("scalar", lambda e: e.activation(
                mstat[:, :], bank(vb_), AF.Ln, bias=bias_ap("c_eps", 0),
                scale=1.0 / D), "ACT")
            bank_free[vb_] = ("ACT", a_ln)
            a_rs = P.inc("scalar", lambda e: e.activation(
                rstdb[:, :], mstat[:, :], AF.Exp, bias=0.0, scale=-0.5), "ACT")
            for cp in range(NCP):
                P.w("vector", "ACT", a_rs)
                d = dst_fn(cp, tc)
                P.inc("vector", lambda e, d=d: e.tensor_tensor(
                    d, d, rstdb[:, :], ALU.mult), "DVE")
                if g is not None:
                    P.inc("vector", lambda e, d=d, gg=bias_ap(g, cp),
                          bb=bias_ap(bnm, cp): e.tensor_scalar(
                              d, d, gg, bb, ALU.mult, ALU.add), "DVE")
                if out_fn is not None:
                    out_fn(cp, tc, d)

    INSTR = [dict(STb=(0, 1), OTb=2, DENb=3, ex=0),
             dict(STb=(4, 5), OTb=6, DENb=7, ex=1)]
    inst_free = [{"ACT": 0, "DVE": 0}, {"ACT": 0, "DVE": 0}]

    def st_ap(R, c0, c1):
        if R["STb"] == (0, 1):
            return p01[:, c0:c1]
        return p45[:, c0:c1]

    def attention(cross):
        vb_name = "cattn_b" if not cross else "xkv_b"
        vb_js = [16 + p for p in range(8)] if not cross else \
                [8 + p for p in range(8)]

        def make_steps(hp, qc, R):
            ex = R["ex"]
            exA0, exB0 = ex * 1024, ex * 1024 + 512
            if not cross:
                kbs = list(range(4 * qc + 4))
            else:
                kbs = list(range(min(15, 4 * qc + 4) + 1))
            steps = []
            for i, kb in enumerate(kbs):
                eb = expst if i % 2 == 0 else xln_sb
                off = kb * 128 - qc * 512
                lo = max(0, off) if not cross else max(0, off - 2)
                n = 512 - lo
                first, lastkb = i == 0, i == len(kbs) - 1
                qc0 = hp * 2048 + qc * 512 + lo
                kc0 = 16384 + hp * 2048 + kb * 128

                masked = off >= 0
                m0 = 0 if not cross else (0 if off >= 2 else 2)

                def ph1(kc0=kc0, qc0=qc0, lo=lo, n=n, R=R, masked=masked,
                        m0=m0):
                    if masked:
                        msl = maskbf[:, m0:m0 + n]
                        mm(st_ap(R, lo, lo + n), idbf[:, :], msl, True, False)
                        mm(st_ap(R, 512 + lo, 512 + lo + n), idbf[:, :], msl,
                           True, False)
                    st = not masked
                    mm(st_ap(R, lo, lo + n), qkt[0:64, kc0:kc0 + 128],
                       qkt[0:64, qc0:qc0 + n], st, True, (0, 0))
                    mm(st_ap(R, 512 + lo, 512 + lo + n),
                       qkt[64:128, kc0:kc0 + 128], qkt[64:128, qc0:qc0 + n],
                       st, True, (64, 0))
                    return pe_mark()

                def phe(kb=kb, ex=ex, R=R, lo=lo, n=n, eb=eb):
                    bias = bias_ap("c_shift", 0) if not cross \
                        else scor20[:, kb:kb + 1]
                    if lo == 0:
                        return P.inc("scalar", lambda e, b=bias: e.activation(
                            eb[:, ex * 1024:(ex + 1) * 1024],
                            st_ap(R, 0, 1024), AF.Exp, bias=b, scale=0.125),
                            "ACT")
                    P.inc("scalar", lambda e, b=bias: e.activation(
                        eb[:, ex * 1024 + lo:ex * 1024 + 512],
                        st_ap(R, lo, 512), AF.Exp, bias=b, scale=0.125), "ACT")
                    return P.inc("scalar", lambda e, b=bias: e.activation(
                        eb[:, ex * 1024 + 512 + lo:(ex + 1) * 1024],
                        st_ap(R, 512 + lo, 1024), AF.Exp, bias=b, scale=0.125),
                        "ACT")

                def ph2(kb=kb, lo=lo, n=n, first=first, lastkb=lastkb, hp=hp,
                        ex=ex, R=R, eb=eb):
                    va = v_sb[:, kb * 1024 + hp * 128:kb * 1024 + hp * 128 + 64]
                    vb_ = v_sb[:, kb * 1024 + hp * 128 + 64:
                               kb * 1024 + hp * 128 + 128]
                    ob = R["OTb"]
                    db = R["DENb"]
                    mm(bank(ob, 0, 64, lo, lo + n), va,
                       eb[:, exA0 + lo:exA0 + lo + n], first, lastkb, (0, 0),
                       skip=True)
                    mm(bank(ob, 64, 128, lo, lo + n), vb_,
                       eb[:, exB0 + lo:exB0 + lo + n], first, lastkb, (0, 64),
                       skip=True)
                    mm(bank(db, 0, 1, lo, lo + n), ones_bf[:, 0:1],
                       eb[:, exA0 + lo:exA0 + lo + n], first, lastkb, (0, 0),
                       skip=True)
                    mm(bank(db, 32, 33, lo, lo + n), ones_bf[:, 1:2],
                       eb[:, exB0 + lo:exB0 + lo + n], first, lastkb, (0, 32),
                       skip=True)
                    return pe_mark()

                steps.append((ph1, phe, ph2))

            def tail(hp=hp, qc=qc, R=R):
                pe_den = P.c["PE"]
                db, ob, ex = R["DENb"], R["OTb"], R["ex"]
                P.w("scalar", "PE", pe_den)
                P.inc("scalar", lambda e: e.activation(
                    mstat[0:1, :], bank(db, 0, 1), AF.Ln, bias=0.0,
                    scale=1.0), "ACT")
                P.inc("scalar", lambda e: e.activation(
                    rden[0:1, :], mstat[0:1, :], AF.Exp, bias=0.0,
                    scale=-1.0), "ACT")
                P.inc("scalar", lambda e: e.activation(
                    mstat[32:33, :], bank(db, 32, 33), AF.Ln, bias=0.0,
                    scale=1.0), "ACT")
                a_rd = P.inc("scalar", lambda e: e.activation(
                    rden[32:33, :], mstat[32:33, :], AF.Exp, bias=0.0,
                    scale=-1.0), "ACT")
                P.w("tensor", "ACT", a_rd)
                mm(bank(db), sel33[:, :], rden[:, :], True, True)
                c_bc = pe_mark()
                bch = bcsb[:, ex * 512:(ex + 1) * 512]
                dst = qkt[:, hp * 2048 + qc * 512:hp * 2048 + (qc + 1) * 512]
                P.w("vector", "PE", c_bc)
                P.inc("vector", lambda e, bc=bch, db=db: e.tensor_copy(
                    bc, bank(db)), "DVE")
                P.inc("vector", lambda e, d=dst, bc=bch:
                      e.tensor_tensor(d, bank(ob), bc, ALU.mult), "DVE")
                v_fin = P.inc("vector", lambda e, d=dst, bb=bias_ap(
                    vb_name, vb_js[hp]): e.tensor_scalar(
                        d, d, bb, None, ALU.add), "DVE")
                inst_free[ex] = {"ACT": a_rd, "DVE": v_fin}

            return steps, tail

        pend = [None, None]
        last_e = [0, 0]
        unit_dve = [inst_free[0]["DVE"], inst_free[1]["DVE"]]
        for hp0 in range(0, 8, 2):
            for qc in range(NTC):
                sA, tA = make_steps(hp0, qc, INSTR[0])
                sB, tB = make_steps(hp0 + 1, qc, INSTR[1])
                for i in range(len(sA)):
                    for ex, steps in ((0, sA), (1, sB)):
                        ph1, phe, ph2 = steps[i]
                        P.w("tensor", "ACT", last_e[ex])
                        pe_c = ph1()
                        P.w("scalar", "PE", pe_c)
                        last_e[ex] = phe()
                        prev = pend[ex]
                        pend[ex] = (ph2, last_e[ex], i == 0)
                        if prev is not None:
                            p2, ec, isf = prev
                            if isf:
                                P.w("tensor", "DVE", unit_dve[ex])
                            P.w("tensor", "ACT", ec)
                            p2()
                for ex, tl in ((0, tA), (1, tB)):
                    p2, ec, isf = pend[ex]
                    pend[ex] = None
                    if isf:
                        P.w("tensor", "DVE", unit_dve[ex])
                    P.w("tensor", "ACT", ec)
                    p2()
                    tl()
                    unit_dve[ex] = inst_free[ex]["DVE"]
        P.barrier()

    def src_xln(cp, tc):
        return xln_sb[:, cp * 2048 + tc * 512:cp * 2048 + (tc + 1) * 512]

    def src_h(cp, tc):
        return h_sb[:, cp * 2048 + tc * 512:cp * 2048 + (tc + 1) * 512]

    def src_x1(cp, tc):
        return v_sb[:, cp * 2048 + tc * 512:cp * 2048 + (tc + 1) * 512]

    def src_qt(cp, tc):
        return qkt[:, cp * 2048 + tc * 512:cp * 2048 + (tc + 1) * 512]

    def load_x_chunk(b, tc):
        src = xbf[b].rearrange("(cp p) t -> p cp t", p=128)[
            :, :, tc * 512:(tc + 1) * 512]
        P.w("sync", *cast_done["x%d" % b])
        qguard()
        return ld("WX", xchunk[:, :], src)

    def gstore(dst, src):
        P.w("gpsimd", "PE", P.c["PE"])
        P.w("gpsimd", "DVE", P.c["DVE"])
        P.w("gpsimd", "ACT", P.c["ACT"])
        return P.inc("gpsimd", lambda e, d=dst, s=src: e.dma_start(d, s),
                     "TS", 16)

    def fm_tap(nme, b, srct):
        if nme not in taps:
            return
        dst = tap_out[nme] if b is None else tap_out[nme][b]
        c = gstore(dst.rearrange("(cp p) t -> p cp t", p=128),
                   srct.rearrange("p (cp t) -> p cp t", cp=8))
        P.w("gpsimd", "TS", c)

    pool_free = {}

    def run_b(b):
        phase("b%d:ln0" % b)
        for tc in range(NTC):
            c = load_x_chunk(b, tc)
            P.w("tensor", *c)
            P.w("vector", *c)
            fm_ln(lambda cp, t: xchunk[:, cp * 512:(cp + 1) * 512],
                  lambda cp, t, tc=tc: src_h(cp, tc), tcs=[tc])
        P.barrier()
        fm_tap("h0", b, h_sb)
        if last_stage == "h0":
            return

        phase("ln1")
        fm_ln(src_h, src_xln, g="ln1_g", bnm="ln1_b")
        P.barrier()
        fm_tap("xln1", b, xln_sb)
        if last_stage == "xln1":
            return

        phase("qkv_qk")
        fm_proj(wb["cattn"], cast_done["cattn"], range(8), src_xln,
                evac_act(lambda fp, tc: qkt[:, fp * 2048 + tc * 512:
                                            fp * 2048 + (tc + 1) * 512],
                         "cattn_b"))
        fm_proj(wb["cattn"], cast_done["cattn"], range(8), src_xln,
                evac_act(lambda fp, tc: qkt[:, 16384 + fp * 2048 + tc * 512:
                                            16384 + fp * 2048 + (tc + 1) * 512],
                         "cattn_b", bias_js=[8 + j for j in range(8)]),
                wfp_base=8)

        def v_tm(wdram, wdep, f_base, vdst16):
            wv = wdram.rearrange("(cp p) f -> p cp f", p=128)
            qguard()
            P.w("sync", *wdep)
            ca = ld("WV0", wbuf[:, 0:4096], wv[:, :, f_base:f_base + 512])
            cb = ld("WV1", wbuf[:, 4096:8192],
                    wv[:, :, f_base + 512:f_base + 1024])
            kv = 0
            for tt in range(16):
                tc, to = tt // 4, (tt % 4) * 128
                for fc in range(2):
                    bk = kv % 4
                    kv += 1
                    take_bank(bk)
                    P.w("tensor", *(cb if fc else ca))
                    for cp in range(NCP):
                        mm(bank(bk),
                           vdst16(cp, tc, to),
                           wbuf[:, fc * 4096 + cp * 512:
                                fc * 4096 + (cp + 1) * 512],
                           cp == 0, cp == NCP - 1)
                    pe_c = pe_mark()
                    P.w("vector", "PE", pe_c)
                    dstv = v_sb[:, tt * 1024 + fc * 512:tt * 1024 + (fc + 1) * 512]
                    v_c = P.inc("vector", lambda e, d=dstv, p=bank(bk):
                                e.tensor_copy(d, p), "DVE")
                    bank_free[bk] = ("DVE", v_c)

        phase("qkv_v")
        v_tm(wb["cattn"], cast_done["cattn"], 2048,
             lambda cp, tc, to: src_xln(cp, tc)[:, to:to + 128])
        ring1.bump(P.c["PE"])
        ring_mp.bump(P.c["PE"])
        P.barrier()
        if "qt" in taps:
            c = gstore(tap_out["qt"][b].rearrange("(cp p) t -> p cp t", p=128),
                       qkt[:, 0:16384].rearrange("p (cp t) -> p cp t", cp=8))
            c = gstore(tap_out["kt"][b].rearrange("(cp p) t -> p cp t", p=128),
                       qkt[:, 16384:32768].rearrange("p (cp t) -> p cp t", cp=8))
            c = gstore(tap_out["v"][b].rearrange("(tt p) f -> p tt f", p=128),
                       v_sb[:, :].rearrange("p (tt f) -> p tt f", tt=16))
            P.w("gpsimd", "TS", c)
        if last_stage == "qkv":
            return

        phase("selfattn")
        attention(False)
        if last_stage == "selfattn":
            fm_tap("hself", b, qkt[0:128, 0:16384])
            return

        phase("aproj")
        fm_proj(wb["aproj"], cast_done["aproj"], range(8), src_qt,
                evac_resid(src_h, "aproj_b", src_h))
        P.barrier()
        fm_tap("hself", b, h_sb)
        if last_stage == "hself":
            return

        phase("enckv")
        if b == 0:
            for tc in range(NTC):
                src = enc_fm.rearrange("(cp p) t -> p cp t", p=128)[
                    :, :, tc * 512:(tc + 1) * 512]
                P.w("gpsimd", "PE", P.c["PE"])
                P.w("gpsimd", "DVE", P.c["DVE"])
                c = ldg("WXG", xchunk[:, :], src)
                P.w("tensor", *c)
                fm_proj(wb["xkv"], cast_done["xkv"], range(8),
                        lambda cp, t: xchunk[:, cp * 512:(cp + 1) * 512],
                        evac_act(lambda fp, t, tc=tc: qkt[
                            :, 16384 + fp * 2048 + tc * 512:
                            16384 + fp * 2048 + (tc + 1) * 512], "xkv_b"),
                        tcs=[0])
                wv2 = wb["xkv"].rearrange("(cp p) f -> p cp f", p=128)
                qguard()
                P.w("sync", *cast_done["xkv"])
                ca = ld("WV0", wbuf[:, 0:4096], wv2[:, :, 1024:1536])
                cb = ld("WV1", wbuf[:, 4096:8192], wv2[:, :, 1536:2048])
                for ti in range(4):
                    tt = tc * 4 + ti
                    for fc in range(2):
                        bk = (ti * 2 + fc) % 4
                        take_bank(bk)
                        P.w("tensor", *(cb if fc else ca))
                        for cp in range(NCP):
                            mm(bank(bk),
                               xchunk[:, cp * 512 + ti * 128:
                                      cp * 512 + ti * 128 + 128],
                               wbuf[:, fc * 4096 + cp * 512:
                                    fc * 4096 + (cp + 1) * 512],
                               cp == 0, cp == NCP - 1)
                        pe_c = pe_mark()
                        P.w("vector", "PE", pe_c)
                        dstv = v_sb[:, tt * 1024 + fc * 512:
                                    tt * 1024 + (fc + 1) * 512]
                        v_c = P.inc("vector", lambda e, d=dstv, p=bank(bk):
                                    e.tensor_copy(d, p), "DVE")
                        bank_free[bk] = ("DVE", v_c)
                ring1.bump(P.c["PE"])
                ring_mp.bump(P.c["PE"])
            P.barrier()
            qguard()
            store("KS", k2t_s.rearrange("(cp p) t -> p cp t", p=128),
                  qkt[:, 16384:32768].rearrange("p (cp t) -> p cp t", cp=8))
            c2 = store("KS", v2_s.rearrange("(tt p) f -> p tt f", p=128),
                       v_sb[:, :].rearrange("p (tt f) -> p tt f", tt=16))
            if "k2t" in taps:
                gstore(tap_out["k2t"].rearrange("(cp p) t -> p cp t", p=128),
                       qkt[:, 16384:32768].rearrange("p (cp t) -> p cp t", cp=8))
                cx = gstore(tap_out["v2"].rearrange("(tt p) f -> p tt f", p=128),
                            v_sb[:, :].rearrange("p (tt f) -> p tt f", tt=16))
                P.w("gpsimd", "TS", cx)
            enc_dep = c2
        else:
            qguard()
            P.w("sync", "KS", 32)
            ld("KL", qkt[:, 16384:32768].rearrange("p (cp t) -> p cp t", cp=8),
               k2t_s.rearrange("(cp p) t -> p cp t", p=128))
            c2 = ld("VL", v_sb[:, :].rearrange("p (tt f) -> p tt f", tt=16),
                    v2_s.rearrange("(tt p) f -> p tt f", p=128))
            P.w("tensor", "KL", P.c["KL"])
            enc_dep = c2

        phase("lnx")
        fm_ln(src_h, src_xln, g="lnx_g", bnm="lnx_b")
        P.barrier()
        phase("qattn")
        fm_proj(wb["qattn"], cast_done["qattn"], range(8), src_xln,
                evac_act(lambda fp, tc: qkt[:, fp * 2048 + tc * 512:
                                            fp * 2048 + (tc + 1) * 512],
                         "qattn_b"))
        P.w("tensor", *enc_dep)
        P.barrier()
        phase("crossattn")
        attention(True)
        if last_stage == "crossattn":
            fm_tap("hcross", b, qkt[0:128, 0:16384])
            return

        phase("xproj")
        fm_proj(wb["xproj"], cast_done["xproj"], range(8), src_qt,
                evac_resid(src_h, "xproj_b", src_h))
        P.barrier()
        fm_tap("hcross", b, h_sb)
        if last_stage == "hcross":
            return

        phase("ln2")
        fm_ln(src_h, src_xln, g="ln2_g", bnm="ln2_b")
        P.barrier()
        phase("fc_mproj")
        wfc = wb["fc"].rearrange("(cp p) f -> p cp f", p=128)
        wmp = wb["mproj"].rearrange("(hp p) f -> p hp f", p=128)
        hid_free_pe = 0
        for tcp in (0, 2):
            for hp in range(32):
                s, wsl, c = ring1.load(wfc[:, :, hp * 128:(hp + 1) * 128],
                                       cast_done["fc"])
                for t2 in range(2):
                    bk = (hp % 2) * 2 + t2
                    take_bank(bk)
                    P.w("tensor", *c)
                    for cp in range(NCP):
                        mm(bank(bk), wsl[:, cp * 128:(cp + 1) * 128],
                           src_xln(cp, tcp + t2), cp == 0, cp == NCP - 1)
                    pe_c = pe_mark()
                    P.w("scalar", "PE", max(pe_c, hid_free_pe))
                    dsth = qkt[:, t2 * 16384 + hp * 512:
                               t2 * 16384 + (hp + 1) * 512]
                    a_c = P.inc("scalar", lambda e, d=dsth, p=bank(bk),
                                bb=bias_ap("fc_b", hp): e.activation(
                                    d, p, AF.Gelu_apprx_tanh, bias=bb,
                                    scale=1.0), "ACT")
                    bank_free[bk] = ("ACT", a_c)
                ring1.done(s, pe_c)
            a_hid = P.c["ACT"]
            for fp in range(8):
                for t2 in range(2):
                    take_bank(4 + t2)
                P.w("tensor", "ACT", a_hid)
                for q in range(4):
                    s, wsl, c = ring_mp.load(
                        wmp[:, q * 8:(q + 1) * 8,
                            fp * 128:(fp + 1) * 128], cast_done["mproj"])
                    P.w("tensor", *c)
                    for t2 in range(2):
                        for hh in range(8):
                            hp = q * 8 + hh
                            mm(bank(4 + t2), wsl[:, hh * 128:(hh + 1) * 128],
                               qkt[:, t2 * 16384 + hp * 512:
                                   t2 * 16384 + (hp + 1) * 512],
                               hp == 0, hp == 31)
                    pe_c = pe_mark()
                    ring_mp.done(s, pe_c)
                P.w("vector", "PE", pe_c)
                for t2 in range(2):
                    v_c = P.inc("vector", lambda e,
                                d=src_h(fp, tcp + t2), p=bank(4 + t2),
                                bb=bias_ap("mproj_b", fp),
                                r=src_h(fp, tcp + t2):
                                e.scalar_tensor_tensor(d, p, bb, r, ALU.add,
                                                       ALU.add), "DVE")
                    bank_free[4 + t2] = ("DVE", v_c)
            hid_free_pe = P.c["PE"]
        P.barrier()
        fm_tap("hmlp", b, h_sb)
        if last_stage == "hmlp":
            return

        phase("outer1")
        for tc in range(NTC):
            c = load_x_chunk(b, tc)
            P.w("vector", *c)

            def out_add(cp, t, d, tc=tc):
                P.inc("vector", lambda e, d=d, o=src_x1(cp, tc),
                      a=xchunk[:, cp * 512:(cp + 1) * 512]:
                      e.tensor_tensor(o, d, a, ALU.add), "DVE")

            fm_ln(src_h, lambda cp, t, tc=tc: src_xln(cp, tc), tcs=[tc],
                  out_fn=out_add)
        P.barrier()
        fm_tap("x1", b, v_sb)
        if last_stage == "x1":
            return

        phase("x1ln")
        fm_ln(src_x1, src_xln)
        P.barrier()
        fm_tap("x1ln", b, xln_sb)

        def load_expert(e_, w_pe=(0, 0)):
            if e_ == -1:
                s1_, s2_ = wb["sw1"], wb["sw2"]
                d1, d2 = cast_done["sw1"], cast_done["sw2"]
            else:
                s1_, s2_ = wb["rw1"][e_], wb["rw2"][e_]
                d1, d2 = cast_done["rw1_%d" % e_], cast_done["rw2_%d" % e_]
            P.w("sync", "PE", w_pe[0])
            P.w("sync", *d1)
            c1 = ld("WE1", h_sb[:, :].rearrange("p (cp f) -> p cp f", cp=8),
                    s1_.rearrange("(cp p) f -> p cp f", p=128))
            P.w("sync", "PE", w_pe[1])
            P.w("sync", *d2)
            c2 = ld("WE2", wbuf[:, 0:8192].rearrange("p (hp f) -> p hp f", hp=8),
                    s2_.rearrange("(hp p) f -> p hp f", p=128))
            return c1, c2

        if MOVE_LE:
            cexp = None
        else:
            qguard()
            cexp = load_expert(-1)

        phase("gate")
        wg = wb["gate"].rearrange("(cp p) f -> p cp f", p=128)
        P.w("sync", *cast_done["gate"])
        cg = ld("WG", gwb[:, 0:64], wg[:, :, :])
        for tt in range(16):
            tc, to = tt // 4, (tt % 4) * 128
            bkg = GATE_BANKS[tt % 2]
            take_bank(bkg)
            P.w("tensor", *cg)
            for cp in range(NCP):
                mm(bank(bkg, 0, 128, 0, 8), src_xln(cp, tc)[:, to:to + 128],
                   gwb[:, cp * 8:(cp + 1) * 8], cp == 0, cp == NCP - 1)
            pe_c = pe_mark()
            P.w("vector", "PE", pe_c)
            g0 = gsm[:, 0:8]
            P.inc("vector", lambda e, p=bank(bkg, 0, 128, 0, 8): e.tensor_tensor(
                g0, p, gbb[:, :], ALU.add), "DVE")
            bank_free[bkg] = ("DVE", P.c["DVE"])
            m1 = gsm[:, 8:9]
            P.inc("vector", lambda e: e.reduce_max(m1, g0, AX.X,
                                                   negate=True), "DVE")
            ge = gsm[:, 16:24]
            P.w("scalar", "DVE", P.c["DVE"])
            a_e = P.inc("scalar", lambda e: e.activation(
                ge, g0, AF.Exp, bias=m1, scale=1.0), "ACT")
            P.w("vector", "ACT", a_e)
            sm = gsm[:, 9:10]
            P.inc("vector", lambda e: e.reduce_sum(sm, ge, AX.X), "DVE")
            rc = gsm[:, 10:11]
            P.inc("vector", lambda e: e.reciprocal(rc, sm), "DVE")
            pr = gsm[:, 24:32]
            P.inc("vector", lambda e: e.tensor_scalar(pr, ge, rc, None,
                                                      ALU.mult), "DVE")
            mp = gsm[:, 11:12]
            P.inc("vector", lambda e: e.reduce_max(mp, pr, AX.X), "DVE")
            s1 = gsm[:, 32:40]
            P.inc("vector", lambda e: e.tensor_scalar(s1, pr, mp, None,
                                                      ALU.is_ge), "DVE")
            pq = gsm[:, 40:48]
            P.inc("vector", lambda e: e.scalar_tensor_tensor(
                pq, s1, -1e30, pr, ALU.mult, ALU.add), "DVE")
            m2 = gsm[:, 12:13]
            P.inc("vector", lambda e: e.reduce_max(m2, pq, AX.X), "DVE")
            P.inc("vector", lambda e: e.tensor_scalar(s1, pr, m2, None,
                                                      ALU.is_ge), "DVE")
            v_cw = P.inc("vector", lambda e, tt=tt: e.tensor_tensor(
                cwtm[:, tt * 8:(tt + 1) * 8], pr, s1, ALU.mult), "DVE")
            if GATE_SYNC:
                P.barrier()
        P.barrier()
        if "cw" in taps:
            c = gstore(tap_out["cw"][b], cwtm[:, :])
            P.w("gpsimd", "TS", c)
        if last_stage == "gate":
            return

        if cexp is None:
            qguard()
            cexp = load_expert(-1)
        swz_free_pe = 0
        for ei, e_ in enumerate([-1, 0, 1, 2, 3, 4, 5, 6, 7][:N_EXP_DBG]):
            phase("exp%d" % e_)
            c1, c2 = cexp
            routed = e_ >= 0
            last_w1 = last_w2 = 0
            for tc in range(NTC):
                xch_dve = [0] * 8
                P.w("vector", "PE", swz_free_pe)
                for j in range(8 if "hid" not in DBG_SKIP else 0):
                    bka, bkb = (j % 2) * 2, (j % 2) * 2 + 1
                    take_bank(bka)
                    P.w("tensor", *c1)
                    for cp in range(NCP):
                        mm(bank(bka), h_sb[:, cp * 2048 + j * 128:
                                           cp * 2048 + j * 128 + 128],
                           src_xln(cp, tc), cp == 0, cp == NCP - 1)
                    take_bank(bkb)
                    for cp in range(NCP):
                        mm(bank(bkb), h_sb[:, cp * 2048 + 1024 + j * 128:
                                           cp * 2048 + 1024 + j * 128 + 128],
                           src_xln(cp, tc), cp == 0, cp == NCP - 1)
                    pe_c = pe_mark()
                    if tc == NTC - 1 and j == 7:
                        last_w1 = pe_c
                    if "silu" in DBG_SKIP:
                        bank_free[bka] = ("PE", pe_c)
                        bank_free[bkb] = ("PE", pe_c)
                        continue
                    P.w("scalar", "PE", pe_c)
                    sl = sqbuf[:, (j % 2) * 512:(j % 2 + 1) * 512]
                    bj = ("sb1", j) if not routed else ("rb1", e_ * 16 + j)
                    a_c = P.inc("scalar", lambda e, s=sl, p=bank(bka),
                                bb=bias_ap(*bj): e.activation(
                                    s, p, AF.Silu, bias=bb, scale=1.0), "ACT")
                    bank_free[bka] = ("ACT", a_c)
                    if "stt" in DBG_SKIP:
                        bank_free[bkb] = ("ACT", a_c)
                        continue
                    P.w("vector", "ACT", a_c)
                    bj2 = ("sb1", 8 + j) if not routed else \
                          ("rb1", e_ * 16 + 8 + j)
                    v_c = P.inc("vector", lambda e,
                                d=xchunk[:, j * 512:(j + 1) * 512],
                                p=bank(bkb), bb=bias_ap(*bj2), s=sl:
                                e.scalar_tensor_tensor(d, p, bb, s, ALU.add,
                                                       ALU.mult), "DVE")
                    bank_free[bkb] = ("DVE", v_c)
                    xch_dve[j] = v_c
                if routed:
                    take_bank(6)
                    for ti in range(4):
                        mmc[0] += 1
                        P.on("tensor", lambda e,
                             o=bank(6, 0, 1, ti * 128, ti * 128 + 128),
                             i_=cwtm[:, (tc * 4 + ti) * 8 + e_:
                                     (tc * 4 + ti) * 8 + e_ + 1]:
                             e.transpose(o, i_, id32[:, :]))
                    c_tr = pe_mark()
                    P.w("vector", "PE", c_tr)
                    v_cr = P.inc("vector", lambda e: e.tensor_copy(
                        crow[0:1, :], bank(6, 0, 1, 0, 512)), "DVE")
                    P.w("tensor", "DVE", v_cr)
                    mm(bank(6), ones_bf[0:1, :], crow[0:1, :], True, True)
                    c_bc = pe_mark()
                    P.w("vector", "PE", c_bc)
                    v_bc = P.inc("vector", lambda e: e.tensor_copy(
                        bcsb[:, 0:512], bank(6)), "DVE")
                    bank_free[6] = ("DVE", v_bc)
                for fp in range(8):
                    if "w2" in DBG_SKIP:
                        break
                    if "w2one" in DBG_SKIP and (fp > 0 or tc > 0):
                        break
                    bk = 4 + fp % 2
                    take_bank(bk)
                    P.w("tensor", *c2)
                    if "pedrain" in DBG_SKIP:
                        P.w("tensor", "PE", P.c["PE"])
                    for hp in range(NCP):
                        P.w("tensor", "DVE", xch_dve[hp])
                        rhs_ = xchunk[:, hp * 512:(hp + 1) * 512]
                        mm(bank(bk), wbuf[:, hp * 1024 + fp * 128:
                                          hp * 1024 + fp * 128 + 128],
                           rhs_, hp == 0, hp == NCP - 1)
                    pe_c = pe_mark()
                    if tc == NTC - 1 and fp == 7:
                        last_w2 = pe_c
                    acc_ap = accum[:, fp * 2048 + tc * 512:
                                   fp * 2048 + (tc + 1) * 512]
                    P.w("vector", "PE", pe_c)
                    if "acc" in DBG_SKIP:
                        bank_free[bk] = ("PE", pe_c)
                        continue
                    if not routed:
                        v_c = P.inc("vector", lambda e, a=acc_ap, p=bank(bk),
                                    bb=bias_ap("sb2", fp): e.tensor_scalar(
                                        a, p, bb, None, ALU.add), "DVE")
                        bank_free[bk] = ("DVE", v_c)
                    else:
                        tsl = tmp1[:, (fp % 2) * 512:(fp % 2 + 1) * 512]
                        P.w("vector", "POOL", pool_free.get(fp % 2, 0))
                        v_c = P.inc("vector", lambda e, t=tsl, p=bank(bk),
                                    bb=bias_ap("rb2", e_ * 8 + fp):
                                    e.scalar_tensor_tensor(
                                        t, p, bb, bcsb[:, 0:512], ALU.add,
                                        ALU.mult), "DVE")
                        bank_free[bk] = ("DVE", v_c)
                        P.w("gpsimd", "DVE", v_c)
                        pc = P.inc("gpsimd", lambda e, a=acc_ap, t=tsl:
                                   e.tensor_tensor(a, a, t, ALU.add), "POOL")
                        pool_free[fp % 2] = pc
                swz_free_pe = P.c["PE"]
            if ei < 8 and "pref" not in DBG_SKIP:
                cexp = load_expert(e_ + 1 if routed else 0,
                                   w_pe=(last_w1, last_w2))
        ring1.bump(P.c["PE"])
        ring_mp.bump(P.c["PE"])
        P.barrier()
        if "acc" in taps:
            c = gstore(tap_out["acc"][b].rearrange("(cp p) t -> p cp t", p=128),
                       accum[:, :].rearrange("p (cp t) -> p cp t", cp=8))
            P.w("gpsimd", "TS", c)
        if last_stage == "acc":
            return

        phase("finalln")
        st_prev = [0]
        P.w("vector", "POOL", P.c["POOL"])
        for tc in range(NTC):
            P.w("vector", "PE", P.c["PE"])
            for cp in range(NCP):
                P.inc("vector", lambda e, cp=cp, tc=tc: e.tensor_copy(
                    xchunk[:, cp * 512:(cp + 1) * 512],
                    accum[:, cp * 2048 + tc * 512:cp * 2048 + (tc + 1) * 512]),
                    "DVE")

            def out_store(cp, t, d, tc=tc):
                P.w("vector", "OS", st_prev[0])
                ost = ostage[:, 0:512]
                v_c = P.inc("vector", lambda e, o=ost, d=d,
                            a=src_x1(cp, tc): e.tensor_tensor(o, d, a, ALU.add),
                            "DVE")
                P.w("sync", "DVE", v_c)
                st_prev[0] = store(
                    "OS", y[b][cp * 128:(cp + 1) * 128,
                               tc * 512:(tc + 1) * 512], ost)[1]

            fsrc = lambda cp, t: xchunk[:, cp * 512:(cp + 1) * 512]
            DBG_LN_SRC[0] = fsrc
            fm_ln(fsrc, lambda cp, t, tc=tc: src_xln(cp, tc), tcs=[tc],
                  out_fn=out_store)
        P.barrier()

    if last_stage == "init":
        if "init" in taps:
            qguard()
            c = store("OS", tap_out["init"], bia[:, :])
    else:
        for b in range(n_b):
            run_b(b)

    for s in DMA_SEMS:
        eng = "gpsimd" if (s == "TS" or s.startswith(("cx", "cw_", "cr"))
                           or s in ("WC", "WXG")) else "sync"
        P.w(eng, s, P.c[s])

    with nc.Block() as block:
        with nc.allow_low_precision(reason="bf16 operand staging is intentional"):
            P.install(block)
    return nc, list(tap_out)


def _pack_biases(i):
    cols = np.zeros((128, BIAS_COLS), dtype=np.float32)

    def put(nme, vec):
        v = np.asarray(vec, dtype=np.float32).reshape(-1)
        n = v.size // 128
        cols[:, BIAS_OFF[nme]:BIAS_OFF[nme] + n] = v.reshape(n, 128).T

    put("ln1_g", i["L0_attn_ln1_g"]); put("ln1_b", i["L0_attn_ln1_b"])
    put("lnx_g", i["L0_attn_lnx_g"]); put("lnx_b", i["L0_attn_lnx_b"])
    put("ln2_g", i["L0_attn_ln2_g"]); put("ln2_b", i["L0_attn_ln2_b"])
    put("cattn_b", i["L0_attn_c_attn_b"])
    put("aproj_b", i["L0_attn_attn_proj_b"])
    put("qattn_b", i["L0_attn_q_attn_b"])
    put("xkv_b", i["L0_attn_x_kv_b"])
    put("xproj_b", i["L0_attn_x_proj_b"])
    put("fc_b", i["L0_attn_fc_b"])
    put("mproj_b", i["L0_attn_mlp_proj_b"])
    put("sb1", i["L0_ffn_sb1"]); put("sb2", i["L0_ffn_sb2"])
    put("rb1", i["L0_ffn_rb1"]); put("rb2", i["L0_ffn_rb2"])
    put("scorer", i["L0_attn_scorer_logits"])
    gb = np.zeros(128, dtype=np.float32)
    gb[0:8] = np.asarray(i["L0_ffn_gate_b"], dtype=np.float32)
    cols[:, BIAS_OFF["gate_b"]] = gb
    cols[:, BIAS_OFF["c_eps"]] = 1e-5
    cols[:, BIAS_OFF["c_shift"]] = -20.0
    return cols


def kernel(**inputs):
    n_b = N_B
    i = inputs
    cc = np.ascontiguousarray

    x = np.asarray(i["L0_attn_hidden_states"], dtype=np.float32)
    enc = np.asarray(i["L0_attn_encoder_hidden_states"], dtype=np.float32)[0]
    biases = _pack_biases(i)
    jj = np.arange(644)[None, :]
    pp = np.arange(128)[:, None]
    mask = np.where(jj >= pp, 0.0, -10000.0).astype(np.float32)
    ones_bf = np.ones((128, 128), dtype=np.float32)
    sel = np.zeros((33, 128), dtype=np.float32)
    sel[0, 0:64] = 1.0
    sel[32, 64:128] = 1.0
    id32 = np.eye(128, dtype=np.float32)

    base = dict(
        gbrow=np.tile(np.asarray(i["L0_ffn_gate_b"], dtype=np.float32)[None, :],
                      (128, 1)),
        enc_fm=cc(enc.T),
        biases=biases,
        maskbf=mask,
        ones_bf=ones_bf,
        sel33=sel,
        id32=id32,
        w_cattn=np.asarray(i["L0_attn_c_attn_w"], dtype=np.float32),
        w_qattn=np.asarray(i["L0_attn_q_attn_w"], dtype=np.float32),
        w_xkv=np.asarray(i["L0_attn_x_kv_w"], dtype=np.float32),
        w_aproj=np.asarray(i["L0_attn_attn_proj_w"], dtype=np.float32),
        w_xproj=np.asarray(i["L0_attn_x_proj_w"], dtype=np.float32),
        w_fc=np.asarray(i["L0_attn_fc_w"], dtype=np.float32),
        w_mproj=np.asarray(i["L0_attn_mlp_proj_w"], dtype=np.float32),
        w_gate=np.asarray(i["L0_ffn_gate_w"], dtype=np.float32),
        w_sw1=np.asarray(i["L0_ffn_sw1"], dtype=np.float32)[0],
        w_sw2=np.asarray(i["L0_ffn_sw2"], dtype=np.float32)[0],
        w_rw1=np.asarray(i["L0_ffn_rw1"], dtype=np.float32),
        w_rw2=np.asarray(i["L0_ffn_rw2"], dtype=np.float32),
    )

    nc, tap_names = build_nc(n_b, TAPS, LAST_STAGE)
    in_maps = []
    for c in range(8):
        m = dict(base)
        xs = x[c * n_b:(c + 1) * n_b]
        m["x_fm"] = cc(xs.transpose(0, 2, 1))
        in_maps.append(m)

    res = run_bass_kernel_spmd(nc, in_maps, core_ids=list(range(8)))

    global _dbg
    _dbg = {t: [np.asarray(r["dbg_" + t]) for r in res.results]
            for t in tap_names}

    outs = []
    for c in range(8):
        yc = np.asarray(res.results[c]["y"])
        outs.append(yc.transpose(0, 2, 1))
    return np.concatenate(outs, axis=0).astype(np.float32)


# version 07 (final)
# speedup vs baseline: 1.2672x; 1.0097x over previous
import numpy as np
import ml_dtypes
import concourse.bass as bass
import concourse.mybir as mybir
from concourse.bass_utils import run_bass_kernel_spmd

F32 = mybir.dt.float32
BF16 = mybir.dt.bfloat16
AF = mybir.ActivationFunctionType
ALU = mybir.AluOpType
AX = mybir.AxisListType
BF = ml_dtypes.bfloat16

ENGINES = ("sync", "gpsimd", "tensor", "vector", "scalar")
SEMS = ("LD", "LDG", "PE", "ACT", "DVE", "POOL", "ST")

T = 2048
NTC = 4
D = 1024
NCP = 8
EPS = 1e-5
SHIFT = -20.0

N_B = 4
TAPS = ()
LAST_STAGE = "all"
_dbg = {}

BIAS_LAYOUT = [
    ("ln1_g", 8), ("ln1_b", 8), ("lnx_g", 8), ("lnx_b", 8),
    ("ln2_g", 8), ("ln2_b", 8),
    ("cattn_b", 24), ("aproj_b", 8), ("qattn_b", 8), ("xkv_b", 16),
    ("xproj_b", 8), ("fc_b", 32), ("mproj_b", 8),
    ("sb1", 16), ("sb2", 8), ("rb1", 128), ("rb2", 64),
    ("scorer", 16), ("gate_b", 1), ("c_eps", 1), ("c_shift", 1),
]
BIAS_OFF = {}
_o = 0
for _n, _c in BIAS_LAYOUT:
    BIAS_OFF[_n] = _o
    _o += _c
BIAS_COLS = _o


class Prog:
    def __init__(self):
        self.q = {e: [] for e in ENGINES}
        self.c = {}
        self.sem = {}
        self.last_wait = {e: {} for e in ENGINES}
        self.real_top = {e: False for e in ENGINES}

    def on(self, eng, fn):
        self.q[eng].append(fn)
        self.real_top[eng] = True

    OWN = {"vector": "DVE", "scalar": "ACT", "gpsimd": "POOL"}

    def inc(self, eng, fn, sem, n=1):
        own = self.OWN.get(eng)
        if own is not None:
            self.w(eng, own, self.c.get(own, 0))
        self.c[sem] = self.c.get(sem, 0) + n
        s = self.sem[sem]
        self.q[eng].append(lambda e, fn=fn, s=s, n=n: fn(e).then_inc(s, n))
        self.real_top[eng] = False
        return self.c[sem]

    def mark(self, eng, sem):
        assert self.real_top[eng], "mark() must follow a real engine op"
        f = self.q[eng].pop()
        self.c[sem] = self.c.get(sem, 0) + 1
        s = self.sem[sem]
        self.q[eng].append(lambda e, f=f, s=s: f(e).then_inc(s, 1))
        self.real_top[eng] = False
        return self.c[sem]

    def w(self, eng, sem, val):
        if val <= 0 or val <= self.last_wait[eng].get(sem, 0):
            return
        self.last_wait[eng][sem] = val
        s = self.sem[sem]
        self.q[eng].append(lambda e, s=s, v=val: e.wait_ge(s, v))
        self.real_top[eng] = False

    def barrier(self):
        engs = ("tensor", "vector", "scalar", "gpsimd")
        m = {"tensor": "PE", "vector": "DVE", "scalar": "ACT", "gpsimd": "POOL"}
        for e in engs:
            if self.real_top[e]:
                self.mark(e, m[e])
        snap = dict(self.c)
        for e in engs:
            for o in engs:
                if o != e:
                    self.w(e, m[o], snap.get(m[o], 0))

    def install(self, block):
        for e in ENGINES:
            ops = self.q[e]
            getattr(block, e)(lambda eng, ops=ops: [f(eng) for f in ops])


GATE_SYNC = False
GATE_BANKS = (0, 1)
DBG_LN = False
DBG_LN_TC = 0
DBG_LN_SRC = [None]
N_EXP_DBG = 9
MOVE_LE = False
DBG_SKIP = frozenset()
PHASE_LOG = []


def build_nc(n_b, taps, last_stage):
    nc = bass.Bass("TRN2", target_bir_lowering=False)
    P = Prog()
    taps = set(taps)

    x_fm = nc.dram_tensor("x_fm", [n_b, D, T], F32, kind="ExternalInput").ap()
    enc_fm = nc.dram_tensor("enc_fm", [D, T], F32, kind="ExternalInput").ap()
    biases_i = nc.dram_tensor("biases", [128, BIAS_COLS], F32,
                              kind="ExternalInput").ap()
    mask_i = nc.dram_tensor("maskbf", [128, 644], F32, kind="ExternalInput").ap()
    ones_i = nc.dram_tensor("ones_bf", [128, 128], F32, kind="ExternalInput").ap()
    sel_i = nc.dram_tensor("sel33", [33, 128], F32, kind="ExternalInput").ap()
    id_i = nc.dram_tensor("id32", [128, 128], F32, kind="ExternalInput").ap()
    gbrow_i = nc.dram_tensor("gbrow", [128, 8], F32, kind="ExternalInput").ap()
    wshape = dict(cattn=[D, 3072], qattn=[D, D], xkv=[D, 2048], aproj=[D, D],
                  xproj=[D, D], fc=[D, 4096], mproj=[4096, D], gate=[D, 8],
                  sw1=[D, 2048], sw2=[D, D], rw1=[8, D, 2048], rw2=[8, D, D])
    w_in, wb = {}, {}
    for nme, shp in wshape.items():
        w_in[nme] = nc.dram_tensor("w_" + nme, shp, F32, kind="ExternalInput").ap()
        wb[nme] = nc.dram_tensor("wb_" + nme, shp, BF16).ap()
    y = nc.dram_tensor("y", [n_b, D, T], F32, kind="ExternalOutput").ap()
    xbf = nc.dram_tensor("xbf", [n_b, D, T], BF16).ap()
    k2t_s = nc.dram_tensor("k2t_s", [D, T], BF16).ap()
    v2_s = nc.dram_tensor("v2_s", [T, D], BF16).ap()

    tap_out = {}
    for nme, shp in [("init", [128, BIAS_COLS]),
                     ("h0", [n_b, D, T]), ("xln1", [n_b, D, T]),
                     ("qt", [n_b, D, T]), ("kt", [n_b, D, T]),
                     ("v", [n_b, T, D]), ("hself", [n_b, D, T]),
                     ("k2t", [D, T]), ("v2", [T, D]),
                     ("hcross", [n_b, D, T]),
                     ("hmlp", [n_b, D, T]), ("x1", [n_b, D, T]),
                     ("x1ln", [n_b, D, T]), ("cw", [n_b, 128, 128]),
                     ("acc", [n_b, D, T])]:
        if nme in taps:
            tap_out[nme] = nc.dram_tensor("dbg_" + nme, shp, F32,
                                          kind="ExternalOutput").ap()

    A = nc.alloc_sbuf_tensor
    ones_bf = A("s_ones_bf", [128, 128], BF16)
    sel33 = A("s_sel33", [33, 128], BF16)
    maskbf = A("s_maskbf", [128, 644], BF16)
    idbf = A("s_idbf", [128, 128], BF16)
    id32 = A("s_id32", [128, 128], F32)
    bia = A("bia", [128, BIAS_COLS], F32)
    scor20 = A("scor20", [128, 16], F32)
    mstat = A("mstat", [128, 512], F32)
    rstdb = A("rstdb", [128, 512], BF16)
    sqbuf = A("sqbuf", [128, 1024], BF16)
    rden = A("rden", [33, 512], BF16)
    bcsb = A("bcsb", [128, 1024], BF16)
    tmp1 = A("tmp1", [128, 1024], BF16)
    ostage = A("ostage", [128, 512], F32)
    expst = A("expst", [128, 2048], BF16)
    xchunk = A("xchunk", [128, 4096], BF16)
    wbuf = A("wbuf", [128, 8192], BF16)
    h_sb = A("h_sb", [128, 16384], BF16)
    xln_sb = A("xln_sb", [128, 16384], BF16)
    qkt = A("qkt", [128, 32768], BF16)
    v_sb = A("v_sb", [128, 16384], BF16)
    gsm = A("gsm", [128, 48], F32)
    gwb = A("gwb", [128, 64], BF16)
    gbb = A("gbb", [128, 8], F32)
    cwtm = A("cwtm", [128, 128], F32)
    crow = A("crow", [1, 512], BF16)
    accum = qkt.bitcast(F32)

    p01 = nc.alloc_psum_tensor("p01", [128, 1024], F32)
    p2 = nc.alloc_psum_tensor("p2", [128, 512], F32)
    p3 = nc.alloc_psum_tensor("p3", [128, 512], F32)
    p45 = nc.alloc_psum_tensor("p45", [128, 1024], F32)
    p6 = nc.alloc_psum_tensor("p6", [128, 512], F32)
    p7 = nc.alloc_psum_tensor("p7", [128, 512], F32)

    def bank(i, r0=0, r1=128, c0=0, c1=512):
        if i == 0:
            return p01[r0:r1, c0:c1]
        if i == 1:
            return p01[r0:r1, 512 + c0:512 + c1]
        if i == 4:
            return p45[r0:r1, c0:c1]
        if i == 5:
            return p45[r0:r1, 512 + c0:512 + c1]
        return {2: p2, 3: p3, 6: p6, 7: p7}[i][r0:r1, c0:c1]

    DMA_SEMS = (["WC", "WCH", "WX", "WXG", "WG", "WE1", "WE2", "WV0", "WV1", "KL", "VL",
                 "KS", "OS", "TS"]
                + ["WS%d" % j for j in range(4)] + ["WM%d" % j for j in range(4)]
                + ["cx%d" % b for b in range(n_b)]
                + ["cw_" + nm for nm in ("cattn", "qattn", "xkv", "aproj",
                                         "xproj", "fc", "mproj", "gate",
                                         "sw1", "sw2")]
                + ["cr1_%d" % e for e in range(8)]
                + ["cr2_%d" % e for e in range(8)])
    for s in list(SEMS) + DMA_SEMS:
        P.sem[s] = nc.semaphore(s).__enter__()
        P.c[s] = 0

    def ld(sem, dst, src):
        return (sem, P.inc("sync", lambda e, d=dst, s=src: e.dma_start(d, s),
                           sem, 16))

    def ldg(sem, dst, src):
        return (sem, P.inc("gpsimd", lambda e, d=dst, s=src: e.dma_start(d, s),
                           sem, 16))

    def store(sem, dst, src):
        return (sem, P.inc("sync", lambda e, d=dst, s=src: e.dma_start(d, s),
                           sem, 16))

    mmc = [0]

    def phase(nm):
        PHASE_LOG.append((nm, mmc[0]))

    def mm(out, lhsT, rhs, start, stop, tp=None, skip=False):
        mmc[0] += 1
        P.on("tensor", lambda e, o=out, l=lhsT, r=rhs, st=start, sp=stop, tp=tp,
             sk=skip: e.matmul(o, l, r, start=st, stop=sp, tile_position=tp,
                               skip_group_check=sk))

    def pe_mark():
        return P.mark("tensor", "PE")

    def tr(out, in_, ident):
        mmc[0] += 1
        P.on("tensor", lambda e, o=out, i=in_, d=ident: e.transpose(o, i, d))
        return pe_mark()

    bank_free = {}

    def take_bank(i):
        if i in bank_free:
            s, v = bank_free[i]
            P.w("tensor", s, v)

    def bias_ap(nme, j, n=1):
        off = BIAS_OFF[nme] + j
        return bia[:, off:off + n]

    def qguard():
        P.w("sync", "PE", P.c["PE"])
        P.w("sync", "DVE", P.c["DVE"])
        P.w("sync", "ACT", P.c["ACT"])
        P.w("sync", "POOL", P.c["POOL"])

    ldg("WC", ones_bf[:, :], ones_i[:, :])
    ldg("WC", sel33[:, :], sel_i[:, :])
    ldg("WC", maskbf[:, :], mask_i[:, :])
    ldg("WC", idbf[:, :], id_i[:, :])
    ld("WCH", id32[:, :], id_i[:, :])
    ld("WCH", gbb[:, :], gbrow_i[:, :])
    ld("WCH", bia[:, :], biases_i[:, :])
    WC_ALL = P.c["WC"]
    WCH_ALL = P.c["WCH"]

    cast_done = {}
    for b in range(n_b):
        cast_done["x%d" % b] = ldg("cx%d" % b, xbf[b], x_fm[b])
    for nme in ("cattn", "qattn", "xkv", "aproj", "xproj", "fc", "mproj",
                "gate", "sw1", "sw2"):
        cast_done[nme] = ldg("cw_" + nme, wb[nme], w_in[nme])
    for e8 in range(8):
        cast_done["rw1_%d" % e8] = ldg("cr1_%d" % e8, wb["rw1"][e8],
                                       w_in["rw1"][e8])
        cast_done["rw2_%d" % e8] = ldg("cr2_%d" % e8, wb["rw2"][e8],
                                       w_in["rw2"][e8])

    for eng in ("vector", "tensor", "scalar"):
        P.w(eng, "WC", WC_ALL)
        P.w(eng, "WCH", WCH_ALL)
    P.inc("vector", lambda e: e.tensor_scalar(
        scor20[:, :], bia[:, BIAS_OFF["scorer"]:BIAS_OFF["scorer"] + 16],
        SHIFT, None, ALU.add), "DVE")
    P.inc("vector", lambda e: e.memset(rden[:, :], 0.0), "DVE")
    P.barrier()

    class Ring:
        def __init__(self, lo_col, n_slots, slot_cols, sempfx):
            self.base, self.n, self.cols = lo_col, n_slots, slot_cols
            self.free_pe = [0] * n_slots
            self.sems = ["%s%d" % (sempfx, j) for j in range(n_slots)]
            self.i = 0

        def bump(self, pe):
            self.free_pe = [max(pe, x) for x in self.free_pe]

        def load(self, src, dep=None):
            s = self.i % self.n
            self.i += 1
            P.w("sync", "PE", self.free_pe[s])
            if dep:
                P.w("sync", *dep)
            b0 = self.base + s * self.cols
            half = self.cols // 2
            k = src.shape[1]
            ld(self.sems[s], wbuf[:, b0:b0 + half], src[:, 0:k // 2, :])
            c = ld(self.sems[s], wbuf[:, b0 + half:b0 + self.cols],
                   src[:, k // 2:k, :])
            dst = wbuf[:, b0:b0 + self.cols]
            return s, dst, c

        def done(self, s, pe_cnt):
            self.free_pe[s] = pe_cnt

    ring1 = Ring(0, 4, 1024, "WS")
    ring_mp = Ring(4096, 4, 1024, "WM")

    def fm_proj(wdram, wdep, fps, src_fn, evac, wfp_base=0, tcs=range(NTC)):
        wr = wdram.rearrange("(cp p) f -> p cp f", p=128)
        for fp in fps:
            s, wsl, c = ring1.load(
                wr[:, :, (wfp_base + fp) * 128:(wfp_base + fp + 1) * 128], wdep)
            pe_c = 0
            for tc in tcs:
                bk = (fp * len(tcs) + tc) % 4
                take_bank(bk)
                P.w("tensor", *c)
                for cp in range(NCP):
                    mm(bank(bk), wsl[:, cp * 128:(cp + 1) * 128],
                       src_fn(cp, tc), cp == 0, cp == NCP - 1)
                pe_c = pe_mark()
                bank_free[bk] = evac(fp, tc, bk, pe_c)
            ring1.done(s, pe_c)

    def evac_act(dst_fn, bias_name, bias_js=None, func=AF.Identity, scale=1.0):
        def f(fp, tc, bk, pe_c):
            P.w("scalar", "PE", pe_c)
            bap = bias_ap(bias_name, bias_js[fp] if bias_js else fp) \
                if bias_name else 0.0
            v = P.inc("scalar", lambda e, d=dst_fn(fp, tc), p=bank(bk), b=bap:
                      e.activation(d, p, func, bias=b, scale=scale), "ACT")
            return ("ACT", v)
        return f

    def evac_resid(dst_fn, bias_name, res_fn):
        def f(fp, tc, bk, pe_c):
            P.w("vector", "PE", pe_c)
            v = P.inc("vector", lambda e, d=dst_fn(fp, tc), p=bank(bk),
                      b=bias_ap(bias_name, fp), r=res_fn(fp, tc):
                      e.scalar_tensor_tensor(d, p, b, r, ALU.add, ALU.add),
                      "DVE")
            return ("DVE", v)
        return f

    def fm_ln(src_fn, dst_fn, g=None, bnm=None, tcs=range(NTC), out_fn=None):
        for tc in tcs:
            mb_, vb_ = 6, 7
            take_bank(mb_)
            P.w("tensor", "DVE", P.c["DVE"])
            for cp in range(NCP):
                mm(bank(mb_), ones_bf[:, :], src_fn(cp, tc), cp == 0,
                   cp == NCP - 1)
            pe_mean = pe_mark()
            P.w("scalar", "PE", pe_mean)
            P.w("scalar", "DVE", P.c["DVE"])
            a_m = P.inc("scalar", lambda e: e.activation(
                mstat[:, :], bank(mb_), AF.Copy, bias=0.0, scale=1.0 / D), "ACT")
            bank_free[mb_] = ("ACT", a_m)
            take_bank(vb_)
            sq_free = [0, 0]
            last_xc = 0
            for cp in range(NCP):
                P.w("vector", "ACT", a_m)
                sl = cp % 2
                P.w("vector", "PE", sq_free[sl])
                d = dst_fn(cp, tc)
                last_xc = P.inc("vector", lambda e, d=d, s=src_fn(cp, tc):
                                e.tensor_tensor(d, s, mstat[:, :], ALU.subtract),
                                "DVE")
                sq = sqbuf[:, sl * 512:(sl + 1) * 512]
                v_sq = P.inc("vector", lambda e, q=sq, d=d:
                             e.tensor_tensor(q, d, d, ALU.mult), "DVE")
                P.w("tensor", "DVE", v_sq)
                mm(bank(vb_), ones_bf[:, :], sq, cp == 0, cp == NCP - 1)
                sq_free[sl] = pe_mark()
            pe_var = P.c["PE"]
            P.w("scalar", "PE", pe_var)
            P.w("scalar", "DVE", last_xc)
            a_ln = P.inc("scalar", lambda e: e.activation(
                mstat[:, :], bank(vb_), AF.Ln, bias=bias_ap("c_eps", 0),
                scale=1.0 / D), "ACT")
            bank_free[vb_] = ("ACT", a_ln)
            a_rs = P.inc("scalar", lambda e: e.activation(
                rstdb[:, :], mstat[:, :], AF.Exp, bias=0.0, scale=-0.5), "ACT")
            for cp in range(NCP):
                P.w("vector", "ACT", a_rs)
                d = dst_fn(cp, tc)
                P.inc("vector", lambda e, d=d: e.tensor_tensor(
                    d, d, rstdb[:, :], ALU.mult), "DVE")
                if g is not None:
                    P.inc("vector", lambda e, d=d, gg=bias_ap(g, cp),
                          bb=bias_ap(bnm, cp): e.tensor_scalar(
                              d, d, gg, bb, ALU.mult, ALU.add), "DVE")
                if out_fn is not None:
                    out_fn(cp, tc, d)

    INSTR = [dict(STb=(0, 1), OTb=2, DENb=3, ex=0),
             dict(STb=(4, 5), OTb=6, DENb=7, ex=1)]
    inst_free = [{"ACT": 0, "DVE": 0}, {"ACT": 0, "DVE": 0}]

    def st_ap(R, c0, c1):
        if R["STb"] == (0, 1):
            return p01[:, c0:c1]
        return p45[:, c0:c1]

    def attention(cross):
        vb_name = "cattn_b" if not cross else "xkv_b"
        vb_js = [16 + p for p in range(8)] if not cross else \
                [8 + p for p in range(8)]

        def make_steps(hp, qc, R):
            ex = R["ex"]
            exA0, exB0 = ex * 1024, ex * 1024 + 512
            if not cross:
                kbs = list(range(4 * qc + 4))
            else:
                kbs = list(range(min(15, 4 * qc + 4) + 1))
            steps = []
            for i, kb in enumerate(kbs):
                eb = expst if i % 2 == 0 else xln_sb
                off = kb * 128 - qc * 512
                lo = max(0, off) if not cross else max(0, off - 2)
                n = 512 - lo
                first, lastkb = i == 0, i == len(kbs) - 1
                qc0 = hp * 2048 + qc * 512 + lo
                kc0 = 16384 + hp * 2048 + kb * 128

                masked = off >= 0
                m0 = 0 if not cross else (0 if off >= 2 else 2)

                def ph1(kc0=kc0, qc0=qc0, lo=lo, n=n, R=R, masked=masked,
                        m0=m0):
                    if masked:
                        msl = maskbf[:, m0:m0 + n]
                        mm(st_ap(R, lo, lo + n), idbf[:, :], msl, True, False)
                        mm(st_ap(R, 512 + lo, 512 + lo + n), idbf[:, :], msl,
                           True, False)
                    st = not masked
                    mm(st_ap(R, lo, lo + n), qkt[0:64, kc0:kc0 + 128],
                       qkt[0:64, qc0:qc0 + n], st, True, (0, 0))
                    mm(st_ap(R, 512 + lo, 512 + lo + n),
                       qkt[64:128, kc0:kc0 + 128], qkt[64:128, qc0:qc0 + n],
                       st, True, (64, 0))
                    return pe_mark()

                def phe(kb=kb, ex=ex, R=R, lo=lo, n=n, eb=eb):
                    bias = bias_ap("c_shift", 0) if not cross \
                        else scor20[:, kb:kb + 1]
                    if lo == 0:
                        return P.inc("scalar", lambda e, b=bias: e.activation(
                            eb[:, ex * 1024:(ex + 1) * 1024],
                            st_ap(R, 0, 1024), AF.Exp, bias=b, scale=0.125),
                            "ACT")
                    P.inc("scalar", lambda e, b=bias: e.activation(
                        eb[:, ex * 1024 + lo:ex * 1024 + 512],
                        st_ap(R, lo, 512), AF.Exp, bias=b, scale=0.125), "ACT")
                    return P.inc("scalar", lambda e, b=bias: e.activation(
                        eb[:, ex * 1024 + 512 + lo:(ex + 1) * 1024],
                        st_ap(R, 512 + lo, 1024), AF.Exp, bias=b, scale=0.125),
                        "ACT")

                def ph2(kb=kb, lo=lo, n=n, first=first, lastkb=lastkb, hp=hp,
                        ex=ex, R=R, eb=eb):
                    va = v_sb[:, kb * 1024 + hp * 128:kb * 1024 + hp * 128 + 64]
                    vb_ = v_sb[:, kb * 1024 + hp * 128 + 64:
                               kb * 1024 + hp * 128 + 128]
                    ob = R["OTb"]
                    db = R["DENb"]
                    mm(bank(ob, 0, 64, lo, lo + n), va,
                       eb[:, exA0 + lo:exA0 + lo + n], first, lastkb, (0, 0),
                       skip=True)
                    mm(bank(ob, 64, 128, lo, lo + n), vb_,
                       eb[:, exB0 + lo:exB0 + lo + n], first, lastkb, (0, 64),
                       skip=True)
                    mm(bank(db, 0, 1, lo, lo + n), ones_bf[:, 0:1],
                       eb[:, exA0 + lo:exA0 + lo + n], first, lastkb, (0, 0),
                       skip=True)
                    mm(bank(db, 32, 33, lo, lo + n), ones_bf[:, 1:2],
                       eb[:, exB0 + lo:exB0 + lo + n], first, lastkb, (0, 32),
                       skip=True)
                    return pe_mark()

                steps.append((ph1, phe, ph2))

            def tail(hp=hp, qc=qc, R=R):
                pe_den = P.c["PE"]
                db, ob, ex = R["DENb"], R["OTb"], R["ex"]
                P.w("scalar", "PE", pe_den)
                P.inc("scalar", lambda e: e.activation(
                    mstat[0:1, :], bank(db, 0, 1), AF.Ln, bias=0.0,
                    scale=1.0), "ACT")
                P.inc("scalar", lambda e: e.activation(
                    rden[0:1, :], mstat[0:1, :], AF.Exp, bias=0.0,
                    scale=-1.0), "ACT")
                P.inc("scalar", lambda e: e.activation(
                    mstat[32:33, :], bank(db, 32, 33), AF.Ln, bias=0.0,
                    scale=1.0), "ACT")
                a_rd = P.inc("scalar", lambda e: e.activation(
                    rden[32:33, :], mstat[32:33, :], AF.Exp, bias=0.0,
                    scale=-1.0), "ACT")
                P.w("tensor", "ACT", a_rd)
                mm(bank(db), sel33[:, :], rden[:, :], True, True)
                c_bc = pe_mark()
                bch = bcsb[:, ex * 512:(ex + 1) * 512]
                dst = qkt[:, hp * 2048 + qc * 512:hp * 2048 + (qc + 1) * 512]
                P.w("vector", "PE", c_bc)
                P.inc("vector", lambda e, bc=bch, db=db: e.tensor_copy(
                    bc, bank(db)), "DVE")
                P.inc("vector", lambda e, d=dst, bc=bch:
                      e.tensor_tensor(d, bank(ob), bc, ALU.mult), "DVE")
                v_fin = P.inc("vector", lambda e, d=dst, bb=bias_ap(
                    vb_name, vb_js[hp]): e.tensor_scalar(
                        d, d, bb, None, ALU.add), "DVE")
                inst_free[ex] = {"ACT": a_rd, "DVE": v_fin}

            return steps, tail

        pend = [None, None]
        last_e = [0, 0]
        unit_dve = [inst_free[0]["DVE"], inst_free[1]["DVE"]]
        for hp0 in range(0, 8, 2):
            for qc in range(NTC):
                sA, tA = make_steps(hp0, qc, INSTR[0])
                sB, tB = make_steps(hp0 + 1, qc, INSTR[1])
                for i in range(len(sA)):
                    for ex, steps in ((0, sA), (1, sB)):
                        ph1, phe, ph2 = steps[i]
                        P.w("tensor", "ACT", last_e[ex])
                        pe_c = ph1()
                        P.w("scalar", "PE", pe_c)
                        last_e[ex] = phe()
                        prev = pend[ex]
                        pend[ex] = (ph2, last_e[ex], i == 0)
                        if prev is not None:
                            p2, ec, isf = prev
                            if isf:
                                P.w("tensor", "DVE", unit_dve[ex])
                            P.w("tensor", "ACT", ec)
                            p2()
                for ex, tl in ((0, tA), (1, tB)):
                    p2, ec, isf = pend[ex]
                    pend[ex] = None
                    if isf:
                        P.w("tensor", "DVE", unit_dve[ex])
                    P.w("tensor", "ACT", ec)
                    p2()
                    tl()
                    unit_dve[ex] = inst_free[ex]["DVE"]
        P.barrier()

    def src_xln(cp, tc):
        return xln_sb[:, cp * 2048 + tc * 512:cp * 2048 + (tc + 1) * 512]

    def src_h(cp, tc):
        return h_sb[:, cp * 2048 + tc * 512:cp * 2048 + (tc + 1) * 512]

    def src_x1(cp, tc):
        return v_sb[:, cp * 2048 + tc * 512:cp * 2048 + (tc + 1) * 512]

    def src_qt(cp, tc):
        return qkt[:, cp * 2048 + tc * 512:cp * 2048 + (tc + 1) * 512]

    def load_x_chunk(b, tc):
        src = xbf[b].rearrange("(cp p) t -> p cp t", p=128)[
            :, :, tc * 512:(tc + 1) * 512]
        P.w("sync", *cast_done["x%d" % b])
        qguard()
        return ld("WX", xchunk[:, :], src)

    def gstore(dst, src):
        P.w("gpsimd", "PE", P.c["PE"])
        P.w("gpsimd", "DVE", P.c["DVE"])
        P.w("gpsimd", "ACT", P.c["ACT"])
        return P.inc("gpsimd", lambda e, d=dst, s=src: e.dma_start(d, s),
                     "TS", 16)

    def fm_tap(nme, b, srct):
        if nme not in taps:
            return
        dst = tap_out[nme] if b is None else tap_out[nme][b]
        c = gstore(dst.rearrange("(cp p) t -> p cp t", p=128),
                   srct.rearrange("p (cp t) -> p cp t", cp=8))
        P.w("gpsimd", "TS", c)

    pool_free = {}

    def run_b(b):
        phase("b%d:ln0" % b)
        for tc in range(NTC):
            c = load_x_chunk(b, tc)
            P.w("tensor", *c)
            P.w("vector", *c)
            fm_ln(lambda cp, t: xchunk[:, cp * 512:(cp + 1) * 512],
                  lambda cp, t, tc=tc: src_h(cp, tc), tcs=[tc])
        P.barrier()
        fm_tap("h0", b, h_sb)
        if last_stage == "h0":
            return

        phase("ln1")
        fm_ln(src_h, src_xln, g="ln1_g", bnm="ln1_b")
        P.barrier()
        fm_tap("xln1", b, xln_sb)
        if last_stage == "xln1":
            return

        phase("qkv_qk")
        fm_proj(wb["cattn"], cast_done["cattn"], range(8), src_xln,
                evac_act(lambda fp, tc: qkt[:, fp * 2048 + tc * 512:
                                            fp * 2048 + (tc + 1) * 512],
                         "cattn_b"))
        fm_proj(wb["cattn"], cast_done["cattn"], range(8), src_xln,
                evac_act(lambda fp, tc: qkt[:, 16384 + fp * 2048 + tc * 512:
                                            16384 + fp * 2048 + (tc + 1) * 512],
                         "cattn_b", bias_js=[8 + j for j in range(8)]),
                wfp_base=8)

        def v_tm(wdram, wdep, f_base, vdst16):
            wv = wdram.rearrange("(cp p) f -> p cp f", p=128)
            qguard()
            P.w("sync", *wdep)
            ca = ld("WV0", wbuf[:, 0:4096], wv[:, :, f_base:f_base + 512])
            cb = ld("WV1", wbuf[:, 4096:8192],
                    wv[:, :, f_base + 512:f_base + 1024])
            kv = 0
            for tt in range(16):
                tc, to = tt // 4, (tt % 4) * 128
                for fc in range(2):
                    bk = kv % 4
                    kv += 1
                    take_bank(bk)
                    P.w("tensor", *(cb if fc else ca))
                    for cp in range(NCP):
                        mm(bank(bk),
                           vdst16(cp, tc, to),
                           wbuf[:, fc * 4096 + cp * 512:
                                fc * 4096 + (cp + 1) * 512],
                           cp == 0, cp == NCP - 1)
                    pe_c = pe_mark()
                    P.w("vector", "PE", pe_c)
                    dstv = v_sb[:, tt * 1024 + fc * 512:tt * 1024 + (fc + 1) * 512]
                    v_c = P.inc("vector", lambda e, d=dstv, p=bank(bk):
                                e.tensor_copy(d, p), "DVE")
                    bank_free[bk] = ("DVE", v_c)

        phase("qkv_v")
        v_tm(wb["cattn"], cast_done["cattn"], 2048,
             lambda cp, tc, to: src_xln(cp, tc)[:, to:to + 128])
        ring1.bump(P.c["PE"])
        ring_mp.bump(P.c["PE"])
        P.barrier()
        if "qt" in taps:
            c = gstore(tap_out["qt"][b].rearrange("(cp p) t -> p cp t", p=128),
                       qkt[:, 0:16384].rearrange("p (cp t) -> p cp t", cp=8))
            c = gstore(tap_out["kt"][b].rearrange("(cp p) t -> p cp t", p=128),
                       qkt[:, 16384:32768].rearrange("p (cp t) -> p cp t", cp=8))
            c = gstore(tap_out["v"][b].rearrange("(tt p) f -> p tt f", p=128),
                       v_sb[:, :].rearrange("p (tt f) -> p tt f", tt=16))
            P.w("gpsimd", "TS", c)
        if last_stage == "qkv":
            return

        phase("selfattn")
        attention(False)
        if last_stage == "selfattn":
            fm_tap("hself", b, qkt[0:128, 0:16384])
            return

        phase("aproj")
        fm_proj(wb["aproj"], cast_done["aproj"], range(8), src_qt,
                evac_resid(src_h, "aproj_b", src_h))
        P.barrier()
        fm_tap("hself", b, h_sb)
        if last_stage == "hself":
            return

        phase("enckv")
        if b == 0:
            for tc in range(NTC):
                src = enc_fm.rearrange("(cp p) t -> p cp t", p=128)[
                    :, :, tc * 512:(tc + 1) * 512]
                P.w("gpsimd", "PE", P.c["PE"])
                P.w("gpsimd", "DVE", P.c["DVE"])
                c = ldg("WXG", xchunk[:, :], src)
                P.w("tensor", *c)
                fm_proj(wb["xkv"], cast_done["xkv"], range(8),
                        lambda cp, t: xchunk[:, cp * 512:(cp + 1) * 512],
                        evac_act(lambda fp, t, tc=tc: qkt[
                            :, 16384 + fp * 2048 + tc * 512:
                            16384 + fp * 2048 + (tc + 1) * 512], "xkv_b"),
                        tcs=[0])
                wv2 = wb["xkv"].rearrange("(cp p) f -> p cp f", p=128)
                qguard()
                P.w("sync", *cast_done["xkv"])
                ca = ld("WV0", wbuf[:, 0:4096], wv2[:, :, 1024:1536])
                cb = ld("WV1", wbuf[:, 4096:8192], wv2[:, :, 1536:2048])
                for ti in range(4):
                    tt = tc * 4 + ti
                    for fc in range(2):
                        bk = (ti * 2 + fc) % 4
                        take_bank(bk)
                        P.w("tensor", *(cb if fc else ca))
                        for cp in range(NCP):
                            mm(bank(bk),
                               xchunk[:, cp * 512 + ti * 128:
                                      cp * 512 + ti * 128 + 128],
                               wbuf[:, fc * 4096 + cp * 512:
                                    fc * 4096 + (cp + 1) * 512],
                               cp == 0, cp == NCP - 1)
                        pe_c = pe_mark()
                        P.w("vector", "PE", pe_c)
                        dstv = v_sb[:, tt * 1024 + fc * 512:
                                    tt * 1024 + (fc + 1) * 512]
                        v_c = P.inc("vector", lambda e, d=dstv, p=bank(bk):
                                    e.tensor_copy(d, p), "DVE")
                        bank_free[bk] = ("DVE", v_c)
                ring1.bump(P.c["PE"])
                ring_mp.bump(P.c["PE"])
            P.barrier()
            qguard()
            store("KS", k2t_s.rearrange("(cp p) t -> p cp t", p=128),
                  qkt[:, 16384:32768].rearrange("p (cp t) -> p cp t", cp=8))
            c2 = store("KS", v2_s.rearrange("(tt p) f -> p tt f", p=128),
                       v_sb[:, :].rearrange("p (tt f) -> p tt f", tt=16))
            if "k2t" in taps:
                gstore(tap_out["k2t"].rearrange("(cp p) t -> p cp t", p=128),
                       qkt[:, 16384:32768].rearrange("p (cp t) -> p cp t", cp=8))
                cx = gstore(tap_out["v2"].rearrange("(tt p) f -> p tt f", p=128),
                            v_sb[:, :].rearrange("p (tt f) -> p tt f", tt=16))
                P.w("gpsimd", "TS", cx)
            enc_dep = c2
        else:
            qguard()
            P.w("sync", "KS", 32)
            ld("KL", qkt[:, 16384:32768].rearrange("p (cp t) -> p cp t", cp=8),
               k2t_s.rearrange("(cp p) t -> p cp t", p=128))
            c2 = ld("VL", v_sb[:, :].rearrange("p (tt f) -> p tt f", tt=16),
                    v2_s.rearrange("(tt p) f -> p tt f", p=128))
            P.w("tensor", "KL", P.c["KL"])
            enc_dep = c2

        phase("lnx")
        fm_ln(src_h, src_xln, g="lnx_g", bnm="lnx_b")
        P.barrier()
        phase("qattn")
        fm_proj(wb["qattn"], cast_done["qattn"], range(8), src_xln,
                evac_act(lambda fp, tc: qkt[:, fp * 2048 + tc * 512:
                                            fp * 2048 + (tc + 1) * 512],
                         "qattn_b"))
        P.w("tensor", *enc_dep)
        P.barrier()
        phase("crossattn")
        attention(True)
        if last_stage == "crossattn":
            fm_tap("hcross", b, qkt[0:128, 0:16384])
            return

        phase("xproj")
        fm_proj(wb["xproj"], cast_done["xproj"], range(8), src_qt,
                evac_resid(src_h, "xproj_b", src_h))
        P.barrier()
        fm_tap("hcross", b, h_sb)
        if last_stage == "hcross":
            return

        phase("ln2")
        fm_ln(src_h, src_xln, g="ln2_g", bnm="ln2_b")
        P.barrier()
        phase("fc_mproj")
        wfc = wb["fc"].rearrange("(cp p) f -> p cp f", p=128)
        wmp = wb["mproj"].rearrange("(hp p) f -> p hp f", p=128)
        hid_free_pe = 0
        for tcp in (0, 2):
            for hp in range(32):
                s, wsl, c = ring1.load(wfc[:, :, hp * 128:(hp + 1) * 128],
                                       cast_done["fc"])
                for t2 in range(2):
                    bk = (hp % 2) * 2 + t2
                    take_bank(bk)
                    P.w("tensor", *c)
                    for cp in range(NCP):
                        mm(bank(bk), wsl[:, cp * 128:(cp + 1) * 128],
                           src_xln(cp, tcp + t2), cp == 0, cp == NCP - 1)
                    pe_c = pe_mark()
                    P.w("scalar", "PE", max(pe_c, hid_free_pe))
                    dsth = qkt[:, t2 * 16384 + hp * 512:
                               t2 * 16384 + (hp + 1) * 512]
                    a_c = P.inc("scalar", lambda e, d=dsth, p=bank(bk),
                                bb=bias_ap("fc_b", hp): e.activation(
                                    d, p, AF.Gelu_apprx_tanh, bias=bb,
                                    scale=1.0), "ACT")
                    bank_free[bk] = ("ACT", a_c)
                ring1.done(s, pe_c)
            a_hid = P.c["ACT"]
            for fp in range(8):
                for t2 in range(2):
                    take_bank(4 + t2)
                P.w("tensor", "ACT", a_hid)
                for q in range(4):
                    s, wsl, c = ring_mp.load(
                        wmp[:, q * 8:(q + 1) * 8,
                            fp * 128:(fp + 1) * 128], cast_done["mproj"])
                    P.w("tensor", *c)
                    for t2 in range(2):
                        for hh in range(8):
                            hp = q * 8 + hh
                            mm(bank(4 + t2), wsl[:, hh * 128:(hh + 1) * 128],
                               qkt[:, t2 * 16384 + hp * 512:
                                   t2 * 16384 + (hp + 1) * 512],
                               hp == 0, hp == 31)
                    pe_c = pe_mark()
                    ring_mp.done(s, pe_c)
                P.w("vector", "PE", pe_c)
                for t2 in range(2):
                    v_c = P.inc("vector", lambda e,
                                d=src_h(fp, tcp + t2), p=bank(4 + t2),
                                bb=bias_ap("mproj_b", fp),
                                r=src_h(fp, tcp + t2):
                                e.scalar_tensor_tensor(d, p, bb, r, ALU.add,
                                                       ALU.add), "DVE")
                    bank_free[4 + t2] = ("DVE", v_c)
            hid_free_pe = P.c["PE"]
        P.barrier()
        fm_tap("hmlp", b, h_sb)
        if last_stage == "hmlp":
            return

        phase("outer1")
        for tc in range(NTC):
            c = load_x_chunk(b, tc)
            P.w("vector", *c)

            def out_add(cp, t, d, tc=tc):
                P.inc("vector", lambda e, d=d, o=src_x1(cp, tc),
                      a=xchunk[:, cp * 512:(cp + 1) * 512]:
                      e.tensor_tensor(o, d, a, ALU.add), "DVE")

            fm_ln(src_h, lambda cp, t, tc=tc: src_xln(cp, tc), tcs=[tc],
                  out_fn=out_add)
        P.barrier()
        fm_tap("x1", b, v_sb)
        if last_stage == "x1":
            return

        phase("x1ln")
        fm_ln(src_x1, src_xln)
        P.barrier()
        fm_tap("x1ln", b, xln_sb)

        def load_expert(e_, w_pe=(0, 0)):
            if e_ == -1:
                s1_, s2_ = wb["sw1"], wb["sw2"]
                d1, d2 = cast_done["sw1"], cast_done["sw2"]
            else:
                s1_, s2_ = wb["rw1"][e_], wb["rw2"][e_]
                d1, d2 = cast_done["rw1_%d" % e_], cast_done["rw2_%d" % e_]
            P.w("sync", "PE", w_pe[0])
            P.w("sync", *d1)
            c1 = ld("WE1", h_sb[:, :].rearrange("p (cp f) -> p cp f", cp=8),
                    s1_.rearrange("(cp p) f -> p cp f", p=128))
            P.w("sync", "PE", w_pe[1])
            P.w("sync", *d2)
            c2 = ld("WE2", wbuf[:, 0:8192].rearrange("p (hp f) -> p hp f", hp=8),
                    s2_.rearrange("(hp p) f -> p hp f", p=128))
            return c1, c2

        if MOVE_LE:
            cexp = None
        else:
            qguard()
            cexp = load_expert(-1)

        phase("gate")
        wg = wb["gate"].rearrange("(cp p) f -> p cp f", p=128)
        P.w("sync", *cast_done["gate"])
        cg = ld("WG", gwb[:, 0:64], wg[:, :, :])
        for tt in range(16):
            tc, to = tt // 4, (tt % 4) * 128
            bkg = GATE_BANKS[tt % 2]
            take_bank(bkg)
            P.w("tensor", *cg)
            for cp in range(NCP):
                mm(bank(bkg, 0, 128, 0, 8), src_xln(cp, tc)[:, to:to + 128],
                   gwb[:, cp * 8:(cp + 1) * 8], cp == 0, cp == NCP - 1)
            pe_c = pe_mark()
            P.w("vector", "PE", pe_c)
            g0 = gsm[:, 0:8]
            P.inc("vector", lambda e, p=bank(bkg, 0, 128, 0, 8): e.tensor_tensor(
                g0, p, gbb[:, :], ALU.add), "DVE")
            bank_free[bkg] = ("DVE", P.c["DVE"])
            m1 = gsm[:, 8:9]
            P.inc("vector", lambda e: e.reduce_max(m1, g0, AX.X,
                                                   negate=True), "DVE")
            ge = gsm[:, 16:24]
            P.w("scalar", "DVE", P.c["DVE"])
            a_e = P.inc("scalar", lambda e: e.activation(
                ge, g0, AF.Exp, bias=m1, scale=1.0), "ACT")
            P.w("vector", "ACT", a_e)
            sm = gsm[:, 9:10]
            P.inc("vector", lambda e: e.reduce_sum(sm, ge, AX.X), "DVE")
            rc = gsm[:, 10:11]
            P.inc("vector", lambda e: e.reciprocal(rc, sm), "DVE")
            pr = gsm[:, 24:32]
            P.inc("vector", lambda e: e.tensor_scalar(pr, ge, rc, None,
                                                      ALU.mult), "DVE")
            mp = gsm[:, 11:12]
            P.inc("vector", lambda e: e.reduce_max(mp, pr, AX.X), "DVE")
            s1 = gsm[:, 32:40]
            P.inc("vector", lambda e: e.tensor_scalar(s1, pr, mp, None,
                                                      ALU.is_ge), "DVE")
            pq = gsm[:, 40:48]
            P.inc("vector", lambda e: e.scalar_tensor_tensor(
                pq, s1, -1e30, pr, ALU.mult, ALU.add), "DVE")
            m2 = gsm[:, 12:13]
            P.inc("vector", lambda e: e.reduce_max(m2, pq, AX.X), "DVE")
            P.inc("vector", lambda e: e.tensor_scalar(s1, pr, m2, None,
                                                      ALU.is_ge), "DVE")
            v_cw = P.inc("vector", lambda e, tt=tt: e.tensor_tensor(
                cwtm[:, tt * 8:(tt + 1) * 8], pr, s1, ALU.mult), "DVE")
            if GATE_SYNC:
                P.barrier()
        P.barrier()
        if "cw" in taps:
            c = gstore(tap_out["cw"][b], cwtm[:, :])
            P.w("gpsimd", "TS", c)
        if last_stage == "gate":
            return

        if cexp is None:
            qguard()
            cexp = load_expert(-1)
        swz_free_pe = 0
        for ei, e_ in enumerate([-1, 0, 1, 2, 3, 4, 5, 6, 7][:N_EXP_DBG]):
            phase("exp%d" % e_)
            c1, c2 = cexp
            routed = e_ >= 0
            last_w1 = last_w2 = 0
            for tc in range(NTC):
                xch_dve = [0] * 8
                P.w("vector", "PE", swz_free_pe)
                for j in range(8 if "hid" not in DBG_SKIP else 0):
                    bka, bkb = (j % 2) * 2, (j % 2) * 2 + 1
                    take_bank(bka)
                    P.w("tensor", *c1)
                    for cp in range(NCP):
                        mm(bank(bka), h_sb[:, cp * 2048 + j * 128:
                                           cp * 2048 + j * 128 + 128],
                           src_xln(cp, tc), cp == 0, cp == NCP - 1)
                    take_bank(bkb)
                    for cp in range(NCP):
                        mm(bank(bkb), h_sb[:, cp * 2048 + 1024 + j * 128:
                                           cp * 2048 + 1024 + j * 128 + 128],
                           src_xln(cp, tc), cp == 0, cp == NCP - 1)
                    pe_c = pe_mark()
                    if tc == NTC - 1 and j == 7:
                        last_w1 = pe_c
                    if "silu" in DBG_SKIP:
                        bank_free[bka] = ("PE", pe_c)
                        bank_free[bkb] = ("PE", pe_c)
                        continue
                    P.w("scalar", "PE", pe_c)
                    sl = sqbuf[:, (j % 2) * 512:(j % 2 + 1) * 512]
                    bj = ("sb1", j) if not routed else ("rb1", e_ * 16 + j)
                    a_c = P.inc("scalar", lambda e, s=sl, p=bank(bka),
                                bb=bias_ap(*bj): e.activation(
                                    s, p, AF.Silu, bias=bb, scale=1.0), "ACT")
                    bank_free[bka] = ("ACT", a_c)
                    if "stt" in DBG_SKIP:
                        bank_free[bkb] = ("ACT", a_c)
                        continue
                    P.w("vector", "ACT", a_c)
                    bj2 = ("sb1", 8 + j) if not routed else \
                          ("rb1", e_ * 16 + 8 + j)
                    v_c = P.inc("vector", lambda e,
                                d=xchunk[:, j * 512:(j + 1) * 512],
                                p=bank(bkb), bb=bias_ap(*bj2), s=sl:
                                e.scalar_tensor_tensor(d, p, bb, s, ALU.add,
                                                       ALU.mult), "DVE")
                    bank_free[bkb] = ("DVE", v_c)
                    xch_dve[j] = v_c
                if routed:
                    take_bank(6)
                    for ti in range(4):
                        mmc[0] += 1
                        P.on("tensor", lambda e,
                             o=bank(6, 0, 1, ti * 128, ti * 128 + 128),
                             i_=cwtm[:, (tc * 4 + ti) * 8 + e_:
                                     (tc * 4 + ti) * 8 + e_ + 1]:
                             e.transpose(o, i_, id32[:, :]))
                    c_tr = pe_mark()
                    P.w("vector", "PE", c_tr)
                    v_cr = P.inc("vector", lambda e: e.tensor_copy(
                        crow[0:1, :], bank(6, 0, 1, 0, 512)), "DVE")
                    P.w("tensor", "DVE", v_cr)
                    mm(bank(6), ones_bf[0:1, :], crow[0:1, :], True, True)
                    c_bc = pe_mark()
                    P.w("vector", "PE", c_bc)
                    v_bc = P.inc("vector", lambda e: e.tensor_copy(
                        bcsb[:, 0:512], bank(6)), "DVE")
                    bank_free[6] = ("DVE", v_bc)
                for fp in range(8):
                    if "w2" in DBG_SKIP:
                        break
                    if "w2one" in DBG_SKIP and (fp > 0 or tc > 0):
                        break
                    bk = 4 + fp % 2
                    take_bank(bk)
                    P.w("tensor", *c2)
                    if "pedrain" in DBG_SKIP:
                        P.w("tensor", "PE", P.c["PE"])
                    for hp in range(NCP):
                        P.w("tensor", "DVE", xch_dve[hp])
                        rhs_ = xchunk[:, hp * 512:(hp + 1) * 512]
                        mm(bank(bk), wbuf[:, hp * 1024 + fp * 128:
                                          hp * 1024 + fp * 128 + 128],
                           rhs_, hp == 0, hp == NCP - 1)
                    pe_c = pe_mark()
                    if tc == NTC - 1 and fp == 7:
                        last_w2 = pe_c
                    acc_ap = accum[:, fp * 2048 + tc * 512:
                                   fp * 2048 + (tc + 1) * 512]
                    P.w("vector", "PE", pe_c)
                    if "acc" in DBG_SKIP:
                        bank_free[bk] = ("PE", pe_c)
                        continue
                    if not routed:
                        v_c = P.inc("vector", lambda e, a=acc_ap, p=bank(bk),
                                    bb=bias_ap("sb2", fp): e.tensor_scalar(
                                        a, p, bb, None, ALU.add), "DVE")
                        bank_free[bk] = ("DVE", v_c)
                    else:
                        tsl = tmp1[:, (fp % 2) * 512:(fp % 2 + 1) * 512]
                        P.w("vector", "POOL", pool_free.get(fp % 2, 0))
                        v_c = P.inc("vector", lambda e, t=tsl, p=bank(bk),
                                    bb=bias_ap("rb2", e_ * 8 + fp):
                                    e.scalar_tensor_tensor(
                                        t, p, bb, bcsb[:, 0:512], ALU.add,
                                        ALU.mult), "DVE")
                        bank_free[bk] = ("DVE", v_c)
                        P.w("gpsimd", "DVE", v_c)
                        pc = P.inc("gpsimd", lambda e, a=acc_ap, t=tsl:
                                   e.tensor_tensor(a, a, t, ALU.add), "POOL")
                        pool_free[fp % 2] = pc
                swz_free_pe = P.c["PE"]
            if ei < 8 and "pref" not in DBG_SKIP:
                cexp = load_expert(e_ + 1 if routed else 0,
                                   w_pe=(last_w1, last_w2))
        ring1.bump(P.c["PE"])
        ring_mp.bump(P.c["PE"])
        P.barrier()
        if "acc" in taps:
            c = gstore(tap_out["acc"][b].rearrange("(cp p) t -> p cp t", p=128),
                       accum[:, :].rearrange("p (cp t) -> p cp t", cp=8))
            P.w("gpsimd", "TS", c)
        if last_stage == "acc":
            return

        phase("finalln")
        st_prev = [0]
        P.w("vector", "POOL", P.c["POOL"])
        for tc in range(NTC):
            P.w("vector", "PE", P.c["PE"])
            for cp in range(NCP):
                P.inc("vector", lambda e, cp=cp, tc=tc: e.tensor_copy(
                    xchunk[:, cp * 512:(cp + 1) * 512],
                    accum[:, cp * 2048 + tc * 512:cp * 2048 + (tc + 1) * 512]),
                    "DVE")

            def out_store(cp, t, d, tc=tc):
                P.w("vector", "OS", st_prev[0])
                ost = ostage[:, 0:512]
                v_c = P.inc("vector", lambda e, o=ost, d=d,
                            a=src_x1(cp, tc): e.tensor_tensor(o, d, a, ALU.add),
                            "DVE")
                P.w("sync", "DVE", v_c)
                st_prev[0] = store(
                    "OS", y[b][cp * 128:(cp + 1) * 128,
                               tc * 512:(tc + 1) * 512], ost)[1]

            fsrc = lambda cp, t: xchunk[:, cp * 512:(cp + 1) * 512]
            DBG_LN_SRC[0] = fsrc
            fm_ln(fsrc, lambda cp, t, tc=tc: src_xln(cp, tc), tcs=[tc],
                  out_fn=out_store)
        P.barrier()

    if last_stage == "init":
        if "init" in taps:
            qguard()
            c = store("OS", tap_out["init"], bia[:, :])
    else:
        for b in range(n_b):
            run_b(b)

    for s in DMA_SEMS:
        eng = "gpsimd" if (s == "TS" or s.startswith(("cx", "cw_", "cr"))
                           or s in ("WC", "WXG")) else "sync"
        P.w(eng, s, P.c[s])

    with nc.Block() as block:
        with nc.allow_low_precision(reason="bf16 operand staging is intentional"):
            P.install(block)
    return nc, list(tap_out)


def _pack_biases(i):
    cols = np.zeros((128, BIAS_COLS), dtype=np.float32)

    def put(nme, vec):
        v = np.asarray(vec, dtype=np.float32).reshape(-1)
        n = v.size // 128
        cols[:, BIAS_OFF[nme]:BIAS_OFF[nme] + n] = v.reshape(n, 128).T

    put("ln1_g", i["L0_attn_ln1_g"]); put("ln1_b", i["L0_attn_ln1_b"])
    put("lnx_g", i["L0_attn_lnx_g"]); put("lnx_b", i["L0_attn_lnx_b"])
    put("ln2_g", i["L0_attn_ln2_g"]); put("ln2_b", i["L0_attn_ln2_b"])
    put("cattn_b", i["L0_attn_c_attn_b"])
    put("aproj_b", i["L0_attn_attn_proj_b"])
    put("qattn_b", i["L0_attn_q_attn_b"])
    put("xkv_b", i["L0_attn_x_kv_b"])
    put("xproj_b", i["L0_attn_x_proj_b"])
    put("fc_b", i["L0_attn_fc_b"])
    put("mproj_b", i["L0_attn_mlp_proj_b"])
    put("sb1", i["L0_ffn_sb1"]); put("sb2", i["L0_ffn_sb2"])
    put("rb1", i["L0_ffn_rb1"]); put("rb2", i["L0_ffn_rb2"])
    put("scorer", i["L0_attn_scorer_logits"])
    gb = np.zeros(128, dtype=np.float32)
    gb[0:8] = np.asarray(i["L0_ffn_gate_b"], dtype=np.float32)
    cols[:, BIAS_OFF["gate_b"]] = gb
    cols[:, BIAS_OFF["c_eps"]] = 1e-5
    cols[:, BIAS_OFF["c_shift"]] = -20.0
    return cols


def kernel(**inputs):
    n_b = N_B
    i = inputs
    cc = np.ascontiguousarray

    x = np.asarray(i["L0_attn_hidden_states"], dtype=np.float32)
    enc = np.asarray(i["L0_attn_encoder_hidden_states"], dtype=np.float32)[0]
    biases = _pack_biases(i)
    jj = np.arange(644)[None, :]
    pp = np.arange(128)[:, None]
    mask = np.where(jj >= pp, 0.0, -10000.0).astype(np.float32)
    ones_bf = np.ones((128, 128), dtype=np.float32)
    sel = np.zeros((33, 128), dtype=np.float32)
    sel[0, 0:64] = 1.0
    sel[32, 64:128] = 1.0
    id32 = np.eye(128, dtype=np.float32)

    base = dict(
        gbrow=np.tile(np.asarray(i["L0_ffn_gate_b"], dtype=np.float32)[None, :],
                      (128, 1)),
        enc_fm=cc(enc.T),
        biases=biases,
        maskbf=mask,
        ones_bf=ones_bf,
        sel33=sel,
        id32=id32,
        w_cattn=np.asarray(i["L0_attn_c_attn_w"], dtype=np.float32),
        w_qattn=np.asarray(i["L0_attn_q_attn_w"], dtype=np.float32),
        w_xkv=np.asarray(i["L0_attn_x_kv_w"], dtype=np.float32),
        w_aproj=np.asarray(i["L0_attn_attn_proj_w"], dtype=np.float32),
        w_xproj=np.asarray(i["L0_attn_x_proj_w"], dtype=np.float32),
        w_fc=np.asarray(i["L0_attn_fc_w"], dtype=np.float32),
        w_mproj=np.asarray(i["L0_attn_mlp_proj_w"], dtype=np.float32),
        w_gate=np.asarray(i["L0_ffn_gate_w"], dtype=np.float32),
        w_sw1=np.asarray(i["L0_ffn_sw1"], dtype=np.float32)[0],
        w_sw2=np.asarray(i["L0_ffn_sw2"], dtype=np.float32)[0],
        w_rw1=np.asarray(i["L0_ffn_rw1"], dtype=np.float32),
        w_rw2=np.asarray(i["L0_ffn_rw2"], dtype=np.float32),
    )

    nc, tap_names = build_nc(n_b, TAPS, LAST_STAGE)
    in_maps = []
    for c in range(8):
        m = dict(base)
        xs = x[c * n_b:(c + 1) * n_b]
        m["x_fm"] = cc(xs.transpose(0, 2, 1))
        in_maps.append(m)

    res = run_bass_kernel_spmd(nc, in_maps, core_ids=list(range(8)))

    global _dbg
    _dbg = {t: [np.asarray(r["dbg_" + t]) for r in res.results]
            for t in tap_names}

    outs = []
    for c in range(8):
        yc = np.asarray(res.results[c]["y"])
        outs.append(yc.transpose(0, 2, 1))
    return np.concatenate(outs, axis=0).astype(np.float32)
